# version 08
# baseline (speedup 1.0000x reference)
import numpy as np
import concourse.bass as bass
import concourse.mybir as mybir
from concourse.bass_utils import run_bass_kernel_spmd
from contextlib import ExitStack

F32 = mybir.dt.float32
F32R = mybir.dt.float32r
U8 = mybir.dt.uint8
AF = mybir.ActivationFunctionType
ALU = mybir.AluOpType
AX = mybir.AxisListType.X

LOAD_SEMS = ["blk", "wA", "wB", "xA", "xB", "kA", "kB"]
STORE_SEMS = ["stA", "stB", "stC", "stD", "stO"]
ALL_SEMS = LOAD_SEMS + STORE_SEMS + ["tM", "vC", "aC", "gC", "CC"]


class CFG:
    def __init__(self, D=4096, T=1024, HD=128, KVH=8, P_EI=0.8):
        self.D = D
        self.T = T
        self.HD = HD
        self.DC = D // 128
        self.HEADS = D // HD
        self.KVH = KVH
        self.KVD = KVH * HD
        self.KVC = self.KVD // 128
        self.NREP = self.HEADS // self.KVH
        self.TS = T // 2
        self.NTS = 2
        self.TC = self.TS // 128
        self.SEQK = 2 * T
        self.KC = self.SEQK // 128
        self.QC = T // 128
        self.KT = self.SEQK // self.TS
        self.NE = round(D * P_EI)
        self.EI_NEG = -P_EI / (1.0 - P_EI)
        att_need = (2 * self.T + 2 * 2 * self.T + 2 * self.KC * 128 +
                    2 * self.KC * self.TS + 2 * self.TS)
        self.INLW = max(2 * self.DC * self.TS, att_need, 2 * D)
        assert self.TS <= 512 and HD == 128
        assert self.DC % 4 == 0


class Build:
    def __init__(self, nc, cfg, phase_limit=None):
        self.nc = nc
        self.c = cfg
        self.phase_limit = phase_limit
        self.ct = {s: 0 for s in ALL_SEMS}

    def V(self, vec, make):
        vec.wait_ge(self.sem["vC"], self.ct["vC"])
        make().then_inc(self.sem["vC"], 1)
        self.ct["vC"] += 1

    def A(self, sc, make):
        sc.wait_ge(self.sem["aC"], self.ct["aC"])
        make().then_inc(self.sem["aC"], 1)
        self.ct["aC"] += 1

    def G(self, gp, make):
        gp.wait_ge(self.sem["gC"], self.ct["gC"])
        make().then_inc(self.sem["gC"], 1)
        self.ct["gC"] += 1

    def tinc(self, ins):
        ins.then_inc(self.sem["tM"], 1)
        self.ct["tM"] += 1

    def dma(self, eng, stream, out, in_, **kw):
        eng.dma_start(out=out, in_=in_, **kw).then_inc(self.sem[stream], 16)
        self.ct[stream] += 16

    def wait_stores(self, eng, S):
        for s in STORE_SEMS:
            if S[s] > 0:
                eng.wait_ge(self.sem[s], S[s])

    def build(self):
        nc, c = self.nc, self.c
        D, T, TS, DC = c.D, c.T, c.TS, c.DC
        dt = nc.dram_tensor
        self.x0 = dt("x0", [DC, 128, T], F32, kind="ExternalInput").ap()
        self.wq_n = dt("wq_n", [D, D], F32, kind="ExternalInput").ap()
        self.wk_n = dt("wk_n", [c.KVD, D], F32, kind="ExternalInput").ap()
        self.wv_n = dt("wv_n", [c.KVD, D], F32, kind="ExternalInput").ap()
        self.wo_n = dt("wo_n", [D, D], F32, kind="ExternalInput").ap()
        self.qsc_in = dt("qsc_in", [D], F32, kind="ExternalInput").ap()
        self.ksc_in = dt("ksc_in", [c.KVD], F32, kind="ExternalInput").ap()
        self.vsc_in = dt("vsc_in", [c.KVD], F32, kind="ExternalInput").ap()
        self.osc_in = dt("osc_in", [D], F32, kind="ExternalInput").ap()
        self.k0u = dt("k0u", [D, D], U8, kind="ExternalInput").ap()
        self.k1u = dt("k1u", [D, D], U8, kind="ExternalInput").ap()
        self.k2u = dt("k2u", [D, D], U8, kind="ExternalInput").ap()
        self.s0 = dt("s0", [1, 1], F32, kind="ExternalInput").ap()
        self.s1 = dt("s1", [1, 1], F32, kind="ExternalInput").ap()
        self.s2 = dt("s2", [1, 1], F32, kind="ExternalInput").ap()
        self.wqT = dt("wqT", [D, D], F32, kind="ExternalInput").ap()
        self.wkT = dt("wkT", [D, c.KVD], F32, kind="ExternalInput").ap()
        self.wvT = dt("wvT", [D, c.KVD], F32, kind="ExternalInput").ap()
        self.woT = dt("woT", [D, D], F32, kind="ExternalInput").ap()
        self.cei = dt("cei", [128, DC], F32, kind="ExternalInput").ap()
        self.cei2 = dt("cei2", [128, DC], F32, kind="ExternalInput").ap()
        self.yout = dt("yout", [DC, 128, T], F32, kind="ExternalOutput").ap()

        self.xcur = dt("xcur", [DC, 128, T], F32).ap()
        self.w0s = dt("w0s", [DC, 128, DC * 128], F32R).ap()
        self.w1s = dt("w1s", [DC, 128, DC * 128], F32R).ap()
        self.w2s = dt("w2s", [DC, 128, DC * 128], F32R).ap()
        self.wqs = dt("wqs", [DC, 128, DC * 128], F32R).ap()
        self.wks = dt("wks", [c.KVC, 128, DC * 128], F32R).ap()
        self.wos = dt("wos", [DC, 128, DC * 128], F32R).ap()
        self.wvs = dt("wvs", [DC, 128, c.KVD], F32R).ap()
        self.qLd = dt("qLd", [2, DC, 128, T], F32R).ap()
        self.hLd = dt("hLd", [2, DC, 128, TS], F32R).ap()
        self.ccink = dt("ccink", [2 * c.KVC * 128, T], F32R).ap()
        self.ccoutk = dt("ccoutk", [2 * 2 * c.KVC * 128, T], F32R).ap()
        self.ccinv = dt("ccinv", [c.KVC * 2 * 128, T], F32R).ap()
        self.ccoutv = dt("ccoutv", [2 * c.KVC * 2 * 128, T], F32R).ap()
        self.attnL = dt("attnL", [2, DC, 128, T], F32R).ap()
        self.mrow_d = dt("mrow_d", [c.HEADS, c.QC, 128], F32).ap()
        self.vrow_d = dt("vrow_d", [c.KVC, 128], F32).ap()

        es = ExitStack()
        sb, ps = nc.sbuf_tensor, nc.psum_tensor
        E = es.enter_context
        self.inl = E(sb("inl", [128, c.INLW], F32R))
        self.wsb = E(sb("wsb", [128, 2 * DC * 128], F32R))
        self.ksb = E(sb("ksb", [128, 2 * D], U8))
        self.tmp32 = E(sb("tmp32", [128, 2 * TS], F32))
        self.ltmp = E(sb("ltmp", [128, 4 * TS], F32R))
        self.xck = E(sb("xck", [128, 2 * TS], F32))
        self.bcast = E(sb("bcast", [128, 2 * TS], F32))
        self.ptm32 = E(sb("ptm32", [128, 2 * TS], F32))
        self.rows = E(sb("rows", [1, 3 * T], F32))
        self.sct = E(sb("sct", [128, 2 * DC + c.KVC + 8], F32))
        self.ceis = E(sb("ceis", [128, 2 * DC], F32))
        self.onesf = E(sb("onesf", [1, 128], F32))
        self.onescf = E(sb("onescf", [128, 1], F32))
        self.onescr = E(sb("onescr", [128, 1], F32R))
        self.psb = E(ps("psb", [128, 4096], F32))

        self.sem = {}
        for s in ALL_SEMS:
            self.sem[s] = E(nc.semaphore(s))
        self.block = E(nc.Block())
        self.emit_all(phase_limit=self.phase_limit)
        es.close()
        return nc

    def bank(self, i):
        return self.psb[:, i * 512:(i + 1) * 512]

    def scal(self, name):
        c = self.c
        base = 2 * c.DC + c.KVC
        idx = {"s0": base, "s1": base + 1, "s2": base + 2,
               "rs0": base + 3, "rs1": base + 4, "hs2": base + 5}[name]
        return self.sct[:, idx:idx + 1]

    def emit_all(self, phase_limit=None):
        c = self.c
        phases = [self.ph_consts,
                  lambda: self.ph_conv_ffn(self.k0u, self.w0s, use_c=False),
                  lambda: self.ph_conv_ffn(self.k1u, self.w1s, use_c=True),
                  lambda: self.ph_conv_ffn(self.k2u, self.w2s, use_c=True),
                  self.ph_conv_att_q,
                  self.ph_conv_att_k,
                  self.ph_conv_att_v,
                  self.ph_conv_att_o,
                  self.ph_conv_att_tail,
                  lambda: self.ph_ffn(src=self.x0, dst=self.xcur),
                  lambda: self.ph_qkv(src=self.xcur)]
        phases += [(lambda h=h: self.ph_att_head(h)) for h in range(c.HEADS)]
        phases += [(lambda ts=ts: self.ph_mm(
            self.wos, c.DC, in_src="attnL", evac="resid",
            resid_src=self.xcur, out_dst=self.xcur, ts=ts,
            scale_col=(None, c.DC))) for ts in range(c.NTS)]
        phases += [lambda: self.ph_ffn(src=self.xcur, dst=self.yout)]
        if phase_limit is None:
            phase_limit = len(phases)
        for ph in phases[:phase_limit]:
            ph()
        self.ph_end()

    def ph_consts(self):
        b, c = self.block, self.c

        @b.sync
        def _(sync):
            self.dma(sync, "blk", self.ceis[:, 0:c.DC], self.cei)
            self.dma(sync, "blk", self.ceis[:, c.DC:2 * c.DC], self.cei2)
            base = 2 * c.DC + c.KVC
            for j, s in enumerate([self.s0, self.s1, self.s2]):
                self.dma(sync, "blk", self.sct[:, base + j:base + j + 1],
                         bass.AP(s.tensor, 0, [[0, 128], [1, 1]]),
                         allow_slow_non_contiguous=True)
        blk_done = self.ct["blk"]

        @b.vector
        def _(vec):
            vec.wait_ge(self.sem["blk"], blk_done)
            self.V(vec, lambda: vec.memset(self.onesf[:], 1.0))
            self.V(vec, lambda: vec.memset(self.onescf[:], 1.0))
            self.V(vec, lambda: vec.tensor_copy(self.onescr[:], self.onescf[:]))
            base = 2 * c.DC + c.KVC
            self.V(vec, lambda: vec.tensor_scalar_mul(
                self.sct[:, base + 3:base + 4], self.sct[:, base:base + 1], 2.0))
            self.V(vec, lambda: vec.tensor_scalar_mul(
                self.sct[:, base + 4:base + 5], self.sct[:, base + 1:base + 2], 2.0))
            self.V(vec, lambda: vec.tensor_scalar_mul(
                self.sct[:, base + 5:base + 6], self.sct[:, base + 2:base + 3], 0.5))

    def ph_conv_ffn(self, ksrc, wdst, use_c):
        b, c = self.block, self.c
        DC = c.DC
        S = dict(self.ct)
        nEv = 1
        ksem = ("kA", "kB")
        stsem = ("stA", "stB")

        @b.sync
        def _(sync):
            for i in range(DC):
                sync.wait_ge(self.sem["gC"],
                             S["gC"] + ((i - 1) * nEv if i >= 2 else 0))
                self.dma(sync, ksem[i % 2],
                         self.ksb[:, (i % 2) * c.D:(i % 2 + 1) * c.D],
                         ksrc[i * 128:(i + 1) * 128, :])

        @b.gpsimd
        def _(gp):
            for i in range(DC):
                gp.wait_ge(self.sem[ksem[i % 2]],
                           S[ksem[i % 2]] + 16 * (i // 2 + 1))
                if i >= 2:
                    gp.wait_ge(self.sem[stsem[i % 2]],
                               S[stsem[i % 2]] + 16 * DC * (i // 2))
                else:
                    self.wait_stores(gp, S)
                    gp.wait_ge(self.sem["tM"], S["tM"])
                    gp.wait_ge(self.sem["aC"], S["aC"])
                    gp.wait_ge(self.sem["vC"], S["vC"])
                sgn = self.inl[:, (i % 2) * c.D:(i % 2 + 1) * c.D]
                ksl = self.ksb[:, (i % 2) * c.D:(i % 2 + 1) * c.D]
                if use_c:
                    cc = self.ceis[:, i:i + 1]
                    c2 = self.ceis[:, DC + i:DC + i + 1]
                    self.G(gp, lambda sgn=sgn, ksl=ksl, cc=cc, c2=c2:
                           gp.tensor_scalar(sgn, ksl, c2, cc, ALU.mult, ALU.subtract))
                else:
                    self.G(gp, lambda sgn=sgn, ksl=ksl:
                           gp.tensor_scalar(sgn, ksl, 2.0, 1.0, ALU.mult, ALU.subtract))

        @b.scalar
        def _(sc):
            for i in range(DC):
                sc.wait_ge(self.sem["gC"], S["gC"] + (i + 1) * nEv)
                sgn = self.inl[:, (i % 2) * c.D:(i % 2 + 1) * c.D]
                for oc in range(DC):
                    self.dma(sc, stsem[i % 2],
                             wdst[oc, :, i * 128:(i + 1) * 128],
                             sgn[:, oc * 128:(oc + 1) * 128])

    def ph_conv_att_q(self):
        c = self.c
        rsq = float(np.float32(1.0) / np.float32(np.sqrt(np.float64(c.HD))))
        self.ph_scales(self.wq_n, self.qsc_in, c.DC, 0, rsq, vrow=False)
        self.ph_sign(self.wqT, self.wqs, c.DC, whole=False)

    def ph_conv_att_k(self):
        c = self.c
        self.ph_scales(self.wk_n, self.ksc_in, c.KVC, 2 * c.DC, 1.0, vrow=False)
        self.ph_sign(self.wkT, self.wks, c.KVC, whole=False)

    def ph_conv_att_v(self):
        c = self.c
        self.ph_scales(self.wv_n, self.vsc_in, c.KVC, -1, 1.0, vrow=True)
        self.ph_sign(self.wvT, self.wvs, c.KVC, whole=True)

    def ph_conv_att_o(self):
        c = self.c
        self.ph_scales(self.wo_n, self.osc_in, c.DC, c.DC, 1.0, vrow=False)
        self.ph_sign(self.woT, self.wos, c.DC, whole=False)

    def ph_conv_att_tail(self):
        b, c = self.block, self.c
        S = dict(self.ct)

        @b.sync
        def _(sync):
            self.wait_stores(sync, S)
            self.dma(sync, "blk", self.rows[:, 2 * c.T:2 * c.T + c.KVD],
                     self.vrow_d.rearrange("a b -> (a b)").unsqueeze(0))

    def ph_scales(self, wn, scin, OC, sccol, xsc, vrow):
        b, c = self.block, self.c
        S = dict(self.ct)
        nEv = 3
        wsem = ("wA", "wB")

        @b.sync
        def _(sync):
            for o in range(OC):
                if o >= 2:
                    sync.wait_ge(self.sem["vC"], S["vC"] + (o - 1) * nEv)
                else:
                    sync.wait_ge(self.sem["vC"], S["vC"])
                    sync.wait_ge(self.sem["tM"], S["tM"])
                    self.wait_stores(sync, S)
                slab = self.inl[:, 2 * c.D + (o % 2) * c.D:
                                2 * c.D + (o % 2 + 1) * c.D]
                self.dma(sync, wsem[o % 2], slab,
                         wn[o * 128:(o + 1) * 128, :].bitcast(F32R))
                self.dma(sync, wsem[o % 2], self.bcast[:, (o % 2):(o % 2) + 1],
                         bass.AP(scin.tensor, o * 128, [[1, 128], [1, 1]]),
                         allow_slow_non_contiguous=True)

        @b.vector
        def _(vec):
            for o in range(OC):
                vec.wait_ge(self.sem[wsem[o % 2]],
                            S[wsem[o % 2]] + 32 * (o // 2 + 1))
                wslab = self.inl[:, 2 * c.D + (o % 2) * c.D:
                                 2 * c.D + (o % 2 + 1) * c.D].bitcast(F32)
                srow = self.bcast[:, 2 + (o % 2):3 + (o % 2)]
                self.V(vec, lambda srow=srow, w=wslab: vec.reduce_sum(
                    srow, w, axis=AX, apply_absolute_value=True))
                self.V(vec, lambda srow=srow: vec.tensor_scalar(
                    srow, srow, 1.0 / c.D, 1e-8, ALU.mult, ALU.max))
                scch = self.bcast[:, (o % 2):(o % 2) + 1]
                if vrow:
                    out = self.bcast[:, 4 + (o % 2):5 + (o % 2)]
                    vsem = ("stC", "stD")[o % 2]
                    vec.wait_ge(self.sem[vsem], S[vsem] + 16 * (o // 2))
                else:
                    out = self.sct[:, sccol + o:sccol + o + 1]
                self.V(vec, lambda out=out, srow=srow, scch=scch:
                       vec.scalar_tensor_tensor(out, srow, xsc, scch,
                                                ALU.mult, ALU.mult))

        if vrow:
            @b.scalar
            def _(sc):
                for o in range(OC):
                    sc.wait_ge(self.sem["vC"], S["vC"] + (o + 1) * nEv)
                    self.dma(sc, ("stC", "stD")[o % 2],
                             self.vrow_d[o].unsqueeze(1),
                             self.bcast[:, 4 + (o % 2):5 + (o % 2)])

    def ph_sign(self, srcT, wdst, OC, whole):
        b, c = self.block, self.c
        DC = c.DC
        S = dict(self.ct)
        W = OC * 128
        nEa = 1
        nStW = 1 if whole else OC
        wsem = ("wA", "wB")
        stsem = ("stA", "stB")

        @b.sync
        def _(sync):
            for i in range(DC):
                if i >= 2:
                    sync.wait_ge(self.sem["aC"], S["aC"] + (i - 1) * nEa)
                else:
                    sync.wait_ge(self.sem["aC"], S["aC"])
                    sync.wait_ge(self.sem["vC"], S["vC"])
                    sync.wait_ge(self.sem["tM"], S["tM"])
                    self.wait_stores(sync, S)
                slab = self.inl[:, 2 * c.D + (i % 2) * c.D:
                                2 * c.D + (i % 2) * c.D + W]
                self.dma(sync, wsem[i % 2], slab,
                         srcT[i * 128:(i + 1) * 128, :].bitcast(F32R))

        @b.scalar
        def _(sc):
            for i in range(DC):
                sc.wait_ge(self.sem[wsem[i % 2]],
                           S[wsem[i % 2]] + 16 * (i // 2 + 1))
                if i >= 2:
                    sc.wait_ge(self.sem[stsem[i % 2]],
                               S[stsem[i % 2]] + 16 * nStW * (i // 2))
                else:
                    self.wait_stores(sc, S)
                wslab = self.inl[:, 2 * c.D + (i % 2) * c.D:
                                 2 * c.D + (i % 2) * c.D + W].bitcast(F32)
                sgn = self.inl[:, (i % 2) * c.D:(i % 2) * c.D + W]
                self.A(sc, lambda sgn=sgn, w=wslab: sc.activation(sgn, w, AF.Sign))
                sc.wait_ge(self.sem["aC"], S["aC"] + (i + 1) * nEa)
                if whole:
                    self.dma(sc, stsem[i % 2], wdst[i, :, :], sgn)
                else:
                    for oc in range(OC):
                        self.dma(sc, stsem[i % 2],
                                 wdst[oc, :, i * 128:(i + 1) * 128],
                                 sgn[:, oc * 128:(oc + 1) * 128])

    def ph_ffn(self, src, dst):
        c = self.c
        for ts in range(c.NTS):
            self.ph_ln(src, ts)
            self.ph_mm(self.w0s, c.DC, in_src="sbuf", evac="relu",
                       out_dst=self.hLd, ts=ts, scale_col=("rs0", None))
            self.ph_mm(self.w1s, c.DC, in_src="hLd", evac="relu",
                       out_dst=self.hLd, ts=ts, scale_col=("rs1", None))
            self.ph_mm(self.w2s, c.DC, in_src="hLd", evac="resid",
                       resid_src=src, out_dst=dst, ts=ts, scale_col=("hs2", None))

    def ph_ln(self, src, ts):
        b, c = self.block, self.c
        DC, TS = c.DC, c.TS
        S = dict(self.ct)
        nV1, nV2, nR = 5, 4, 6
        xsem = ("xA", "xB")

        @b.sync
        def _(sync):
            self.wait_stores(sync, S)
            for i in range(DC):
                sync.wait_ge(self.sem["vC"],
                             S["vC"] + ((i - 1) * nV1 if i >= 2 else 0))
                self.dma(sync, xsem[i % 2],
                         self.xck[:, (i % 2) * TS:(i % 2 + 1) * TS],
                         src[i, :, ts * TS:(ts + 1) * TS])
            for i in range(DC):
                sync.wait_ge(self.sem["vC"], S["vC"] + DC * nV1 + nR +
                             max(i - 1, 0) * nV2)
                self.dma(sync, xsem[i % 2],
                         self.xck[:, (i % 2) * TS:(i % 2 + 1) * TS],
                         src[i, :, ts * TS:(ts + 1) * TS])

        @b.tensor
        def _(t):
            t.wait_ge(self.sem["aC"], S["aC"])
            for i in range(DC):
                t.wait_ge(self.sem["vC"], S["vC"] + i * nV1 + 2)
                A = self.ltmp[:, (i % 2) * 2 * TS:(i % 2) * 2 * TS + TS]
                B = self.ltmp[:, (i % 2) * 2 * TS + TS:(i % 2 + 1) * 2 * TS]
                psx = self.bank(4)[0:1, :TS]
                psq = self.bank(5)[0:1, :TS]
                t.matmul(psx, self.onescr[:], A, start=(i == 0), stop=False)
                self.tinc(t.matmul(psx, self.onescr[:], B, start=False,
                                   stop=(i == DC - 1)))
                t.wait_ge(self.sem["vC"], S["vC"] + i * nV1 + 5)
                t.matmul(psq, self.onescr[:], A, start=(i == 0), stop=False)
                self.tinc(t.matmul(psq, self.onescr[:], B, start=False,
                                   stop=(i == DC - 1)))
            t.wait_ge(self.sem["vC"], S["vC"] + DC * nV1 + nR)
            self.tinc(t.matmul(self.bank(2)[:, :TS], self.onesf[:],
                               self.rows[:, 0:TS], start=True, stop=True))
            self.tinc(t.matmul(self.bank(3)[:, :TS], self.onesf[:],
                               self.rows[:, TS:2 * TS], start=True, stop=True))
        tM_bc = self.ct["tM"]

        @b.vector
        def _(vec):
            for i in range(DC):
                vec.wait_ge(self.sem[xsem[i % 2]], S[xsem[i % 2]] + 16 * (i // 2 + 1))
                if i >= 2:
                    vec.wait_ge(self.sem["tM"], S["tM"] + (i - 1) * 2)
                else:
                    vec.wait_ge(self.sem["tM"], S["tM"])
                    self.wait_stores(vec, S)
                xc = self.xck[:, (i % 2) * TS:(i % 2 + 1) * TS]
                A = self.ltmp[:, (i % 2) * 2 * TS:(i % 2) * 2 * TS + TS]
                B = self.ltmp[:, (i % 2) * 2 * TS + TS:(i % 2 + 1) * 2 * TS]
                tm = self.tmp32[:, (i % 2) * TS:(i % 2 + 1) * TS]
                self.V(vec, lambda A=A, xc=xc: vec.tensor_copy(A, xc))
                self.V(vec, lambda B=B, xc=xc, A=A: vec.tensor_sub(B, xc, A))
                self.V(vec, lambda tm=tm, xc=xc: vec.tensor_mul(tm, xc, xc))
                vec.wait_ge(self.sem["tM"], S["tM"] + i * 2 + 1)
                self.V(vec, lambda A=A, tm=tm: vec.tensor_copy(A, tm))
                self.V(vec, lambda B=B, tm=tm, A=A: vec.tensor_sub(B, tm, A))
            vec.wait_ge(self.sem["tM"], S["tM"] + DC * 2)
            m = self.rows[:, 0:TS]
            r = self.rows[:, TS:2 * TS]
            v2 = self.rows[:, 2 * TS:3 * TS]
            psx = self.bank(4)[0:1, :TS]
            psq = self.bank(5)[0:1, :TS]
            self.V(vec, lambda: vec.tensor_scalar_mul(m, psx, 1.0 / c.D))
            self.V(vec, lambda: vec.tensor_scalar_mul(v2, psq, 1.0 / c.D))
            self.V(vec, lambda: vec.tensor_mul(r, m, m))
            self.V(vec, lambda: vec.tensor_sub(v2, v2, r))
            self.V(vec, lambda: vec.tensor_scalar_add(v2, v2, 1e-5))
            vec.wait_ge(self.sem["aC"], S["aC"] + 1)
            self.V(vec, lambda: vec.reciprocal(r, v2))
            for i in range(DC):
                vec.wait_ge(self.sem[xsem[i % 2]],
                            S[xsem[i % 2]] + 16 * (DC // 2 + i // 2 + 1))
                vec.wait_ge(self.sem["tM"], tM_bc)
                vec.wait_ge(self.sem["aC"], S["aC"] + 3)
                xc = self.xck[:, (i % 2) * TS:(i % 2 + 1) * TS]
                tm = self.tmp32[:, (i % 2) * TS:(i % 2 + 1) * TS]
                ur = self.inl[:, i * TS:(i + 1) * TS]
                ue = self.inl[:, (DC + i) * TS:(DC + i + 1) * TS]
                mb = self.bcast[:, 0:TS]
                rb = self.bcast[:, TS:2 * TS]
                self.V(vec, lambda tm=tm, xc=xc, mb=mb: vec.tensor_sub(tm, xc, mb))
                self.V(vec, lambda tm=tm, rb=rb: vec.tensor_mul(tm, tm, rb))
                self.V(vec, lambda ur=ur, tm=tm: vec.tensor_copy(ur, tm))
                self.V(vec, lambda ue=ue, tm=tm, ur=ur: vec.tensor_sub(ue, tm, ur))

        @b.scalar
        def _(sc):
            sc.wait_ge(self.sem["vC"], S["vC"] + DC * nV1 + 5)
            v2 = self.rows[:, 2 * TS:3 * TS]
            self.A(sc, lambda: sc.activation(v2, v2, AF.Sqrt))
            sc.wait_ge(self.sem["tM"], tM_bc)
            sc.wait_ge(self.sem["vC"], S["vC"] + DC * nV1 + nR)
            self.A(sc, lambda: sc.copy(self.bcast[:, 0:TS], self.bank(2)[:, :TS]))
            self.A(sc, lambda: sc.copy(self.bcast[:, TS:2 * TS], self.bank(3)[:, :TS]))

    def ph_mm(self, wscr, OC, in_src, evac, out_dst, ts, scale_col,
             resid_src=None):
        b, c = self.block, self.c
        DC, TS = c.DC, c.TS
        S = dict(self.ct)
        load_limbs = in_src in ("hLd", "attnL")
        n_in = 2 * DC if load_limbs else 0
        nEv = {"relu": 2, "resid": 1, "qk": 2}[evac]
        nEa = {"relu": 1, "resid": 0, "qk": 1}[evac]
        nStW = {"relu": 2, "resid": 1, "qk": 2}[evac]
        scname, sccol_base = scale_col
        wsem = ("wA", "wB")
        xsem = ("xA", "xB")
        stsem = ("stA", "stB")

        @b.sync
        def _(sync):
            self.wait_stores(sync, S)
            if load_limbs:
                sync.wait_ge(self.sem["tM"], S["tM"])
                srcT = self.hLd if in_src == "hLd" else self.attnL
                for l in range(2):
                    for i in range(DC):
                        ap = (srcT[l, i, :, :] if in_src == "hLd"
                              else srcT[l, i, :, ts * TS:(ts + 1) * TS])
                        dd = self.inl[:, (l * DC + i) * TS:(l * DC + i + 1) * TS]
                        self.dma(sync, "blk", dd, ap)
            for o in range(OC):
                sync.wait_ge(self.sem["tM"],
                             S["tM"] + (o - 1 if o >= 2 else 0))
                self.dma(sync, wsem[o % 2],
                         self.wsb[:, (o % 2) * DC * 128:(o % 2 + 1) * DC * 128],
                         wscr[o, :, :])
                if evac == "resid":
                    sync.wait_ge(self.sem["vC"],
                                 S["vC"] + ((o - 1) * nEv if o >= 2 else 0))
                    self.dma(sync, xsem[o % 2],
                             self.xck[:, (o % 2) * TS:(o % 2 + 1) * TS],
                             resid_src[o, :, ts * TS:(ts + 1) * TS])

        @b.tensor
        def _(t):
            for o in range(OC):
                t.wait_ge(self.sem[wsem[o % 2]], S[wsem[o % 2]] + 16 * (o // 2 + 1))
                if o == 0:
                    if load_limbs:
                        t.wait_ge(self.sem["blk"], S["blk"] + n_in * 16)
                    else:
                        t.wait_ge(self.sem["vC"], S["vC"])
                if o >= 2:
                    if nEa:
                        t.wait_ge(self.sem["aC"], S["aC"] + (o - 1) * nEa)
                    t.wait_ge(self.sem["vC"], S["vC"] + (o - 1) * nEv)
                else:
                    t.wait_ge(self.sem["aC"], S["aC"])
                    t.wait_ge(self.sem["vC"], S["vC"])
                pb = self.bank(o % 2)[:, :TS]
                wbuf = self.wsb[:, (o % 2) * DC * 128:(o % 2 + 1) * DC * 128]
                n = 2 * DC
                k = 0
                ins = None
                for i in range(DC):
                    for l in range(2):
                        lim = self.inl[:, (l * DC + i) * TS:(l * DC + i + 1) * TS]
                        ins = t.matmul(pb, wbuf[:, i * 128:(i + 1) * 128], lim,
                                       start=(k == 0), stop=(k == n - 1))
                        k += 1
                self.tinc(ins)

        @b.scalar
        def _(sc):
            for o in range(OC):
                sc.wait_ge(self.sem["tM"], S["tM"] + o + 1)
                if o >= 2:
                    sc.wait_ge(self.sem["vC"], S["vC"] + (o - 1) * nEv)
                else:
                    sc.wait_ge(self.sem["vC"], S["vC"])
                    self.wait_stores(sc, S)
                pb = self.bank(o % 2)[:, :TS]
                tm = self.tmp32[:, (o % 2) * TS:(o % 2 + 1) * TS]
                lr = self.ltmp[:, (o % 2) * 2 * TS:(o % 2) * 2 * TS + TS]
                le = self.ltmp[:, (o % 2) * 2 * TS + TS:(o % 2 + 1) * 2 * TS]
                if evac == "relu":
                    scv = self.scal(scname)
                    self.A(sc, lambda tm=tm, pb=pb, scv=scv:
                           sc.activation(tm, pb, AF.Relu, scale=scv))
                elif evac == "qk":
                    scv = self.sct[:, sccol_base + o:sccol_base + o + 1]
                    self.A(sc, lambda tm=tm, pb=pb, scv=scv: sc.mul(tm, pb, scv))
                sc.wait_ge(self.sem["vC"], S["vC"] + (o + 1) * nEv)
                if evac == "resid":
                    self.dma(sc, stsem[o % 2],
                             out_dst[o, :, ts * TS:(ts + 1) * TS], tm)
                elif evac == "relu":
                    self.dma(sc, stsem[o % 2], out_dst[0, o, :, :], lr)
                    self.dma(sc, stsem[o % 2], out_dst[1, o, :, :], le)
                else:
                    tgt, rb = out_dst
                    for l, til in ((0, lr), (1, le)):
                        if rb is None:
                            dd = tgt[l, o, :, ts * TS:(ts + 1) * TS]
                        else:
                            r0 = rb + l * (OC * 128) + o * 128
                            dd = tgt[r0:r0 + 128, ts * TS:(ts + 1) * TS]
                        self.dma(sc, stsem[o % 2], dd, til)

        @b.vector
        def _(vec):
            for o in range(OC):
                if nEa:
                    vec.wait_ge(self.sem["aC"], S["aC"] + o + 1)
                else:
                    vec.wait_ge(self.sem["tM"], S["tM"] + o + 1)
                if o >= 2:
                    vec.wait_ge(self.sem[stsem[o % 2]],
                                S[stsem[o % 2]] + 16 * nStW * (o // 2))
                else:
                    self.wait_stores(vec, S)
                tm = self.tmp32[:, (o % 2) * TS:(o % 2 + 1) * TS]
                lr = self.ltmp[:, (o % 2) * 2 * TS:(o % 2) * 2 * TS + TS]
                le = self.ltmp[:, (o % 2) * 2 * TS + TS:(o % 2 + 1) * 2 * TS]
                if evac == "resid":
                    pb = self.bank(o % 2)[:, :TS]
                    xc = self.xck[:, (o % 2) * TS:(o % 2 + 1) * TS]
                    vec.wait_ge(self.sem[xsem[o % 2]],
                                S[xsem[o % 2]] + 16 * (o // 2 + 1))
                    scv = (self.scal(scname) if sccol_base is None
                           else self.sct[:, sccol_base + o:sccol_base + o + 1])
                    self.V(vec, lambda tm=tm, pb=pb, xc=xc, scv=scv:
                           vec.scalar_tensor_tensor(tm, pb, scv, xc,
                                                    ALU.mult, ALU.add))
                else:
                    self.V(vec, lambda lr=lr, tm=tm: vec.tensor_copy(lr, tm))
                    self.V(vec, lambda le=le, tm=tm, lr=lr: vec.tensor_sub(le, tm, lr))

    def ph_qkv(self, src):
        c = self.c
        for ts in range(c.NTS):
            self.ph_ln(src, ts)
            self.ph_mm(self.wqs, c.DC, in_src="sbuf", evac="qk",
                       out_dst=(self.qLd, None), ts=ts, scale_col=("", 0))
            self.ph_mm(self.wks, c.KVC, in_src="sbuf", evac="qk",
                       out_dst=(self.ccink, 0), ts=ts, scale_col=("", 2 * c.DC))
            if ts == c.NTS - 1:
                self.ph_exchange_one(self.ccink, self.ccoutk)
            self.ph_vproj(ts)
        self.ph_exchange_one(self.ccinv, self.ccoutv)

    def ph_exchange_one(self, cin, cout):
        b = self.block
        S = dict(self.ct)

        @b.gpsimd
        def _(gp):
            self.wait_stores(gp, S)
            gp.collective_compute(
                "AllGather", ALU.bypass,
                replica_groups=[[0, 1], [2, 3], [4, 5], [6, 7]],
                ins=[cin.opt()], outs=[cout.opt()],
            ).then_inc(self.sem["CC"], 1)
            self.ct["CC"] += 1

    def ph_vproj(self, ts):
        b, c = self.block, self.c
        DC, TS, KVD = c.DC, c.TS, c.KVD
        TC = c.TC
        VT = max(KVD // 512, 1)
        NTILE = min(512, KVD)
        assert TC <= 4
        wsem = ("wA", "wB")
        stsem = ("stA", "stB")
        nkv = NTILE // 128
        nV = 3

        for vh in range(VT):
            S = dict(self.ct)

            @b.sync
            def _(sync):
                self.wait_stores(sync, S)
                for i in range(DC):
                    sync.wait_ge(self.sem["tM"],
                                 S["tM"] + (1 + i - 1 if i >= 2 else 0))
                    self.dma(sync, wsem[i % 2],
                             self.wsb[:, (i % 2) * DC * 128:
                                      (i % 2) * DC * 128 + NTILE],
                             self.wvs[i, :, vh * 512:vh * 512 + NTILE])

            @b.tensor
            def _(t):
                t.wait_ge(self.sem["vC"], S["vC"])
                t.wait_ge(self.sem["aC"], S["aC"])
                self.tinc(t.matmul(self.bank(5)[:, :NTILE], self.onesf[:],
                                   self.rows[:, 2 * c.T + vh * 512:
                                             2 * c.T + vh * 512 + NTILE],
                                   start=True, stop=True))
                for i in range(DC):
                    t.wait_ge(self.sem[wsem[i % 2]],
                              S[wsem[i % 2]] + 16 * (i // 2 + 1))
                    if i == 0:
                        t.wait_ge(self.sem["aC"], S["aC"] + 1)
                    wbuf = self.wsb[:, (i % 2) * DC * 128:
                                    (i % 2) * DC * 128 + NTILE]
                    ins = None
                    for tc in range(TC):
                        for l in range(2):
                            lim = self.inl[:, (l * DC + i) * TS + tc * 128:
                                           (l * DC + i) * TS + (tc + 1) * 128]
                            ins = t.matmul(self.bank(tc)[:, :NTILE], lim, wbuf,
                                           start=(i == 0 and l == 0),
                                           stop=(i == DC - 1 and l == 1))
                    self.tinc(ins)
            tM_acc = self.ct["tM"]

            @b.scalar
            def _(sc):
                sc.wait_ge(self.sem["tM"], S["tM"] + 1)
                sc.wait_ge(self.sem["vC"], S["vC"])
                vb = self.bcast[:, 0:NTILE]
                self.A(sc, lambda vb=vb: sc.copy(vb, self.bank(5)[:, :NTILE]))

            @b.vector
            def _(vec):
                vec.wait_ge(self.sem["tM"], tM_acc)
                for tc in range(TC):
                    bidx = tc
                    pbk = self.bank(tc)[:, :NTILE]
                    tm = self.tmp32[:, (bidx % 2) * TS:(bidx % 2) * TS + NTILE]
                    lr = self.ltmp[:, (bidx % 2) * 2 * TS:
                                   (bidx % 2) * 2 * TS + NTILE]
                    le = self.ltmp[:, ((bidx % 2) * 2 + 1) * TS:
                                   ((bidx % 2) * 2 + 1) * TS + NTILE]
                    vb = self.bcast[:, 0:NTILE]
                    if bidx >= 2:
                        vec.wait_ge(self.sem[stsem[bidx % 2]],
                                    S[stsem[bidx % 2]] +
                                    16 * 2 * nkv * (bidx // 2))
                    else:
                        self.wait_stores(vec, S)
                    self.V(vec, lambda tm=tm, pbk=pbk, vb=vb:
                           vec.tensor_mul(tm, pbk, vb))
                    self.V(vec, lambda lr=lr, tm=tm: vec.tensor_copy(lr, tm))
                    self.V(vec, lambda le=le, tm=tm, lr=lr:
                           vec.tensor_sub(le, tm, lr))

            @b.scalar
            def _(sc):
                for tc in range(TC):
                    bidx = tc
                    sc.wait_ge(self.sem["vC"], S["vC"] + (bidx + 1) * nV)
                    lr = self.ltmp[:, (bidx % 2) * 2 * TS:
                                   (bidx % 2) * 2 * TS + NTILE]
                    le = self.ltmp[:, ((bidx % 2) * 2 + 1) * TS:
                                   ((bidx % 2) * 2 + 1) * TS + NTILE]
                    kcg = (ts * TS) // 128 + tc
                    for j in range(nkv):
                        kvc = vh * 4 + j if NTILE >= 512 else j
                        for l, til in ((0, lr), (1, le)):
                            self.dma(sc, stsem[bidx % 2],
                                     self.ccinv[(kvc * 2 + l) * 128:
                                                (kvc * 2 + l) * 128 + 128,
                                                kcg * 128:(kcg + 1) * 128],
                                     til[:, j * 128:(j + 1) * 128])

    def att_views(self):
        c = self.c
        T, TS, KC = c.T, c.TS, c.KC
        inl = self.inl
        off = 0
        qA = inl[:, off:off + T]; off += T
        qB = inl[:, off:off + T]; off += T
        kA = inl[:, off:off + 2 * T]; off += 2 * T
        kB = inl[:, off:off + 2 * T]; off += 2 * T
        vA = inl[:, off:off + KC * 128]; off += KC * 128
        vB = inl[:, off:off + KC * 128]; off += KC * 128
        pA = inl[:, off:off + KC * TS]; off += KC * TS
        pB = inl[:, off:off + KC * TS]; off += KC * TS
        assert off <= c.INLW, (off, c.INLW)
        p0 = self.ptm32[:, 0:TS]
        p1 = self.ptm32[:, TS:2 * TS]
        return qA, qB, kA, kB, vA, vB, pA, pB, (p0, p1)

    def ph_att_head(self, h):
        b, c = self.block, self.c
        T, TS, KC, QC, KT = c.T, c.TS, c.KC, c.QC, c.KT
        S = dict(self.ct)
        qA, qB, kA, kB, vA, vB, pA, pB, ptmp = self.att_views()
        kvh = h // c.NREP
        KR = c.KVC * 128
        kcpr = T // 128

        @b.sync
        def _(sync):
            if h == 0:
                sync.wait_ge(self.sem["CC"], 2)
            sync.wait_ge(self.sem["tM"], S["tM"])
            sync.wait_ge(self.sem["vC"], S["vC"])
            self.wait_stores(sync, S)
            self.dma(sync, "blk", qA, self.qLd[0, h, :, :])
            self.dma(sync, "blk", qB, self.qLd[1, h, :, :])
            for r in range(2):
                self.dma(sync, "blk", kA[:, r * T:(r + 1) * T],
                         self.ccoutk[r * 2 * KR + kvh * 128:
                                     r * 2 * KR + (kvh + 1) * 128, :])
                self.dma(sync, "blk", kB[:, r * T:(r + 1) * T],
                         self.ccoutk[r * 2 * KR + KR + kvh * 128:
                                     r * 2 * KR + KR + (kvh + 1) * 128, :])
                for l, vt in ((0, vA), (1, vB)):
                    vr0 = r * (c.KVC * 2 * 128) + (kvh * 2 + l) * 128
                    self.dma(sync, "blk",
                             vt[:, r * kcpr * 128:(r + 1) * kcpr * 128],
                             self.ccoutv[vr0:vr0 + 128, :])
        ld_head = self.ct["blk"]

        nPQ = 2 * KT

        @b.tensor
        def _(t):
            t.wait_ge(self.sem["blk"], ld_head)
            t.wait_ge(self.sem["aC"], S["aC"])
            for qc in range(QC):
                for kt in range(KT):
                    j = qc * KT + kt
                    if j >= 2:
                        qc2, kt2 = divmod(j - 2, KT)
                        t.wait_ge(self.sem["vC"], S["vC"] + qc2 * nPQ + kt2 + 1)
                    self.tinc(t.matmul(self.bank(j % 2)[:, :TS],
                                       qA[:, qc * 128:(qc + 1) * 128],
                                       kA[:, kt * TS:(kt + 1) * TS],
                                       start=True, stop=True))

        @b.vector
        def _(vec):
            vec.wait_ge(self.sem["aC"], S["aC"])
            for qc in range(QC):
                for kt in range(KT):
                    vec.wait_ge(self.sem["tM"], S["tM"] + qc * KT + kt + 1)
                    bidx = (qc * KT + kt) % 2
                    mx = self.bcast[:, kt:kt + 1]
                    self.V(vec, lambda mx=mx, bidx=bidx: vec.reduce_max(
                        mx, self.bank(bidx)[:, :TS], axis=AX))
                mcol = self.bcast[:, 8 + qc:8 + qc + 1]
                self.V(vec, lambda mcol=mcol:
                       vec.tensor_copy(mcol, self.bcast[:, 0:1]))
                for kt in range(1, KT):
                    self.V(vec, lambda mcol=mcol, kt=kt: vec.tensor_max(
                        mcol, mcol, self.bcast[:, kt:kt + 1]))
        vC_stats = self.ct["vC"]

        @b.scalar
        def _(sc):
            sc.wait_ge(self.sem["vC"], vC_stats)
            for qc in range(QC):
                self.dma(sc, "stO", self.mrow_d[h, qc].unsqueeze(1),
                         self.bcast[:, 8 + qc:8 + qc + 1])
        stO_mst = self.ct["stO"]

        @b.sync
        def _(sync):
            sync.wait_ge(self.sem["stO"], stO_mst)
            self.dma(sync, "blk", self.rows[:, 0:T],
                     self.mrow_d[h].rearrange("a b -> (a b)").unsqueeze(0))
        mrow_ld = self.ct["blk"]

        for qt in range(c.NTS):
            self.att_qt(h, qt, qA, qB, kA, kB, vA, vB, pA, pB, ptmp,
                        mrow_ld, stO_mst)

    def att_qt(self, h, qt, qA, qB, kA, kB, vA, vB, pA, pB, ptmp,
               mrow_ld, stO_mst):
        b, c = self.block, self.c
        T, TS, KC = c.T, c.TS, c.KC
        S = dict(self.ct)
        nEvK = 3

        @b.tensor
        def _(t):
            t.wait_ge(self.sem["blk"], mrow_ld)
            t.wait_ge(self.sem["vC"], S["vC"])
            self.tinc(t.matmul(self.bank(4)[:, :TS], self.onesf[:],
                               self.rows[:, qt * TS:(qt + 1) * TS],
                               start=True, stop=True))
            t.wait_ge(self.sem["aC"], S["aC"] + 1)
            for kc in range(KC):
                if kc >= 2:
                    t.wait_ge(self.sem["vC"], S["vC"] + (kc - 1) * nEvK)
                pbk = self.bank(kc % 2)[:, :TS]
                q0 = qA[:, qt * TS:(qt + 1) * TS]
                q1 = qB[:, qt * TS:(qt + 1) * TS]
                kk0 = kA[:, kc * 128:(kc + 1) * 128]
                kk1 = kB[:, kc * 128:(kc + 1) * 128]
                t.matmul(pbk, kk0, q0, start=True, stop=False)
                t.matmul(pbk, kk1, q0, start=False, stop=False)
                self.tinc(t.matmul(pbk, kk0, q1, start=False, stop=True))
            t.wait_ge(self.sem["vC"], S["vC"] + KC * nEvK)
            psz = self.bank(5)[0:1, :TS]
            n = 2 * KC
            k = 0
            ins = None
            for kc in range(KC):
                for pl in (pA, pB):
                    ins = t.matmul(psz, self.onescr[:],
                                   pl[:, kc * TS:(kc + 1) * TS],
                                   start=(k == 0), stop=(k == n - 1))
                    k += 1
            self.tinc(ins)
            t.wait_ge(self.sem["vC"], S["vC"] + KC * nEvK + 1)
            self.tinc(t.matmul(self.bank(6)[:, :TS], self.onesf[:],
                               self.rows[:, T:T + TS], start=True, stop=True))
            psav = self.bank(3)[:, :TS]
            n = 3 * KC
            k = 0
            for kc in range(KC):
                for (vt, pl) in ((vA, pA), (vA, pB), (vB, pA)):
                    ins = t.matmul(psav, vt[:, kc * 128:(kc + 1) * 128],
                                   pl[:, kc * TS:(kc + 1) * TS],
                                   start=(k == 0), stop=(k == n - 1))
                    k += 1
            self.tinc(ins)

        @b.scalar
        def _(sc):
            sc.wait_ge(self.sem["tM"], S["tM"] + 1)
            sc.wait_ge(self.sem["stO"], stO_mst)
            sc.wait_ge(self.sem["vC"], S["vC"])
            self.A(sc, lambda: sc.copy(self.bcast[:, 0:TS], self.bank(4)[:, :TS]))
            for kc in range(KC):
                sc.wait_ge(self.sem["vC"], S["vC"] + kc * nEvK + 1)
                tm = self.tmp32[:, (kc % 2) * TS:(kc % 2 + 1) * TS]
                pt = ptmp[kc % 2]
                self.A(sc, lambda pt=pt, tm=tm: sc.activation(pt, tm, AF.Exp))
            sc.wait_ge(self.sem["tM"], S["tM"] + 1 + KC + 2)
            self.A(sc, lambda: sc.copy(self.bcast[:, TS:2 * TS],
                                       self.bank(6)[:, :TS]))

        @b.vector
        def _(vec):
            for kc in range(KC):
                vec.wait_ge(self.sem["tM"], S["tM"] + 1 + kc + 1)
                if kc >= 2:
                    vec.wait_ge(self.sem["aC"], S["aC"] + 1 + kc - 1)
                tm = self.tmp32[:, (kc % 2) * TS:(kc % 2 + 1) * TS]
                pt = ptmp[kc % 2]
                pbk = self.bank(kc % 2)[:, :TS]
                self.V(vec, lambda tm=tm, pbk=pbk: vec.tensor_sub(
                    tm, pbk, self.bcast[:, 0:TS]))
                vec.wait_ge(self.sem["aC"], S["aC"] + 1 + kc + 1)
                pr = pA[:, kc * TS:(kc + 1) * TS]
                pe = pB[:, kc * TS:(kc + 1) * TS]
                self.V(vec, lambda pr=pr, pt=pt: vec.tensor_copy(pr, pt))
                self.V(vec, lambda pe=pe, pt=pt, pr=pr: vec.tensor_sub(pe, pt, pr))
            vec.wait_ge(self.sem["tM"], S["tM"] + 1 + KC + 1)
            rz = self.rows[:, T:T + TS]
            self.V(vec, lambda rz=rz: vec.reciprocal(rz, self.bank(5)[0:1, :TS]))
            vec.wait_ge(self.sem["tM"], S["tM"] + 1 + KC + 3)
            vec.wait_ge(self.sem["aC"], S["aC"] + 1 + KC + 1)
            self.wait_stores(vec, S)
            tm = self.tmp32[:, 0:TS]
            lr = self.ltmp[:, 0:TS]
            le = self.ltmp[:, TS:2 * TS]
            self.V(vec, lambda tm=tm: vec.tensor_mul(
                tm, self.bank(3)[:, :TS], self.bcast[:, TS:2 * TS]))
            self.V(vec, lambda lr=lr, tm=tm: vec.tensor_copy(lr, tm))
            self.V(vec, lambda le=le, tm=tm, lr=lr: vec.tensor_sub(le, tm, lr))

        @b.scalar
        def _(sc):
            sc.wait_ge(self.sem["vC"], self.ct["vC"])
            lr = self.ltmp[:, 0:TS]
            le = self.ltmp[:, TS:2 * TS]
            self.dma(sc, "stO", self.attnL[0, h, :, qt * TS:(qt + 1) * TS], lr)
            self.dma(sc, "stO", self.attnL[1, h, :, qt * TS:(qt + 1) * TS], le)

    def ph_end(self):
        b = self.block
        S = dict(self.ct)

        @b.gpsimd
        def _(gp):
            for s in ALL_SEMS:
                if S[s] > 0:
                    gp.wait_ge(self.sem[s], S[s])


def make_inputs_per_core(cfg, core, full):
    c = cfg
    seq = core // 2
    half = core % 2
    x = np.asarray(full["L0_attn_hidden_states"])
    xs = np.ascontiguousarray(
        x[seq, half * c.T:(half + 1) * c.T, :].T).reshape(c.DC, 128, c.T)

    def u8(a):
        a = np.asarray(a)
        return a.view(np.uint8) if a.dtype == np.bool_ else a.astype(np.uint8)

    cei = np.ones(c.D, dtype=np.float32)
    cei[c.NE:] = np.float32(c.EI_NEG)
    return {
        "x0": xs.astype(np.float32),
        "wq_n": np.asarray(full["L0_attn_q_weight"], dtype=np.float32),
        "wk_n": np.asarray(full["L0_attn_k_weight"], dtype=np.float32),
        "wv_n": np.asarray(full["L0_attn_v_weight"], dtype=np.float32),
        "wo_n": np.asarray(full["L0_attn_o_weight"], dtype=np.float32),
        "qsc_in": np.asarray(full["L0_attn_q_scale"], dtype=np.float32),
        "ksc_in": np.asarray(full["L0_attn_k_scale"], dtype=np.float32),
        "vsc_in": np.asarray(full["L0_attn_v_scale"], dtype=np.float32),
        "osc_in": np.asarray(full["L0_attn_o_scale"], dtype=np.float32),
        "wqT": np.ascontiguousarray(np.asarray(full["L0_attn_q_weight"], dtype=np.float32).T),
        "wkT": np.ascontiguousarray(np.asarray(full["L0_attn_k_weight"], dtype=np.float32).T),
        "wvT": np.ascontiguousarray(np.asarray(full["L0_attn_v_weight"], dtype=np.float32).T),
        "woT": np.ascontiguousarray(np.asarray(full["L0_attn_o_weight"], dtype=np.float32).T),
        "k0u": u8(full["L0_ffn_k0"]),
        "k1u": u8(full["L0_ffn_k1"]),
        "k2u": u8(full["L0_ffn_k2"]),
        "s0": np.asarray(full["L0_ffn_s0"], dtype=np.float32).reshape(1, 1),
        "s1": np.asarray(full["L0_ffn_s1"], dtype=np.float32).reshape(1, 1),
        "s2": np.asarray(full["L0_ffn_s2"], dtype=np.float32).reshape(1, 1),
        "cei": np.ascontiguousarray(cei.reshape(c.DC, 128).T),
        "cei2": np.ascontiguousarray((2.0 * cei).reshape(c.DC, 128).T),
    }


def build_program(cfg, name="mkmac1", phase_limit=None):
    nc = bass.Bass("TRN2", target_bir_lowering=False, name=name)
    Build(nc, cfg, phase_limit=phase_limit).build()
    return nc


def kernel(**inputs):
    cfg = CFG()
    nc = build_program(cfg, name="mkmacaron_v2")
    in_maps = [make_inputs_per_core(cfg, core, inputs) for core in range(8)]
    res = run_bass_kernel_spmd(nc, in_maps, core_ids=list(range(8)))
    out = np.empty((4, 2048, 4096), dtype=np.float32)
    for core in range(8):
        seq, half = core // 2, core % 2
        y = res.results[core]["yout"].reshape(cfg.D, cfg.T)
        out[seq, half * cfg.T:(half + 1) * cfg.T, :] = y.T
    return out


# version 09
# speedup vs baseline: 1.2364x; 1.2364x over previous
import numpy as np
import concourse.bass as bass
import concourse.mybir as mybir
from concourse.bass_utils import run_bass_kernel_spmd
from contextlib import ExitStack

F32 = mybir.dt.float32
F32R = mybir.dt.float32r
U8 = mybir.dt.uint8
AF = mybir.ActivationFunctionType
ALU = mybir.AluOpType
AX = mybir.AxisListType.X

LOAD_SEMS = ["blk", "wA", "wB", "xA", "xB", "kA", "kB"]
STORE_SEMS = ["stA", "stB", "stC", "stD", "stO"]
ALL_SEMS = LOAD_SEMS + STORE_SEMS + ["tM", "vC", "aC", "gC", "CC"]


class CFG:
    def __init__(self, D=4096, T=1024, HD=128, KVH=8, P_EI=0.8):
        self.D = D
        self.T = T
        self.HD = HD
        self.DC = D // 128
        self.HEADS = D // HD
        self.KVH = KVH
        self.KVD = KVH * HD
        self.KVC = self.KVD // 128
        self.NREP = self.HEADS // self.KVH
        self.TS = T // 2
        self.NTS = 2
        self.TC = self.TS // 128
        self.SEQK = 2 * T
        self.KC = self.SEQK // 128
        self.QC = T // 128
        self.KT = self.SEQK // self.TS
        self.NE = round(D * P_EI)
        self.EI_NEG = -P_EI / (1.0 - P_EI)
        att_need = (2 * self.T + 2 * 2 * self.T + 2 * self.KC * 128 +
                    2 * self.KC * self.TS + 2 * self.TS)
        self.INLW = max(2 * self.DC * self.TS, att_need, 2 * D)
        assert self.TS <= 512 and HD == 128
        assert self.DC % 4 == 0


class Build:
    def __init__(self, nc, cfg, phase_limit=None):
        self.nc = nc
        self.c = cfg
        self.phase_limit = phase_limit
        self.ct = {s: 0 for s in ALL_SEMS}

    def V(self, vec, make):
        vec.wait_ge(self.sem["vC"], self.ct["vC"])
        make().then_inc(self.sem["vC"], 1)
        self.ct["vC"] += 1

    def A(self, sc, make):
        sc.wait_ge(self.sem["aC"], self.ct["aC"])
        make().then_inc(self.sem["aC"], 1)
        self.ct["aC"] += 1

    def G(self, gp, make):
        gp.wait_ge(self.sem["gC"], self.ct["gC"])
        make().then_inc(self.sem["gC"], 1)
        self.ct["gC"] += 1

    def tinc(self, ins):
        ins.then_inc(self.sem["tM"], 1)
        self.ct["tM"] += 1

    def dma(self, eng, stream, out, in_, **kw):
        eng.dma_start(out=out, in_=in_, **kw).then_inc(self.sem[stream], 16)
        self.ct[stream] += 16

    def wait_stores(self, eng, S):
        for s in STORE_SEMS:
            if S[s] > 0:
                eng.wait_ge(self.sem[s], S[s])

    def build(self):
        nc, c = self.nc, self.c
        D, T, TS, DC = c.D, c.T, c.TS, c.DC
        dt = nc.dram_tensor
        self.x0 = dt("x0", [DC, 128, T], F32, kind="ExternalInput").ap()
        self.wq_n = dt("wq_n", [D, D], F32, kind="ExternalInput").ap()
        self.wk_n = dt("wk_n", [c.KVD, D], F32, kind="ExternalInput").ap()
        self.wv_n = dt("wv_n", [c.KVD, D], F32, kind="ExternalInput").ap()
        self.wo_n = dt("wo_n", [D, D], F32, kind="ExternalInput").ap()
        self.qsc_in = dt("qsc_in", [D], F32, kind="ExternalInput").ap()
        self.ksc_in = dt("ksc_in", [c.KVD], F32, kind="ExternalInput").ap()
        self.vsc_in = dt("vsc_in", [c.KVD], F32, kind="ExternalInput").ap()
        self.osc_in = dt("osc_in", [D], F32, kind="ExternalInput").ap()
        self.k0u = dt("k0u", [D, D], U8, kind="ExternalInput").ap()
        self.k1u = dt("k1u", [D, D], U8, kind="ExternalInput").ap()
        self.k2u = dt("k2u", [D, D], U8, kind="ExternalInput").ap()
        self.s0 = dt("s0", [1, 1], F32, kind="ExternalInput").ap()
        self.s1 = dt("s1", [1, 1], F32, kind="ExternalInput").ap()
        self.s2 = dt("s2", [1, 1], F32, kind="ExternalInput").ap()
        self.wqT = dt("wqT", [D, D], F32, kind="ExternalInput").ap()
        self.wkT = dt("wkT", [D, c.KVD], F32, kind="ExternalInput").ap()
        self.wvT = dt("wvT", [D, c.KVD], F32, kind="ExternalInput").ap()
        self.woT = dt("woT", [D, D], F32, kind="ExternalInput").ap()
        self.cei = dt("cei", [128, DC], F32, kind="ExternalInput").ap()
        self.cei2 = dt("cei2", [128, DC], F32, kind="ExternalInput").ap()
        self.yout = dt("yout", [DC, 128, T], F32, kind="ExternalOutput").ap()

        self.xcur = dt("xcur", [DC, 128, T], F32).ap()
        self.w0s = dt("w0s", [DC, 128, DC * 128], F32R).ap()
        self.w1s = dt("w1s", [DC, 128, DC * 128], F32R).ap()
        self.w2s = dt("w2s", [DC, 128, DC * 128], F32R).ap()
        self.wqs = dt("wqs", [DC, 128, DC * 128], F32R).ap()
        self.wks = dt("wks", [c.KVC, 128, DC * 128], F32R).ap()
        self.wos = dt("wos", [DC, 128, DC * 128], F32R).ap()
        self.wvs = dt("wvs", [DC, 128, c.KVD], F32R).ap()
        self.qLd = dt("qLd", [2, DC, 128, T], F32R).ap()
        self.hLd = dt("hLd", [2, DC, 128, TS], F32R).ap()
        self.ccink = dt("ccink", [2 * c.KVC * 128, T], F32R).ap()
        self.ccoutk = dt("ccoutk", [2 * 2 * c.KVC * 128, T], F32R).ap()
        self.ccinv = dt("ccinv", [c.KVC * 2 * 128, T], F32R).ap()
        self.ccoutv = dt("ccoutv", [2 * c.KVC * 2 * 128, T], F32R).ap()
        self.attnL = dt("attnL", [2, DC, 128, T], F32R).ap()
        self.mrow_d = dt("mrow_d", [c.HEADS, c.QC, 128], F32).ap()
        self.vrow_d = dt("vrow_d", [c.KVC, 128], F32).ap()

        es = ExitStack()
        sb, ps = nc.sbuf_tensor, nc.psum_tensor
        E = es.enter_context
        self.inl = E(sb("inl", [128, c.INLW], F32R))
        self.wsb = E(sb("wsb", [128, 2 * DC * 128], F32R))
        self.ksb = E(sb("ksb", [128, 2 * D], U8))
        self.tmp32 = E(sb("tmp32", [128, 2 * TS], F32))
        self.ltmp = E(sb("ltmp", [128, 4 * TS], F32R))
        self.xck = E(sb("xck", [128, 2 * TS], F32))
        self.bcast = E(sb("bcast", [128, 2 * TS], F32))
        self.ptm32 = E(sb("ptm32", [128, 2 * TS], F32))
        self.rows = E(sb("rows", [1, 3 * T], F32))
        self.sct = E(sb("sct", [128, 2 * DC + c.KVC + 8], F32))
        self.ceis = E(sb("ceis", [128, 2 * DC], F32))
        self.onesf = E(sb("onesf", [1, 128], F32))
        self.onescf = E(sb("onescf", [128, 1], F32))
        self.onescr = E(sb("onescr", [128, 1], F32R))
        self.psb = E(ps("psb", [128, 4096], F32))

        self.sem = {}
        for s in ALL_SEMS:
            self.sem[s] = E(nc.semaphore(s))
        self.block = E(nc.Block())
        self.emit_all(phase_limit=self.phase_limit)
        es.close()
        return nc

    def bank(self, i):
        return self.psb[:, i * 512:(i + 1) * 512]

    def scal(self, name):
        c = self.c
        base = 2 * c.DC + c.KVC
        idx = {"s0": base, "s1": base + 1, "s2": base + 2,
               "rs0": base + 3, "rs1": base + 4, "hs2": base + 5}[name]
        return self.sct[:, idx:idx + 1]

    def emit_all(self, phase_limit=None):
        c = self.c
        phases = [self.ph_consts,
                  lambda: self.ph_conv_ffn(self.k0u, self.w0s, use_c=False),
                  lambda: self.ph_conv_ffn(self.k1u, self.w1s, use_c=True),
                  lambda: self.ph_conv_ffn(self.k2u, self.w2s, use_c=True),
                  self.ph_conv_att_q,
                  self.ph_conv_att_k,
                  self.ph_conv_att_v,
                  self.ph_conv_att_o,
                  self.ph_conv_att_tail,
                  lambda: self.ph_ffn(src=self.x0, dst=self.xcur),
                  lambda: self.ph_qkv(src=self.xcur)]
        phases += [(lambda h=h: self.ph_att_head(h)) for h in range(c.HEADS)]
        phases += [(lambda ts=ts: self.ph_mm(
            self.wos, c.DC, in_src="attnL", evac="resid",
            resid_src=self.xcur, out_dst=self.xcur, ts=ts,
            scale_col=(None, c.DC))) for ts in range(c.NTS)]
        phases += [lambda: self.ph_ffn(src=self.xcur, dst=self.yout)]
        if phase_limit is None:
            phase_limit = len(phases)
        for ph in phases[:phase_limit]:
            ph()
        self.ph_end()

    def ph_consts(self):
        b, c = self.block, self.c

        @b.sync
        def _(sync):
            self.dma(sync, "blk", self.ceis[:, 0:c.DC], self.cei)
            self.dma(sync, "blk", self.ceis[:, c.DC:2 * c.DC], self.cei2)
            base = 2 * c.DC + c.KVC
            for j, s in enumerate([self.s0, self.s1, self.s2]):
                self.dma(sync, "blk", self.sct[:, base + j:base + j + 1],
                         bass.AP(s.tensor, 0, [[0, 128], [1, 1]]),
                         allow_slow_non_contiguous=True)
        blk_done = self.ct["blk"]

        @b.vector
        def _(vec):
            vec.wait_ge(self.sem["blk"], blk_done)
            self.V(vec, lambda: vec.memset(self.onesf[:], 1.0))
            self.V(vec, lambda: vec.memset(self.onescf[:], 1.0))
            self.V(vec, lambda: vec.tensor_copy(self.onescr[:], self.onescf[:]))
            base = 2 * c.DC + c.KVC
            self.V(vec, lambda: vec.tensor_scalar_mul(
                self.sct[:, base + 3:base + 4], self.sct[:, base:base + 1], 2.0))
            self.V(vec, lambda: vec.tensor_scalar_mul(
                self.sct[:, base + 4:base + 5], self.sct[:, base + 1:base + 2], 2.0))
            self.V(vec, lambda: vec.tensor_scalar_mul(
                self.sct[:, base + 5:base + 6], self.sct[:, base + 2:base + 3], 0.5))

    def ph_conv_ffn(self, ksrc, wdst, use_c):
        b, c = self.block, self.c
        DC = c.DC
        S = dict(self.ct)
        nEv = 1
        GRP = 4
        nStW = DC // GRP
        ksem = ("kA", "kB")
        stsem = ("stA", "stB")

        @b.sync
        def _(sync):
            for i in range(DC):
                sync.wait_ge(self.sem["vC"],
                             S["vC"] + ((i - 1) * nEv if i >= 2 else 0))
                self.dma(sync, ksem[i % 2],
                         self.ksb[:, (i % 2) * c.D:(i % 2 + 1) * c.D],
                         ksrc[i * 128:(i + 1) * 128, :])

        @b.vector
        def _(vec):
            for i in range(DC):
                vec.wait_ge(self.sem[ksem[i % 2]],
                            S[ksem[i % 2]] + 16 * (i // 2 + 1))
                if i >= 2:
                    vec.wait_ge(self.sem[stsem[i % 2]],
                                S[stsem[i % 2]] + 16 * nStW * (i // 2))
                else:
                    self.wait_stores(vec, S)
                    vec.wait_ge(self.sem["tM"], S["tM"])
                    vec.wait_ge(self.sem["aC"], S["aC"])
                sgn = self.inl[:, (i % 2) * c.D:(i % 2 + 1) * c.D]
                ksl = self.ksb[:, (i % 2) * c.D:(i % 2 + 1) * c.D]
                if use_c:
                    cc = self.ceis[:, i:i + 1]
                    c2 = self.ceis[:, DC + i:DC + i + 1]
                    self.V(vec, lambda sgn=sgn, ksl=ksl, cc=cc, c2=c2:
                           vec.tensor_scalar(sgn, ksl, c2, cc, ALU.mult, ALU.subtract))
                else:
                    self.V(vec, lambda sgn=sgn, ksl=ksl:
                           vec.tensor_scalar(sgn, ksl, 2.0, 1.0, ALU.mult, ALU.subtract))

        @b.gpsimd
        def _(gp):
            for i in range(DC):
                gp.wait_ge(self.sem["vC"], S["vC"] + (i + 1) * nEv)
                sgn = self.inl[:, (i % 2) * c.D:(i % 2 + 1) * c.D]
                for g in range(DC // GRP):
                    oc0 = g * GRP
                    self.dma(gp, stsem[i % 2],
                             wdst[oc0:oc0 + GRP, :, i * 128:(i + 1) * 128]
                             .transpose([1, 0, 2]),
                             sgn[:, oc0 * 128:(oc0 + GRP) * 128]
                             .rearrange("p (g o) -> p g o", o=128))

    def ph_conv_att_q(self):
        c = self.c
        rsq = float(np.float32(1.0) / np.float32(np.sqrt(np.float64(c.HD))))
        self.ph_scales(self.wq_n, self.qsc_in, c.DC, 0, rsq, vrow=False)
        self.ph_sign(self.wqT, self.wqs, c.DC, whole=False)

    def ph_conv_att_k(self):
        c = self.c
        self.ph_scales(self.wk_n, self.ksc_in, c.KVC, 2 * c.DC, 1.0, vrow=False)
        self.ph_sign(self.wkT, self.wks, c.KVC, whole=False)

    def ph_conv_att_v(self):
        c = self.c
        self.ph_scales(self.wv_n, self.vsc_in, c.KVC, -1, 1.0, vrow=True)
        self.ph_sign(self.wvT, self.wvs, c.KVC, whole=True)

    def ph_conv_att_o(self):
        c = self.c
        self.ph_scales(self.wo_n, self.osc_in, c.DC, c.DC, 1.0, vrow=False)
        self.ph_sign(self.woT, self.wos, c.DC, whole=False)

    def ph_conv_att_tail(self):
        b, c = self.block, self.c
        S = dict(self.ct)

        @b.sync
        def _(sync):
            self.wait_stores(sync, S)
            self.dma(sync, "blk", self.rows[:, 2 * c.T:2 * c.T + c.KVD],
                     self.vrow_d.rearrange("a b -> (a b)").unsqueeze(0))

    def ph_scales(self, wn, scin, OC, sccol, xsc, vrow):
        b, c = self.block, self.c
        S = dict(self.ct)
        nEv = 3
        wsem = ("wA", "wB")

        @b.sync
        def _(sync):
            for o in range(OC):
                if o >= 2:
                    sync.wait_ge(self.sem["vC"], S["vC"] + (o - 1) * nEv)
                else:
                    sync.wait_ge(self.sem["vC"], S["vC"])
                    sync.wait_ge(self.sem["tM"], S["tM"])
                    self.wait_stores(sync, S)
                slab = self.inl[:, 2 * c.D + (o % 2) * c.D:
                                2 * c.D + (o % 2 + 1) * c.D]
                self.dma(sync, wsem[o % 2], slab,
                         wn[o * 128:(o + 1) * 128, :].bitcast(F32R))
                self.dma(sync, wsem[o % 2], self.bcast[:, (o % 2):(o % 2) + 1],
                         bass.AP(scin.tensor, o * 128, [[1, 128], [1, 1]]),
                         allow_slow_non_contiguous=True)

        @b.vector
        def _(vec):
            for o in range(OC):
                vec.wait_ge(self.sem[wsem[o % 2]],
                            S[wsem[o % 2]] + 32 * (o // 2 + 1))
                wslab = self.inl[:, 2 * c.D + (o % 2) * c.D:
                                 2 * c.D + (o % 2 + 1) * c.D].bitcast(F32)
                srow = self.bcast[:, 2 + (o % 2):3 + (o % 2)]
                self.V(vec, lambda srow=srow, w=wslab: vec.reduce_sum(
                    srow, w, axis=AX, apply_absolute_value=True))
                self.V(vec, lambda srow=srow: vec.tensor_scalar(
                    srow, srow, 1.0 / c.D, 1e-8, ALU.mult, ALU.max))
                scch = self.bcast[:, (o % 2):(o % 2) + 1]
                if vrow:
                    out = self.bcast[:, 4 + (o % 2):5 + (o % 2)]
                    vsem = ("stC", "stD")[o % 2]
                    vec.wait_ge(self.sem[vsem], S[vsem] + 16 * (o // 2))
                else:
                    out = self.sct[:, sccol + o:sccol + o + 1]
                self.V(vec, lambda out=out, srow=srow, scch=scch:
                       vec.scalar_tensor_tensor(out, srow, xsc, scch,
                                                ALU.mult, ALU.mult))

        if vrow:
            @b.gpsimd
            def _(gp):
                for o in range(OC):
                    gp.wait_ge(self.sem["vC"], S["vC"] + (o + 1) * nEv)
                    self.dma(gp, ("stC", "stD")[o % 2],
                             self.vrow_d[o].unsqueeze(1),
                             self.bcast[:, 4 + (o % 2):5 + (o % 2)])

    def ph_sign(self, srcT, wdst, OC, whole):
        b, c = self.block, self.c
        DC = c.DC
        S = dict(self.ct)
        W = OC * 128
        nEa = 1
        GRP = 4 if OC % 4 == 0 else 1
        nStW = 1 if whole else OC // GRP
        wsem = ("wA", "wB")
        stsem = ("stA", "stB")

        @b.sync
        def _(sync):
            for i in range(DC):
                if i >= 2:
                    sync.wait_ge(self.sem["aC"], S["aC"] + (i - 1) * nEa)
                else:
                    sync.wait_ge(self.sem["aC"], S["aC"])
                    sync.wait_ge(self.sem["vC"], S["vC"])
                    sync.wait_ge(self.sem["tM"], S["tM"])
                    self.wait_stores(sync, S)
                slab = self.inl[:, 2 * c.D + (i % 2) * c.D:
                                2 * c.D + (i % 2) * c.D + W]
                self.dma(sync, wsem[i % 2], slab,
                         srcT[i * 128:(i + 1) * 128, :].bitcast(F32R))

        @b.scalar
        def _(sc):
            for i in range(DC):
                sc.wait_ge(self.sem[wsem[i % 2]],
                           S[wsem[i % 2]] + 16 * (i // 2 + 1))
                if i >= 2:
                    sc.wait_ge(self.sem[stsem[i % 2]],
                               S[stsem[i % 2]] + 16 * nStW * (i // 2))
                else:
                    self.wait_stores(sc, S)
                wslab = self.inl[:, 2 * c.D + (i % 2) * c.D:
                                 2 * c.D + (i % 2) * c.D + W].bitcast(F32)
                sgn = self.inl[:, (i % 2) * c.D:(i % 2) * c.D + W]
                self.A(sc, lambda sgn=sgn, w=wslab: sc.activation(sgn, w, AF.Sign))

        @b.gpsimd
        def _(gp):
            for i in range(DC):
                gp.wait_ge(self.sem["aC"], S["aC"] + (i + 1) * nEa)
                sgn = self.inl[:, (i % 2) * c.D:(i % 2) * c.D + W]
                if whole:
                    self.dma(gp, stsem[i % 2], wdst[i, :, :], sgn)
                else:
                    for g in range(OC // GRP):
                        oc0 = g * GRP
                        if GRP == 1:
                            self.dma(gp, stsem[i % 2],
                                     wdst[oc0, :, i * 128:(i + 1) * 128],
                                     sgn[:, oc0 * 128:(oc0 + 1) * 128])
                        else:
                            self.dma(gp, stsem[i % 2],
                                     wdst[oc0:oc0 + GRP, :, i * 128:(i + 1) * 128]
                                     .transpose([1, 0, 2]),
                                     sgn[:, oc0 * 128:(oc0 + GRP) * 128]
                                     .rearrange("p (g o) -> p g o", o=128))

    def ph_ffn(self, src, dst):
        c = self.c
        for ts in range(c.NTS):
            self.ph_ln(src, ts)
            self.ph_mm(self.w0s, c.DC, in_src="sbuf", evac="relu",
                       out_dst=self.hLd, ts=ts, scale_col=("rs0", None))
            self.ph_mm(self.w1s, c.DC, in_src="hLd", evac="relu",
                       out_dst=self.hLd, ts=ts, scale_col=("rs1", None))
            self.ph_mm(self.w2s, c.DC, in_src="hLd", evac="resid",
                       resid_src=src, out_dst=dst, ts=ts, scale_col=("hs2", None))

    def ph_ln(self, src, ts):
        b, c = self.block, self.c
        DC, TS = c.DC, c.TS
        S = dict(self.ct)
        nV1, nV2, nR = 5, 4, 6
        xsem = ("xA", "xB")

        @b.sync
        def _(sync):
            self.wait_stores(sync, S)
            for i in range(DC):
                sync.wait_ge(self.sem["vC"],
                             S["vC"] + ((i - 1) * nV1 if i >= 2 else 0))
                self.dma(sync, xsem[i % 2],
                         self.xck[:, (i % 2) * TS:(i % 2 + 1) * TS],
                         src[i, :, ts * TS:(ts + 1) * TS])
            for i in range(DC):
                sync.wait_ge(self.sem["vC"], S["vC"] + DC * nV1 + nR +
                             max(i - 1, 0) * nV2)
                self.dma(sync, xsem[i % 2],
                         self.xck[:, (i % 2) * TS:(i % 2 + 1) * TS],
                         src[i, :, ts * TS:(ts + 1) * TS])

        @b.tensor
        def _(t):
            t.wait_ge(self.sem["aC"], S["aC"])
            for i in range(DC):
                t.wait_ge(self.sem["vC"], S["vC"] + i * nV1 + 2)
                A = self.ltmp[:, (i % 2) * 2 * TS:(i % 2) * 2 * TS + TS]
                B = self.ltmp[:, (i % 2) * 2 * TS + TS:(i % 2 + 1) * 2 * TS]
                psx = self.bank(4)[0:1, :TS]
                psq = self.bank(5)[0:1, :TS]
                t.matmul(psx, self.onescr[:], A, start=(i == 0), stop=False)
                self.tinc(t.matmul(psx, self.onescr[:], B, start=False,
                                   stop=(i == DC - 1)))
                t.wait_ge(self.sem["vC"], S["vC"] + i * nV1 + 5)
                t.matmul(psq, self.onescr[:], A, start=(i == 0), stop=False)
                self.tinc(t.matmul(psq, self.onescr[:], B, start=False,
                                   stop=(i == DC - 1)))
            t.wait_ge(self.sem["vC"], S["vC"] + DC * nV1 + nR)
            self.tinc(t.matmul(self.bank(2)[:, :TS], self.onesf[:],
                               self.rows[:, 0:TS], start=True, stop=True))
            self.tinc(t.matmul(self.bank(3)[:, :TS], self.onesf[:],
                               self.rows[:, TS:2 * TS], start=True, stop=True))
        tM_bc = self.ct["tM"]

        @b.vector
        def _(vec):
            for i in range(DC):
                vec.wait_ge(self.sem[xsem[i % 2]], S[xsem[i % 2]] + 16 * (i // 2 + 1))
                if i >= 2:
                    vec.wait_ge(self.sem["tM"], S["tM"] + (i - 1) * 2)
                else:
                    vec.wait_ge(self.sem["tM"], S["tM"])
                    self.wait_stores(vec, S)
                xc = self.xck[:, (i % 2) * TS:(i % 2 + 1) * TS]
                A = self.ltmp[:, (i % 2) * 2 * TS:(i % 2) * 2 * TS + TS]
                B = self.ltmp[:, (i % 2) * 2 * TS + TS:(i % 2 + 1) * 2 * TS]
                tm = self.tmp32[:, (i % 2) * TS:(i % 2 + 1) * TS]
                self.V(vec, lambda A=A, xc=xc: vec.tensor_copy(A, xc))
                self.V(vec, lambda B=B, xc=xc, A=A: vec.tensor_sub(B, xc, A))
                self.V(vec, lambda tm=tm, xc=xc: vec.tensor_mul(tm, xc, xc))
                vec.wait_ge(self.sem["tM"], S["tM"] + i * 2 + 1)
                self.V(vec, lambda A=A, tm=tm: vec.tensor_copy(A, tm))
                self.V(vec, lambda B=B, tm=tm, A=A: vec.tensor_sub(B, tm, A))
            vec.wait_ge(self.sem["tM"], S["tM"] + DC * 2)
            m = self.rows[:, 0:TS]
            r = self.rows[:, TS:2 * TS]
            v2 = self.rows[:, 2 * TS:3 * TS]
            psx = self.bank(4)[0:1, :TS]
            psq = self.bank(5)[0:1, :TS]
            self.V(vec, lambda: vec.tensor_scalar_mul(m, psx, 1.0 / c.D))
            self.V(vec, lambda: vec.tensor_scalar_mul(v2, psq, 1.0 / c.D))
            self.V(vec, lambda: vec.tensor_mul(r, m, m))
            self.V(vec, lambda: vec.tensor_sub(v2, v2, r))
            self.V(vec, lambda: vec.tensor_scalar_add(v2, v2, 1e-5))
            vec.wait_ge(self.sem["aC"], S["aC"] + 1)
            self.V(vec, lambda: vec.reciprocal(r, v2))
            for i in range(DC):
                vec.wait_ge(self.sem[xsem[i % 2]],
                            S[xsem[i % 2]] + 16 * (DC // 2 + i // 2 + 1))
                vec.wait_ge(self.sem["tM"], tM_bc)
                vec.wait_ge(self.sem["aC"], S["aC"] + 3)
                xc = self.xck[:, (i % 2) * TS:(i % 2 + 1) * TS]
                tm = self.tmp32[:, (i % 2) * TS:(i % 2 + 1) * TS]
                ur = self.inl[:, i * TS:(i + 1) * TS]
                ue = self.inl[:, (DC + i) * TS:(DC + i + 1) * TS]
                mb = self.bcast[:, 0:TS]
                rb = self.bcast[:, TS:2 * TS]
                self.V(vec, lambda tm=tm, xc=xc, mb=mb: vec.tensor_sub(tm, xc, mb))
                self.V(vec, lambda tm=tm, rb=rb: vec.tensor_mul(tm, tm, rb))
                self.V(vec, lambda ur=ur, tm=tm: vec.tensor_copy(ur, tm))
                self.V(vec, lambda ue=ue, tm=tm, ur=ur: vec.tensor_sub(ue, tm, ur))

        @b.scalar
        def _(sc):
            sc.wait_ge(self.sem["vC"], S["vC"] + DC * nV1 + 5)
            v2 = self.rows[:, 2 * TS:3 * TS]
            self.A(sc, lambda: sc.activation(v2, v2, AF.Sqrt))
            sc.wait_ge(self.sem["tM"], tM_bc)
            sc.wait_ge(self.sem["vC"], S["vC"] + DC * nV1 + nR)
            self.A(sc, lambda: sc.copy(self.bcast[:, 0:TS], self.bank(2)[:, :TS]))
            self.A(sc, lambda: sc.copy(self.bcast[:, TS:2 * TS], self.bank(3)[:, :TS]))

    def ph_mm(self, wscr, OC, in_src, evac, out_dst, ts, scale_col,
             resid_src=None):
        b, c = self.block, self.c
        DC, TS = c.DC, c.TS
        S = dict(self.ct)
        load_limbs = in_src in ("hLd", "attnL")
        n_in = 2 * DC if load_limbs else 0
        nEv = {"relu": 2, "resid": 1, "qk": 2}[evac]
        nEa = {"relu": 1, "resid": 0, "qk": 1}[evac]
        nStW = {"relu": 2, "resid": 1, "qk": 2}[evac]
        scname, sccol_base = scale_col
        wsem = ("wA", "wB")
        xsem = ("xA", "xB")
        stsem = ("stA", "stB")

        @b.sync
        def _(sync):
            self.wait_stores(sync, S)
            if load_limbs:
                sync.wait_ge(self.sem["tM"], S["tM"])
                srcT = self.hLd if in_src == "hLd" else self.attnL
                for l in range(2):
                    for i in range(DC):
                        ap = (srcT[l, i, :, :] if in_src == "hLd"
                              else srcT[l, i, :, ts * TS:(ts + 1) * TS])
                        dd = self.inl[:, (l * DC + i) * TS:(l * DC + i + 1) * TS]
                        self.dma(sync, "blk", dd, ap)
            for o in range(OC):
                sync.wait_ge(self.sem["tM"],
                             S["tM"] + (o - 1 if o >= 2 else 0))
                self.dma(sync, wsem[o % 2],
                         self.wsb[:, (o % 2) * DC * 128:(o % 2 + 1) * DC * 128],
                         wscr[o, :, :])
                if evac == "resid":
                    sync.wait_ge(self.sem["vC"],
                                 S["vC"] + ((o - 1) * nEv if o >= 2 else 0))
                    self.dma(sync, xsem[o % 2],
                             self.xck[:, (o % 2) * TS:(o % 2 + 1) * TS],
                             resid_src[o, :, ts * TS:(ts + 1) * TS])

        @b.tensor
        def _(t):
            for o in range(OC):
                t.wait_ge(self.sem[wsem[o % 2]], S[wsem[o % 2]] + 16 * (o // 2 + 1))
                if o == 0:
                    if load_limbs:
                        t.wait_ge(self.sem["blk"], S["blk"] + n_in * 16)
                    else:
                        t.wait_ge(self.sem["vC"], S["vC"])
                if o >= 2:
                    if nEa:
                        t.wait_ge(self.sem["aC"], S["aC"] + (o - 1) * nEa)
                    t.wait_ge(self.sem["vC"], S["vC"] + (o - 1) * nEv)
                else:
                    t.wait_ge(self.sem["aC"], S["aC"])
                    t.wait_ge(self.sem["vC"], S["vC"])
                pb = self.bank(o % 2)[:, :TS]
                wbuf = self.wsb[:, (o % 2) * DC * 128:(o % 2 + 1) * DC * 128]
                n = 2 * DC
                k = 0
                ins = None
                for i in range(DC):
                    for l in range(2):
                        lim = self.inl[:, (l * DC + i) * TS:(l * DC + i + 1) * TS]
                        ins = t.matmul(pb, wbuf[:, i * 128:(i + 1) * 128], lim,
                                       start=(k == 0), stop=(k == n - 1))
                        k += 1
                self.tinc(ins)

        @b.scalar
        def _(sc):
            if nEa == 0:
                return
            for o in range(OC):
                sc.wait_ge(self.sem["tM"], S["tM"] + o + 1)
                if o >= 2:
                    sc.wait_ge(self.sem["vC"], S["vC"] + (o - 1) * nEv)
                else:
                    sc.wait_ge(self.sem["vC"], S["vC"])
                    self.wait_stores(sc, S)
                pb = self.bank(o % 2)[:, :TS]
                tm = self.tmp32[:, (o % 2) * TS:(o % 2 + 1) * TS]
                if evac == "relu":
                    scv = self.scal(scname)
                    self.A(sc, lambda tm=tm, pb=pb, scv=scv:
                           sc.activation(tm, pb, AF.Relu, scale=scv))
                else:
                    scv = self.sct[:, sccol_base + o:sccol_base + o + 1]
                    self.A(sc, lambda tm=tm, pb=pb, scv=scv: sc.mul(tm, pb, scv))

        @b.gpsimd
        def _(gp):
            for o in range(OC):
                gp.wait_ge(self.sem["vC"], S["vC"] + (o + 1) * nEv)
                tm = self.tmp32[:, (o % 2) * TS:(o % 2 + 1) * TS]
                lr = self.ltmp[:, (o % 2) * 2 * TS:(o % 2) * 2 * TS + TS]
                le = self.ltmp[:, (o % 2) * 2 * TS + TS:(o % 2 + 1) * 2 * TS]
                if evac == "resid":
                    self.dma(gp, stsem[o % 2],
                             out_dst[o, :, ts * TS:(ts + 1) * TS], tm)
                elif evac == "relu":
                    self.dma(gp, stsem[o % 2], out_dst[0, o, :, :], lr)
                    self.dma(gp, stsem[o % 2], out_dst[1, o, :, :], le)
                else:
                    tgt, rb = out_dst
                    for l, til in ((0, lr), (1, le)):
                        if rb is None:
                            dd = tgt[l, o, :, ts * TS:(ts + 1) * TS]
                        else:
                            r0 = rb + l * (OC * 128) + o * 128
                            dd = tgt[r0:r0 + 128, ts * TS:(ts + 1) * TS]
                        self.dma(gp, stsem[o % 2], dd, til)

        @b.vector
        def _(vec):
            for o in range(OC):
                if nEa:
                    vec.wait_ge(self.sem["aC"], S["aC"] + o + 1)
                else:
                    vec.wait_ge(self.sem["tM"], S["tM"] + o + 1)
                if o >= 2:
                    vec.wait_ge(self.sem[stsem[o % 2]],
                                S[stsem[o % 2]] + 16 * nStW * (o // 2))
                else:
                    self.wait_stores(vec, S)
                tm = self.tmp32[:, (o % 2) * TS:(o % 2 + 1) * TS]
                lr = self.ltmp[:, (o % 2) * 2 * TS:(o % 2) * 2 * TS + TS]
                le = self.ltmp[:, (o % 2) * 2 * TS + TS:(o % 2 + 1) * 2 * TS]
                if evac == "resid":
                    pb = self.bank(o % 2)[:, :TS]
                    xc = self.xck[:, (o % 2) * TS:(o % 2 + 1) * TS]
                    vec.wait_ge(self.sem[xsem[o % 2]],
                                S[xsem[o % 2]] + 16 * (o // 2 + 1))
                    scv = (self.scal(scname) if sccol_base is None
                           else self.sct[:, sccol_base + o:sccol_base + o + 1])
                    self.V(vec, lambda tm=tm, pb=pb, xc=xc, scv=scv:
                           vec.scalar_tensor_tensor(tm, pb, scv, xc,
                                                    ALU.mult, ALU.add))
                else:
                    self.V(vec, lambda lr=lr, tm=tm: vec.tensor_copy(lr, tm))
                    self.V(vec, lambda le=le, tm=tm, lr=lr: vec.tensor_sub(le, tm, lr))

    def ph_qkv(self, src):
        c = self.c
        for ts in range(c.NTS):
            self.ph_ln(src, ts)
            self.ph_mm(self.wqs, c.DC, in_src="sbuf", evac="qk",
                       out_dst=(self.qLd, None), ts=ts, scale_col=("", 0))
            self.ph_mm(self.wks, c.KVC, in_src="sbuf", evac="qk",
                       out_dst=(self.ccink, 0), ts=ts, scale_col=("", 2 * c.DC))
            if ts == c.NTS - 1:
                self.ph_exchange_one(self.ccink, self.ccoutk)
            self.ph_vproj(ts)
        self.ph_exchange_one(self.ccinv, self.ccoutv)

    def ph_exchange_one(self, cin, cout):
        b = self.block
        S = dict(self.ct)

        @b.gpsimd
        def _(gp):
            self.wait_stores(gp, S)
            gp.collective_compute(
                "AllGather", ALU.bypass,
                replica_groups=[[0, 1], [2, 3], [4, 5], [6, 7]],
                ins=[cin.opt()], outs=[cout.opt()],
            ).then_inc(self.sem["CC"], 1)
            self.ct["CC"] += 1

    def ph_vproj(self, ts):
        b, c = self.block, self.c
        DC, TS, KVD = c.DC, c.TS, c.KVD
        TC = c.TC
        VT = max(KVD // 512, 1)
        NTILE = min(512, KVD)
        assert TC <= 4
        wsem = ("wA", "wB")
        stsem = ("stA", "stB")
        nkv = NTILE // 128
        nV = 3

        for vh in range(VT):
            S = dict(self.ct)

            @b.sync
            def _(sync):
                self.wait_stores(sync, S)
                for i in range(DC):
                    sync.wait_ge(self.sem["tM"],
                                 S["tM"] + (1 + i - 1 if i >= 2 else 0))
                    self.dma(sync, wsem[i % 2],
                             self.wsb[:, (i % 2) * DC * 128:
                                      (i % 2) * DC * 128 + NTILE],
                             self.wvs[i, :, vh * 512:vh * 512 + NTILE])

            @b.tensor
            def _(t):
                t.wait_ge(self.sem["vC"], S["vC"])
                t.wait_ge(self.sem["aC"], S["aC"])
                self.tinc(t.matmul(self.bank(5)[:, :NTILE], self.onesf[:],
                                   self.rows[:, 2 * c.T + vh * 512:
                                             2 * c.T + vh * 512 + NTILE],
                                   start=True, stop=True))
                for i in range(DC):
                    t.wait_ge(self.sem[wsem[i % 2]],
                              S[wsem[i % 2]] + 16 * (i // 2 + 1))
                    if i == 0:
                        t.wait_ge(self.sem["aC"], S["aC"] + 1)
                    wbuf = self.wsb[:, (i % 2) * DC * 128:
                                    (i % 2) * DC * 128 + NTILE]
                    ins = None
                    for tc in range(TC):
                        for l in range(2):
                            lim = self.inl[:, (l * DC + i) * TS + tc * 128:
                                           (l * DC + i) * TS + (tc + 1) * 128]
                            ins = t.matmul(self.bank(tc)[:, :NTILE], lim, wbuf,
                                           start=(i == 0 and l == 0),
                                           stop=(i == DC - 1 and l == 1))
                    self.tinc(ins)
            tM_acc = self.ct["tM"]

            @b.scalar
            def _(sc):
                sc.wait_ge(self.sem["tM"], S["tM"] + 1)
                sc.wait_ge(self.sem["vC"], S["vC"])
                vb = self.bcast[:, 0:NTILE]
                self.A(sc, lambda vb=vb: sc.copy(vb, self.bank(5)[:, :NTILE]))

            @b.vector
            def _(vec):
                vec.wait_ge(self.sem["tM"], tM_acc)
                for tc in range(TC):
                    bidx = tc
                    pbk = self.bank(tc)[:, :NTILE]
                    tm = self.tmp32[:, (bidx % 2) * TS:(bidx % 2) * TS + NTILE]
                    lr = self.ltmp[:, (bidx % 2) * 2 * TS:
                                   (bidx % 2) * 2 * TS + NTILE]
                    le = self.ltmp[:, ((bidx % 2) * 2 + 1) * TS:
                                   ((bidx % 2) * 2 + 1) * TS + NTILE]
                    vb = self.bcast[:, 0:NTILE]
                    if bidx >= 2:
                        vec.wait_ge(self.sem[stsem[bidx % 2]],
                                    S[stsem[bidx % 2]] +
                                    16 * 2 * nkv * (bidx // 2))
                    else:
                        self.wait_stores(vec, S)
                    self.V(vec, lambda tm=tm, pbk=pbk, vb=vb:
                           vec.tensor_mul(tm, pbk, vb))
                    self.V(vec, lambda lr=lr, tm=tm: vec.tensor_copy(lr, tm))
                    self.V(vec, lambda le=le, tm=tm, lr=lr:
                           vec.tensor_sub(le, tm, lr))

            @b.gpsimd
            def _(gp):
                for tc in range(TC):
                    bidx = tc
                    gp.wait_ge(self.sem["vC"], S["vC"] + (bidx + 1) * nV)
                    lr = self.ltmp[:, (bidx % 2) * 2 * TS:
                                   (bidx % 2) * 2 * TS + NTILE]
                    le = self.ltmp[:, ((bidx % 2) * 2 + 1) * TS:
                                   ((bidx % 2) * 2 + 1) * TS + NTILE]
                    kcg = (ts * TS) // 128 + tc
                    for j in range(nkv):
                        kvc = vh * 4 + j if NTILE >= 512 else j
                        for l, til in ((0, lr), (1, le)):
                            self.dma(gp, stsem[bidx % 2],
                                     self.ccinv[(kvc * 2 + l) * 128:
                                                (kvc * 2 + l) * 128 + 128,
                                                kcg * 128:(kcg + 1) * 128],
                                     til[:, j * 128:(j + 1) * 128])

    def att_views(self):
        c = self.c
        T, TS, KC = c.T, c.TS, c.KC
        inl = self.inl
        off = 0
        qA = inl[:, off:off + T]; off += T
        qB = inl[:, off:off + T]; off += T
        kA = inl[:, off:off + 2 * T]; off += 2 * T
        kB = inl[:, off:off + 2 * T]; off += 2 * T
        vA = inl[:, off:off + KC * 128]; off += KC * 128
        vB = inl[:, off:off + KC * 128]; off += KC * 128
        pA = inl[:, off:off + KC * TS]; off += KC * TS
        pB = inl[:, off:off + KC * TS]; off += KC * TS
        assert off <= c.INLW, (off, c.INLW)
        p0 = self.ptm32[:, 0:TS]
        p1 = self.ptm32[:, TS:2 * TS]
        return qA, qB, kA, kB, vA, vB, pA, pB, (p0, p1)

    def ph_att_head(self, h):
        b, c = self.block, self.c
        T, TS, KC, QC, KT = c.T, c.TS, c.KC, c.QC, c.KT
        S = dict(self.ct)
        qA, qB, kA, kB, vA, vB, pA, pB, ptmp = self.att_views()
        kvh = h // c.NREP
        KR = c.KVC * 128
        kcpr = T // 128

        @b.sync
        def _(sync):
            if h == 0:
                sync.wait_ge(self.sem["CC"], 2)
            sync.wait_ge(self.sem["tM"], S["tM"])
            sync.wait_ge(self.sem["vC"], S["vC"])
            self.wait_stores(sync, S)
            self.dma(sync, "blk", qA, self.qLd[0, h, :, :])
            self.dma(sync, "blk", qB, self.qLd[1, h, :, :])
            for r in range(2):
                self.dma(sync, "blk", kA[:, r * T:(r + 1) * T],
                         self.ccoutk[r * 2 * KR + kvh * 128:
                                     r * 2 * KR + (kvh + 1) * 128, :])
                self.dma(sync, "blk", kB[:, r * T:(r + 1) * T],
                         self.ccoutk[r * 2 * KR + KR + kvh * 128:
                                     r * 2 * KR + KR + (kvh + 1) * 128, :])
                for l, vt in ((0, vA), (1, vB)):
                    vr0 = r * (c.KVC * 2 * 128) + (kvh * 2 + l) * 128
                    self.dma(sync, "blk",
                             vt[:, r * kcpr * 128:(r + 1) * kcpr * 128],
                             self.ccoutv[vr0:vr0 + 128, :])
        ld_head = self.ct["blk"]

        nPQ = 2 * KT

        @b.tensor
        def _(t):
            t.wait_ge(self.sem["blk"], ld_head)
            t.wait_ge(self.sem["aC"], S["aC"])
            for qc in range(QC):
                for kt in range(KT):
                    j = qc * KT + kt
                    if j >= 2:
                        qc2, kt2 = divmod(j - 2, KT)
                        t.wait_ge(self.sem["vC"], S["vC"] + qc2 * nPQ + kt2 + 1)
                    self.tinc(t.matmul(self.bank(j % 2)[:, :TS],
                                       qA[:, qc * 128:(qc + 1) * 128],
                                       kA[:, kt * TS:(kt + 1) * TS],
                                       start=True, stop=True))

        @b.vector
        def _(vec):
            vec.wait_ge(self.sem["aC"], S["aC"])
            for qc in range(QC):
                for kt in range(KT):
                    vec.wait_ge(self.sem["tM"], S["tM"] + qc * KT + kt + 1)
                    bidx = (qc * KT + kt) % 2
                    mx = self.bcast[:, kt:kt + 1]
                    self.V(vec, lambda mx=mx, bidx=bidx: vec.reduce_max(
                        mx, self.bank(bidx)[:, :TS], axis=AX))
                mcol = self.bcast[:, 8 + qc:8 + qc + 1]
                self.V(vec, lambda mcol=mcol:
                       vec.tensor_copy(mcol, self.bcast[:, 0:1]))
                for kt in range(1, KT):
                    self.V(vec, lambda mcol=mcol, kt=kt: vec.tensor_max(
                        mcol, mcol, self.bcast[:, kt:kt + 1]))
        vC_stats = self.ct["vC"]

        @b.gpsimd
        def _(gp):
            gp.wait_ge(self.sem["vC"], vC_stats)
            for qc in range(QC):
                self.dma(gp, "stO", self.mrow_d[h, qc].unsqueeze(1),
                         self.bcast[:, 8 + qc:8 + qc + 1])
        stO_mst = self.ct["stO"]

        @b.sync
        def _(sync):
            sync.wait_ge(self.sem["stO"], stO_mst)
            self.dma(sync, "blk", self.rows[:, 0:T],
                     self.mrow_d[h].rearrange("a b -> (a b)").unsqueeze(0))
        mrow_ld = self.ct["blk"]

        for qt in range(c.NTS):
            self.att_qt(h, qt, qA, qB, kA, kB, vA, vB, pA, pB, ptmp,
                        mrow_ld, stO_mst)

    def att_qt(self, h, qt, qA, qB, kA, kB, vA, vB, pA, pB, ptmp,
               mrow_ld, stO_mst):
        b, c = self.block, self.c
        T, TS, KC = c.T, c.TS, c.KC
        S = dict(self.ct)
        nEvK = 3

        @b.tensor
        def _(t):
            t.wait_ge(self.sem["blk"], mrow_ld)
            t.wait_ge(self.sem["vC"], S["vC"])
            self.tinc(t.matmul(self.bank(4)[:, :TS], self.onesf[:],
                               self.rows[:, qt * TS:(qt + 1) * TS],
                               start=True, stop=True))
            t.wait_ge(self.sem["aC"], S["aC"] + 1)
            for kc in range(KC):
                if kc >= 2:
                    t.wait_ge(self.sem["vC"], S["vC"] + (kc - 1) * nEvK)
                pbk = self.bank(kc % 2)[:, :TS]
                q0 = qA[:, qt * TS:(qt + 1) * TS]
                q1 = qB[:, qt * TS:(qt + 1) * TS]
                kk0 = kA[:, kc * 128:(kc + 1) * 128]
                kk1 = kB[:, kc * 128:(kc + 1) * 128]
                t.matmul(pbk, kk0, q0, start=True, stop=False)
                t.matmul(pbk, kk1, q0, start=False, stop=False)
                self.tinc(t.matmul(pbk, kk0, q1, start=False, stop=True))
            t.wait_ge(self.sem["vC"], S["vC"] + KC * nEvK)
            psz = self.bank(5)[0:1, :TS]
            n = 2 * KC
            k = 0
            ins = None
            for kc in range(KC):
                for pl in (pA, pB):
                    ins = t.matmul(psz, self.onescr[:],
                                   pl[:, kc * TS:(kc + 1) * TS],
                                   start=(k == 0), stop=(k == n - 1))
                    k += 1
            self.tinc(ins)
            t.wait_ge(self.sem["vC"], S["vC"] + KC * nEvK + 1)
            self.tinc(t.matmul(self.bank(6)[:, :TS], self.onesf[:],
                               self.rows[:, T:T + TS], start=True, stop=True))
            psav = self.bank(3)[:, :TS]
            n = 3 * KC
            k = 0
            for kc in range(KC):
                for (vt, pl) in ((vA, pA), (vA, pB), (vB, pA)):
                    ins = t.matmul(psav, vt[:, kc * 128:(kc + 1) * 128],
                                   pl[:, kc * TS:(kc + 1) * TS],
                                   start=(k == 0), stop=(k == n - 1))
                    k += 1
            self.tinc(ins)

        @b.scalar
        def _(sc):
            sc.wait_ge(self.sem["tM"], S["tM"] + 1)
            sc.wait_ge(self.sem["stO"], stO_mst)
            sc.wait_ge(self.sem["vC"], S["vC"])
            self.A(sc, lambda: sc.copy(self.bcast[:, 0:TS], self.bank(4)[:, :TS]))
            for kc in range(KC):
                sc.wait_ge(self.sem["vC"], S["vC"] + kc * nEvK + 1)
                tm = self.tmp32[:, (kc % 2) * TS:(kc % 2 + 1) * TS]
                pt = ptmp[kc % 2]
                self.A(sc, lambda pt=pt, tm=tm: sc.activation(pt, tm, AF.Exp))
            sc.wait_ge(self.sem["tM"], S["tM"] + 1 + KC + 2)
            self.A(sc, lambda: sc.copy(self.bcast[:, TS:2 * TS],
                                       self.bank(6)[:, :TS]))

        @b.vector
        def _(vec):
            for kc in range(KC):
                vec.wait_ge(self.sem["tM"], S["tM"] + 1 + kc + 1)
                if kc >= 2:
                    vec.wait_ge(self.sem["aC"], S["aC"] + 1 + kc - 1)
                tm = self.tmp32[:, (kc % 2) * TS:(kc % 2 + 1) * TS]
                pt = ptmp[kc % 2]
                pbk = self.bank(kc % 2)[:, :TS]
                self.V(vec, lambda tm=tm, pbk=pbk: vec.tensor_sub(
                    tm, pbk, self.bcast[:, 0:TS]))
                vec.wait_ge(self.sem["aC"], S["aC"] + 1 + kc + 1)
                pr = pA[:, kc * TS:(kc + 1) * TS]
                pe = pB[:, kc * TS:(kc + 1) * TS]
                self.V(vec, lambda pr=pr, pt=pt: vec.tensor_copy(pr, pt))
                self.V(vec, lambda pe=pe, pt=pt, pr=pr: vec.tensor_sub(pe, pt, pr))
            vec.wait_ge(self.sem["tM"], S["tM"] + 1 + KC + 1)
            rz = self.rows[:, T:T + TS]
            self.V(vec, lambda rz=rz: vec.reciprocal(rz, self.bank(5)[0:1, :TS]))
            vec.wait_ge(self.sem["tM"], S["tM"] + 1 + KC + 3)
            vec.wait_ge(self.sem["aC"], S["aC"] + 1 + KC + 1)
            self.wait_stores(vec, S)
            tm = self.tmp32[:, 0:TS]
            lr = self.ltmp[:, 0:TS]
            le = self.ltmp[:, TS:2 * TS]
            self.V(vec, lambda tm=tm: vec.tensor_mul(
                tm, self.bank(3)[:, :TS], self.bcast[:, TS:2 * TS]))
            self.V(vec, lambda lr=lr, tm=tm: vec.tensor_copy(lr, tm))
            self.V(vec, lambda le=le, tm=tm, lr=lr: vec.tensor_sub(le, tm, lr))

        @b.gpsimd
        def _(gp):
            gp.wait_ge(self.sem["vC"], self.ct["vC"])
            lr = self.ltmp[:, 0:TS]
            le = self.ltmp[:, TS:2 * TS]
            self.dma(gp, "stO", self.attnL[0, h, :, qt * TS:(qt + 1) * TS], lr)
            self.dma(gp, "stO", self.attnL[1, h, :, qt * TS:(qt + 1) * TS], le)

    def ph_end(self):
        b = self.block
        S = dict(self.ct)

        @b.gpsimd
        def _(gp):
            for s in ALL_SEMS:
                if S[s] > 0:
                    gp.wait_ge(self.sem[s], S[s])


def make_inputs_per_core(cfg, core, full):
    c = cfg
    seq = core // 2
    half = core % 2
    x = np.asarray(full["L0_attn_hidden_states"])
    xs = np.ascontiguousarray(
        x[seq, half * c.T:(half + 1) * c.T, :].T).reshape(c.DC, 128, c.T)

    def u8(a):
        a = np.asarray(a)
        return a.view(np.uint8) if a.dtype == np.bool_ else a.astype(np.uint8)

    cei = np.ones(c.D, dtype=np.float32)
    cei[c.NE:] = np.float32(c.EI_NEG)
    return {
        "x0": xs.astype(np.float32),
        "wq_n": np.asarray(full["L0_attn_q_weight"], dtype=np.float32),
        "wk_n": np.asarray(full["L0_attn_k_weight"], dtype=np.float32),
        "wv_n": np.asarray(full["L0_attn_v_weight"], dtype=np.float32),
        "wo_n": np.asarray(full["L0_attn_o_weight"], dtype=np.float32),
        "qsc_in": np.asarray(full["L0_attn_q_scale"], dtype=np.float32),
        "ksc_in": np.asarray(full["L0_attn_k_scale"], dtype=np.float32),
        "vsc_in": np.asarray(full["L0_attn_v_scale"], dtype=np.float32),
        "osc_in": np.asarray(full["L0_attn_o_scale"], dtype=np.float32),
        "wqT": np.ascontiguousarray(np.asarray(full["L0_attn_q_weight"], dtype=np.float32).T),
        "wkT": np.ascontiguousarray(np.asarray(full["L0_attn_k_weight"], dtype=np.float32).T),
        "wvT": np.ascontiguousarray(np.asarray(full["L0_attn_v_weight"], dtype=np.float32).T),
        "woT": np.ascontiguousarray(np.asarray(full["L0_attn_o_weight"], dtype=np.float32).T),
        "k0u": u8(full["L0_ffn_k0"]),
        "k1u": u8(full["L0_ffn_k1"]),
        "k2u": u8(full["L0_ffn_k2"]),
        "s0": np.asarray(full["L0_ffn_s0"], dtype=np.float32).reshape(1, 1),
        "s1": np.asarray(full["L0_ffn_s1"], dtype=np.float32).reshape(1, 1),
        "s2": np.asarray(full["L0_ffn_s2"], dtype=np.float32).reshape(1, 1),
        "cei": np.ascontiguousarray(cei.reshape(c.DC, 128).T),
        "cei2": np.ascontiguousarray((2.0 * cei).reshape(c.DC, 128).T),
    }


def build_program(cfg, name="mkmac1", phase_limit=None):
    nc = bass.Bass("TRN2", target_bir_lowering=False, name=name)
    Build(nc, cfg, phase_limit=phase_limit).build()
    return nc


def kernel(**inputs):
    cfg = CFG()
    nc = build_program(cfg, name="mkmacaron_v2")
    in_maps = [make_inputs_per_core(cfg, core, inputs) for core in range(8)]
    res = run_bass_kernel_spmd(nc, in_maps, core_ids=list(range(8)))
    out = np.empty((4, 2048, 4096), dtype=np.float32)
    for core in range(8):
        seq, half = core // 2, core % 2
        y = res.results[core]["yout"].reshape(cfg.D, cfg.T)
        out[seq, half * cfg.T:(half + 1) * cfg.T, :] = y.T
    return out


# version 11
# speedup vs baseline: 1.3569x; 1.0974x over previous
import numpy as np
import concourse.bass as bass
import concourse.mybir as mybir
from concourse.bass_utils import run_bass_kernel_spmd
from contextlib import ExitStack

F32 = mybir.dt.float32
F32R = mybir.dt.float32r
U8 = mybir.dt.uint8
AF = mybir.ActivationFunctionType
ALU = mybir.AluOpType
AX = mybir.AxisListType.X

LOAD_SEMS = ["blk", "wA", "wB", "xA", "xB", "kA", "kB"]
STORE_SEMS = ["stA", "stB", "stC", "stD", "stO"]
ALL_SEMS = LOAD_SEMS + STORE_SEMS + ["tM", "vC", "aC", "gC", "CC"]


class CFG:
    def __init__(self, D=4096, T=1024, HD=128, KVH=8, P_EI=0.8):
        self.D = D
        self.T = T
        self.HD = HD
        self.DC = D // 128
        self.HEADS = D // HD
        self.KVH = KVH
        self.KVD = KVH * HD
        self.KVC = self.KVD // 128
        self.NREP = self.HEADS // self.KVH
        self.TS = T // 2
        self.NTS = 2
        self.TC = self.TS // 128
        self.SEQK = 2 * T
        self.KC = self.SEQK // 128
        self.QC = T // 128
        self.KT = self.SEQK // self.TS
        self.NE = round(D * P_EI)
        self.EI_NEG = -P_EI / (1.0 - P_EI)
        att_need = (2 * self.T + 2 * 2 * self.T + 2 * self.KC * 128 +
                    2 * self.KC * self.TS + 2 * self.TS)
        self.INLW = max(2 * self.DC * self.TS, att_need, 2 * D)
        assert self.TS <= 512 and HD == 128
        assert self.DC % 4 == 0


class Build:
    def __init__(self, nc, cfg, phase_limit=None):
        self.nc = nc
        self.c = cfg
        self.phase_limit = phase_limit
        self.ct = {s: 0 for s in ALL_SEMS}

    def V(self, vec, make):
        vec.wait_ge(self.sem["vC"], self.ct["vC"])
        make().then_inc(self.sem["vC"], 1)
        self.ct["vC"] += 1

    def A(self, sc, make):
        sc.wait_ge(self.sem["aC"], self.ct["aC"])
        make().then_inc(self.sem["aC"], 1)
        self.ct["aC"] += 1

    def G(self, gp, make):
        gp.wait_ge(self.sem["gC"], self.ct["gC"])
        make().then_inc(self.sem["gC"], 1)
        self.ct["gC"] += 1

    def tinc(self, ins):
        ins.then_inc(self.sem["tM"], 1)
        self.ct["tM"] += 1

    def dma(self, eng, stream, out, in_, **kw):
        eng.dma_start(out=out, in_=in_, **kw).then_inc(self.sem[stream], 16)
        self.ct[stream] += 16

    def wait_stores(self, eng, S):
        for s in STORE_SEMS:
            if S[s] > 0:
                eng.wait_ge(self.sem[s], S[s])

    def build(self):
        nc, c = self.nc, self.c
        D, T, TS, DC = c.D, c.T, c.TS, c.DC
        dt = nc.dram_tensor
        self.x0 = dt("x0", [DC, 128, T], F32, kind="ExternalInput").ap()
        self.wq_n = dt("wq_n", [D, D], F32, kind="ExternalInput").ap()
        self.wk_n = dt("wk_n", [c.KVD, D], F32, kind="ExternalInput").ap()
        self.wv_n = dt("wv_n", [c.KVD, D], F32, kind="ExternalInput").ap()
        self.wo_n = dt("wo_n", [D, D], F32, kind="ExternalInput").ap()
        self.qsc_in = dt("qsc_in", [D], F32, kind="ExternalInput").ap()
        self.ksc_in = dt("ksc_in", [c.KVD], F32, kind="ExternalInput").ap()
        self.vsc_in = dt("vsc_in", [c.KVD], F32, kind="ExternalInput").ap()
        self.osc_in = dt("osc_in", [D], F32, kind="ExternalInput").ap()
        self.k0u = dt("k0u", [D, D], U8, kind="ExternalInput").ap()
        self.k1u = dt("k1u", [D, D], U8, kind="ExternalInput").ap()
        self.k2u = dt("k2u", [D, D], U8, kind="ExternalInput").ap()
        self.s0 = dt("s0", [1, 1], F32, kind="ExternalInput").ap()
        self.s1 = dt("s1", [1, 1], F32, kind="ExternalInput").ap()
        self.s2 = dt("s2", [1, 1], F32, kind="ExternalInput").ap()
        self.wqT = dt("wqT", [D, D], F32, kind="ExternalInput").ap()
        self.wkT = dt("wkT", [D, c.KVD], F32, kind="ExternalInput").ap()
        self.wvT = dt("wvT", [D, c.KVD], F32, kind="ExternalInput").ap()
        self.woT = dt("woT", [D, D], F32, kind="ExternalInput").ap()
        self.cei = dt("cei", [128, DC], F32, kind="ExternalInput").ap()
        self.cei2 = dt("cei2", [128, DC], F32, kind="ExternalInput").ap()
        self.yout = dt("yout", [DC, 128, T], F32, kind="ExternalOutput").ap()

        self.xcur = dt("xcur", [DC, 128, T], F32).ap()
        self.w0s = dt("w0s", [DC, 128, DC * 128], F32R).ap()
        self.w1s = dt("w1s", [DC, 128, DC * 128], F32R).ap()
        self.w2s = dt("w2s", [DC, 128, DC * 128], F32R).ap()
        self.wqs = dt("wqs", [DC, 128, DC * 128], F32R).ap()
        self.wks = dt("wks", [c.KVC, 128, DC * 128], F32R).ap()
        self.wos = dt("wos", [DC, 128, DC * 128], F32R).ap()
        self.wvs = dt("wvs", [DC, 128, c.KVD], F32R).ap()
        self.qLd = dt("qLd", [2, DC, 128, T], F32R).ap()
        self.hLd = dt("hLd", [2, DC, 128, TS], F32R).ap()
        self.ccink = dt("ccink", [2 * c.KVC * 128, T], F32R).ap()
        self.ccoutk = dt("ccoutk", [2 * 2 * c.KVC * 128, T], F32R).ap()
        self.ccinv = dt("ccinv", [c.KVC * 2 * 128, T], F32R).ap()
        self.ccoutv = dt("ccoutv", [2 * c.KVC * 2 * 128, T], F32R).ap()
        self.attnL = dt("attnL", [2, DC, 128, T], F32R).ap()
        self.mrow_d = dt("mrow_d", [c.HEADS, c.QC, 128], F32).ap()
        self.vrow_d = dt("vrow_d", [c.KVC, 128], F32).ap()

        es = ExitStack()
        sb, ps = nc.sbuf_tensor, nc.psum_tensor
        E = es.enter_context
        self.inl = E(sb("inl", [128, c.INLW], F32R))
        self.wsb = E(sb("wsb", [128, 2 * DC * 128], F32R))
        self.ksb = E(sb("ksb", [128, 2 * D], U8))
        self.tmp32 = E(sb("tmp32", [128, 2 * TS], F32))
        self.ltmp = E(sb("ltmp", [128, 4 * TS], F32R))
        self.xck = E(sb("xck", [128, 2 * TS], F32))
        self.bcast = E(sb("bcast", [128, 2 * TS], F32))
        self.rows = E(sb("rows", [1, 3 * T], F32))
        self.sct = E(sb("sct", [128, 2 * DC + c.KVC + 8], F32))
        self.ceis = E(sb("ceis", [128, 2 * DC], F32))
        self.onesf = E(sb("onesf", [1, 128], F32))
        self.onescf = E(sb("onescf", [128, 1], F32))
        self.onescr = E(sb("onescr", [128, 1], F32R))
        self.negrow = E(sb("negrow", [1, 128], F32R))
        self.rowsR = E(sb("rowsR", [1, c.T], F32R))
        self.psb = E(ps("psb", [128, 4096], F32))

        self.sem = {}
        for s in ALL_SEMS:
            self.sem[s] = E(nc.semaphore(s))
        self.block = E(nc.Block())
        self.emit_all(phase_limit=self.phase_limit)
        es.close()
        return nc

    def bank(self, i):
        return self.psb[:, i * 512:(i + 1) * 512]

    def scal(self, name):
        c = self.c
        base = 2 * c.DC + c.KVC
        idx = {"s0": base, "s1": base + 1, "s2": base + 2,
               "rs0": base + 3, "rs1": base + 4, "hs2": base + 5}[name]
        return self.sct[:, idx:idx + 1]

    def emit_all(self, phase_limit=None):
        c = self.c
        phases = [self.ph_consts,
                  lambda: self.ph_conv_ffn(self.k0u, self.w0s, use_c=False),
                  lambda: self.ph_conv_ffn(self.k1u, self.w1s, use_c=True),
                  lambda: self.ph_conv_ffn(self.k2u, self.w2s, use_c=True),
                  self.ph_conv_att_q,
                  self.ph_conv_att_k,
                  self.ph_conv_att_v,
                  self.ph_conv_att_o,
                  self.ph_conv_att_tail,
                  lambda: self.ph_ffn(src=self.x0, dst=self.xcur),
                  lambda: self.ph_qkv(src=self.xcur)]
        phases += [(lambda h=h: self.ph_att_head(h)) for h in range(c.HEADS)]
        phases += [(lambda ts=ts: self.ph_mm(
            self.wos, c.DC, in_src="attnL", evac="resid",
            resid_src=self.xcur, out_dst=self.xcur, ts=ts,
            scale_col=(None, c.DC))) for ts in range(c.NTS)]
        phases += [lambda: self.ph_ffn(src=self.xcur, dst=self.yout)]
        if phase_limit is None:
            phase_limit = len(phases)
        for ph in phases[:phase_limit]:
            ph()
        self.ph_end()

    def ph_consts(self):
        b, c = self.block, self.c

        @b.sync
        def _(sync):
            self.dma(sync, "blk", self.ceis[:, 0:c.DC], self.cei)
            self.dma(sync, "blk", self.ceis[:, c.DC:2 * c.DC], self.cei2)
            base = 2 * c.DC + c.KVC
            for j, s in enumerate([self.s0, self.s1, self.s2]):
                self.dma(sync, "blk", self.sct[:, base + j:base + j + 1],
                         bass.AP(s.tensor, 0, [[0, 128], [1, 1]]),
                         allow_slow_non_contiguous=True)
        blk_done = self.ct["blk"]

        @b.vector
        def _(vec):
            vec.wait_ge(self.sem["blk"], blk_done)
            self.V(vec, lambda: vec.memset(self.onesf[:], 1.0))
            self.V(vec, lambda: vec.memset(self.onescf[:], 1.0))
            self.V(vec, lambda: vec.tensor_copy(self.onescr[:], self.onescf[:]))
            self.V(vec, lambda: vec.tensor_scalar_mul(self.negrow[:], self.onesf[:], -1.0))
            base = 2 * c.DC + c.KVC
            self.V(vec, lambda: vec.tensor_scalar_mul(
                self.sct[:, base + 3:base + 4], self.sct[:, base:base + 1], 2.0))
            self.V(vec, lambda: vec.tensor_scalar_mul(
                self.sct[:, base + 4:base + 5], self.sct[:, base + 1:base + 2], 2.0))
            self.V(vec, lambda: vec.tensor_scalar_mul(
                self.sct[:, base + 5:base + 6], self.sct[:, base + 2:base + 3], 0.5))

    def ph_conv_ffn(self, ksrc, wdst, use_c):
        b, c = self.block, self.c
        DC = c.DC
        S = dict(self.ct)
        nEv = 1
        GRP = 4
        nStW = DC // GRP
        ksem = ("kA", "kB")
        stsem = ("stA", "stB")

        @b.sync
        def _(sync):
            for i in range(DC):
                sync.wait_ge(self.sem["vC"],
                             S["vC"] + ((i - 1) * nEv if i >= 2 else 0))
                self.dma(sync, ksem[i % 2],
                         self.ksb[:, (i % 2) * c.D:(i % 2 + 1) * c.D],
                         ksrc[i * 128:(i + 1) * 128, :])

        @b.vector
        def _(vec):
            for i in range(DC):
                vec.wait_ge(self.sem[ksem[i % 2]],
                            S[ksem[i % 2]] + 16 * (i // 2 + 1))
                if i >= 2:
                    vec.wait_ge(self.sem[stsem[i % 2]],
                                S[stsem[i % 2]] + 16 * nStW * (i // 2))
                else:
                    self.wait_stores(vec, S)
                    vec.wait_ge(self.sem["tM"], S["tM"])
                    vec.wait_ge(self.sem["aC"], S["aC"])
                sgn = self.inl[:, (i % 2) * c.D:(i % 2 + 1) * c.D]
                ksl = self.ksb[:, (i % 2) * c.D:(i % 2 + 1) * c.D]
                if use_c:
                    cc = self.ceis[:, i:i + 1]
                    c2 = self.ceis[:, DC + i:DC + i + 1]
                    self.V(vec, lambda sgn=sgn, ksl=ksl, cc=cc, c2=c2:
                           vec.tensor_scalar(sgn, ksl, c2, cc, ALU.mult, ALU.subtract))
                else:
                    self.V(vec, lambda sgn=sgn, ksl=ksl:
                           vec.tensor_scalar(sgn, ksl, 2.0, 1.0, ALU.mult, ALU.subtract))

        @b.gpsimd
        def _(gp):
            for i in range(DC):
                gp.wait_ge(self.sem["vC"], S["vC"] + (i + 1) * nEv)
                sgn = self.inl[:, (i % 2) * c.D:(i % 2 + 1) * c.D]
                for g in range(DC // GRP):
                    oc0 = g * GRP
                    self.dma(gp, stsem[i % 2],
                             wdst[oc0:oc0 + GRP, :, i * 128:(i + 1) * 128]
                             .transpose([1, 0, 2]),
                             sgn[:, oc0 * 128:(oc0 + GRP) * 128]
                             .rearrange("p (g o) -> p g o", o=128))

    def ph_conv_att_q(self):
        c = self.c
        rsq = float(np.float32(1.0) / np.float32(np.sqrt(np.float64(c.HD))))
        self.ph_scales(self.wq_n, self.qsc_in, c.DC, 0, rsq, vrow=False)
        self.ph_sign(self.wqT, self.wqs, c.DC, whole=False)

    def ph_conv_att_k(self):
        c = self.c
        self.ph_scales(self.wk_n, self.ksc_in, c.KVC, 2 * c.DC, 1.0, vrow=False)
        self.ph_sign(self.wkT, self.wks, c.KVC, whole=False)

    def ph_conv_att_v(self):
        c = self.c
        self.ph_scales(self.wv_n, self.vsc_in, c.KVC, -1, 1.0, vrow=True)
        self.ph_sign(self.wvT, self.wvs, c.KVC, whole=True)

    def ph_conv_att_o(self):
        c = self.c
        self.ph_scales(self.wo_n, self.osc_in, c.DC, c.DC, 1.0, vrow=False)
        self.ph_sign(self.woT, self.wos, c.DC, whole=False)

    def ph_conv_att_tail(self):
        b, c = self.block, self.c
        S = dict(self.ct)

        @b.sync
        def _(sync):
            self.wait_stores(sync, S)
            self.dma(sync, "blk", self.rows[:, 2 * c.T:2 * c.T + c.KVD],
                     self.vrow_d.rearrange("a b -> (a b)").unsqueeze(0))

    def ph_scales(self, wn, scin, OC, sccol, xsc, vrow):
        b, c = self.block, self.c
        S = dict(self.ct)
        nEv = 3
        wsem = ("wA", "wB")

        @b.sync
        def _(sync):
            for o in range(OC):
                if o >= 2:
                    sync.wait_ge(self.sem["vC"], S["vC"] + (o - 1) * nEv)
                else:
                    sync.wait_ge(self.sem["vC"], S["vC"])
                    sync.wait_ge(self.sem["tM"], S["tM"])
                    self.wait_stores(sync, S)
                slab = self.inl[:, 2 * c.D + (o % 2) * c.D:
                                2 * c.D + (o % 2 + 1) * c.D]
                self.dma(sync, wsem[o % 2], slab,
                         wn[o * 128:(o + 1) * 128, :].bitcast(F32R))
                self.dma(sync, wsem[o % 2], self.bcast[:, (o % 2):(o % 2) + 1],
                         bass.AP(scin.tensor, o * 128, [[1, 128], [1, 1]]),
                         allow_slow_non_contiguous=True)

        @b.vector
        def _(vec):
            for o in range(OC):
                vec.wait_ge(self.sem[wsem[o % 2]],
                            S[wsem[o % 2]] + 32 * (o // 2 + 1))
                wslab = self.inl[:, 2 * c.D + (o % 2) * c.D:
                                 2 * c.D + (o % 2 + 1) * c.D].bitcast(F32)
                srow = self.bcast[:, 2 + (o % 2):3 + (o % 2)]
                self.V(vec, lambda srow=srow, w=wslab: vec.reduce_sum(
                    srow, w, axis=AX, apply_absolute_value=True))
                self.V(vec, lambda srow=srow: vec.tensor_scalar(
                    srow, srow, 1.0 / c.D, 1e-8, ALU.mult, ALU.max))
                scch = self.bcast[:, (o % 2):(o % 2) + 1]
                if vrow:
                    out = self.bcast[:, 4 + (o % 2):5 + (o % 2)]
                    vsem = ("stC", "stD")[o % 2]
                    vec.wait_ge(self.sem[vsem], S[vsem] + 16 * (o // 2))
                else:
                    out = self.sct[:, sccol + o:sccol + o + 1]
                self.V(vec, lambda out=out, srow=srow, scch=scch:
                       vec.scalar_tensor_tensor(out, srow, xsc, scch,
                                                ALU.mult, ALU.mult))

        if vrow:
            @b.gpsimd
            def _(gp):
                for o in range(OC):
                    gp.wait_ge(self.sem["vC"], S["vC"] + (o + 1) * nEv)
                    self.dma(gp, ("stC", "stD")[o % 2],
                             self.vrow_d[o].unsqueeze(1),
                             self.bcast[:, 4 + (o % 2):5 + (o % 2)])

    def ph_sign(self, srcT, wdst, OC, whole):
        b, c = self.block, self.c
        DC = c.DC
        S = dict(self.ct)
        W = OC * 128
        nEa = 1
        GRP = 4 if OC % 4 == 0 else 1
        nStW = 1 if whole else OC // GRP
        wsem = ("wA", "wB")
        stsem = ("stA", "stB")

        @b.sync
        def _(sync):
            for i in range(DC):
                if i >= 2:
                    sync.wait_ge(self.sem["aC"], S["aC"] + (i - 1) * nEa)
                else:
                    sync.wait_ge(self.sem["aC"], S["aC"])
                    sync.wait_ge(self.sem["vC"], S["vC"])
                    sync.wait_ge(self.sem["tM"], S["tM"])
                    self.wait_stores(sync, S)
                slab = self.inl[:, 2 * c.D + (i % 2) * c.D:
                                2 * c.D + (i % 2) * c.D + W]
                self.dma(sync, wsem[i % 2], slab,
                         srcT[i * 128:(i + 1) * 128, :].bitcast(F32R))

        @b.scalar
        def _(sc):
            for i in range(DC):
                sc.wait_ge(self.sem[wsem[i % 2]],
                           S[wsem[i % 2]] + 16 * (i // 2 + 1))
                if i >= 2:
                    sc.wait_ge(self.sem[stsem[i % 2]],
                               S[stsem[i % 2]] + 16 * nStW * (i // 2))
                else:
                    self.wait_stores(sc, S)
                wslab = self.inl[:, 2 * c.D + (i % 2) * c.D:
                                 2 * c.D + (i % 2) * c.D + W].bitcast(F32)
                sgn = self.inl[:, (i % 2) * c.D:(i % 2) * c.D + W]
                self.A(sc, lambda sgn=sgn, w=wslab: sc.activation(sgn, w, AF.Sign))

        @b.gpsimd
        def _(gp):
            for i in range(DC):
                gp.wait_ge(self.sem["aC"], S["aC"] + (i + 1) * nEa)
                sgn = self.inl[:, (i % 2) * c.D:(i % 2) * c.D + W]
                if whole:
                    self.dma(gp, stsem[i % 2], wdst[i, :, :], sgn)
                else:
                    for g in range(OC // GRP):
                        oc0 = g * GRP
                        if GRP == 1:
                            self.dma(gp, stsem[i % 2],
                                     wdst[oc0, :, i * 128:(i + 1) * 128],
                                     sgn[:, oc0 * 128:(oc0 + 1) * 128])
                        else:
                            self.dma(gp, stsem[i % 2],
                                     wdst[oc0:oc0 + GRP, :, i * 128:(i + 1) * 128]
                                     .transpose([1, 0, 2]),
                                     sgn[:, oc0 * 128:(oc0 + GRP) * 128]
                                     .rearrange("p (g o) -> p g o", o=128))

    def ph_ffn(self, src, dst):
        c = self.c
        for ts in range(c.NTS):
            self.ph_ln(src, ts)
            self.ph_mm(self.w0s, c.DC, in_src="sbuf", evac="relu",
                       out_dst=self.hLd, ts=ts, scale_col=("rs0", None))
            self.ph_mm(self.w1s, c.DC, in_src="hLd", evac="relu",
                       out_dst=self.hLd, ts=ts, scale_col=("rs1", None))
            self.ph_mm(self.w2s, c.DC, in_src="hLd", evac="resid",
                       resid_src=src, out_dst=dst, ts=ts, scale_col=("hs2", None))

    def ph_ln(self, src, ts):
        b, c = self.block, self.c
        DC, TS = c.DC, c.TS
        S = dict(self.ct)
        nV1, nV2, nR = 5, 4, 6
        xsem = ("xA", "xB")

        @b.sync
        def _(sync):
            self.wait_stores(sync, S)
            for i in range(DC):
                sync.wait_ge(self.sem["vC"],
                             S["vC"] + ((i - 1) * nV1 if i >= 2 else 0))
                self.dma(sync, xsem[i % 2],
                         self.xck[:, (i % 2) * TS:(i % 2 + 1) * TS],
                         src[i, :, ts * TS:(ts + 1) * TS])
            for i in range(DC):
                sync.wait_ge(self.sem["vC"], S["vC"] + DC * nV1 + nR +
                             max(i - 1, 0) * nV2)
                self.dma(sync, xsem[i % 2],
                         self.xck[:, (i % 2) * TS:(i % 2 + 1) * TS],
                         src[i, :, ts * TS:(ts + 1) * TS])

        @b.tensor
        def _(t):
            t.wait_ge(self.sem["aC"], S["aC"])
            for i in range(DC):
                t.wait_ge(self.sem["vC"], S["vC"] + i * nV1 + 2)
                A = self.ltmp[:, (i % 2) * 2 * TS:(i % 2) * 2 * TS + TS]
                B = self.ltmp[:, (i % 2) * 2 * TS + TS:(i % 2 + 1) * 2 * TS]
                psx = self.bank(4)[0:1, :TS]
                psq = self.bank(5)[0:1, :TS]
                t.matmul(psx, self.onescr[:], A, start=(i == 0), stop=False)
                self.tinc(t.matmul(psx, self.onescr[:], B, start=False,
                                   stop=(i == DC - 1)))
                t.wait_ge(self.sem["vC"], S["vC"] + i * nV1 + 5)
                t.matmul(psq, self.onescr[:], A, start=(i == 0), stop=False)
                self.tinc(t.matmul(psq, self.onescr[:], B, start=False,
                                   stop=(i == DC - 1)))
            t.wait_ge(self.sem["vC"], S["vC"] + DC * nV1 + nR)
            self.tinc(t.matmul(self.bank(2)[:, :TS], self.onesf[:],
                               self.rows[:, 0:TS], start=True, stop=True))
            self.tinc(t.matmul(self.bank(3)[:, :TS], self.onesf[:],
                               self.rows[:, TS:2 * TS], start=True, stop=True))
        tM_bc = self.ct["tM"]

        @b.vector
        def _(vec):
            for i in range(DC):
                vec.wait_ge(self.sem[xsem[i % 2]], S[xsem[i % 2]] + 16 * (i // 2 + 1))
                if i >= 2:
                    vec.wait_ge(self.sem["tM"], S["tM"] + (i - 1) * 2)
                else:
                    vec.wait_ge(self.sem["tM"], S["tM"])
                    self.wait_stores(vec, S)
                xc = self.xck[:, (i % 2) * TS:(i % 2 + 1) * TS]
                A = self.ltmp[:, (i % 2) * 2 * TS:(i % 2) * 2 * TS + TS]
                B = self.ltmp[:, (i % 2) * 2 * TS + TS:(i % 2 + 1) * 2 * TS]
                tm = self.tmp32[:, (i % 2) * TS:(i % 2 + 1) * TS]
                self.V(vec, lambda A=A, xc=xc: vec.tensor_copy(A, xc))
                self.V(vec, lambda B=B, xc=xc, A=A: vec.tensor_sub(B, xc, A))
                self.V(vec, lambda tm=tm, xc=xc: vec.tensor_mul(tm, xc, xc))
                vec.wait_ge(self.sem["tM"], S["tM"] + i * 2 + 1)
                self.V(vec, lambda A=A, tm=tm: vec.tensor_copy(A, tm))
                self.V(vec, lambda B=B, tm=tm, A=A: vec.tensor_sub(B, tm, A))
            vec.wait_ge(self.sem["tM"], S["tM"] + DC * 2)
            m = self.rows[:, 0:TS]
            r = self.rows[:, TS:2 * TS]
            v2 = self.rows[:, 2 * TS:3 * TS]
            psx = self.bank(4)[0:1, :TS]
            psq = self.bank(5)[0:1, :TS]
            self.V(vec, lambda: vec.tensor_scalar_mul(m, psx, 1.0 / c.D))
            self.V(vec, lambda: vec.tensor_scalar_mul(v2, psq, 1.0 / c.D))
            self.V(vec, lambda: vec.tensor_mul(r, m, m))
            self.V(vec, lambda: vec.tensor_sub(v2, v2, r))
            self.V(vec, lambda: vec.tensor_scalar_add(v2, v2, 1e-5))
            vec.wait_ge(self.sem["aC"], S["aC"] + 1)
            self.V(vec, lambda: vec.reciprocal(r, v2))
            for i in range(DC):
                vec.wait_ge(self.sem[xsem[i % 2]],
                            S[xsem[i % 2]] + 16 * (DC // 2 + i // 2 + 1))
                vec.wait_ge(self.sem["tM"], tM_bc)
                vec.wait_ge(self.sem["aC"], S["aC"] + 3)
                xc = self.xck[:, (i % 2) * TS:(i % 2 + 1) * TS]
                tm = self.tmp32[:, (i % 2) * TS:(i % 2 + 1) * TS]
                ur = self.inl[:, i * TS:(i + 1) * TS]
                ue = self.inl[:, (DC + i) * TS:(DC + i + 1) * TS]
                mb = self.bcast[:, 0:TS]
                rb = self.bcast[:, TS:2 * TS]
                self.V(vec, lambda tm=tm, xc=xc, mb=mb: vec.tensor_sub(tm, xc, mb))
                self.V(vec, lambda tm=tm, rb=rb: vec.tensor_mul(tm, tm, rb))
                self.V(vec, lambda ur=ur, tm=tm: vec.tensor_copy(ur, tm))
                self.V(vec, lambda ue=ue, tm=tm, ur=ur: vec.tensor_sub(ue, tm, ur))

        @b.scalar
        def _(sc):
            sc.wait_ge(self.sem["vC"], S["vC"] + DC * nV1 + 5)
            v2 = self.rows[:, 2 * TS:3 * TS]
            self.A(sc, lambda: sc.activation(v2, v2, AF.Sqrt))
            sc.wait_ge(self.sem["tM"], tM_bc)
            sc.wait_ge(self.sem["vC"], S["vC"] + DC * nV1 + nR)
            self.A(sc, lambda: sc.copy(self.bcast[:, 0:TS], self.bank(2)[:, :TS]))
            self.A(sc, lambda: sc.copy(self.bcast[:, TS:2 * TS], self.bank(3)[:, :TS]))

    def ph_mm(self, wscr, OC, in_src, evac, out_dst, ts, scale_col,
             resid_src=None):
        b, c = self.block, self.c
        DC, TS = c.DC, c.TS
        S = dict(self.ct)
        load_limbs = in_src in ("hLd", "attnL")
        n_in = 2 * DC if load_limbs else 0
        nEv = {"relu": 2, "resid": 1, "qk": 2}[evac]
        nEa = {"relu": 1, "resid": 0, "qk": 1}[evac]
        nStW = {"relu": 2, "resid": 1, "qk": 2}[evac]
        scname, sccol_base = scale_col
        wsem = ("wA", "wB")
        xsem = ("xA", "xB")
        stsem = ("stA", "stB")

        @b.sync
        def _(sync):
            self.wait_stores(sync, S)
            if load_limbs:
                sync.wait_ge(self.sem["tM"], S["tM"])
                srcT = self.hLd if in_src == "hLd" else self.attnL
                for l in range(2):
                    for i in range(DC):
                        ap = (srcT[l, i, :, :] if in_src == "hLd"
                              else srcT[l, i, :, ts * TS:(ts + 1) * TS])
                        dd = self.inl[:, (l * DC + i) * TS:(l * DC + i + 1) * TS]
                        self.dma(sync, "blk", dd, ap)
            for o in range(OC):
                sync.wait_ge(self.sem["tM"],
                             S["tM"] + (o - 1 if o >= 2 else 0))
                self.dma(sync, wsem[o % 2],
                         self.wsb[:, (o % 2) * DC * 128:(o % 2 + 1) * DC * 128],
                         wscr[o, :, :])
                if evac == "resid":
                    sync.wait_ge(self.sem["vC"],
                                 S["vC"] + ((o - 1) * nEv if o >= 2 else 0))
                    self.dma(sync, xsem[o % 2],
                             self.xck[:, (o % 2) * TS:(o % 2 + 1) * TS],
                             resid_src[o, :, ts * TS:(ts + 1) * TS])

        @b.tensor
        def _(t):
            for o in range(OC):
                t.wait_ge(self.sem[wsem[o % 2]], S[wsem[o % 2]] + 16 * (o // 2 + 1))
                if o == 0:
                    if load_limbs:
                        t.wait_ge(self.sem["blk"], S["blk"] + n_in * 16)
                    else:
                        t.wait_ge(self.sem["vC"], S["vC"])
                if o >= 4:
                    if nEa:
                        t.wait_ge(self.sem["aC"], S["aC"] + (o - 3) * nEa)
                    t.wait_ge(self.sem["vC"], S["vC"] + (o - 3) * nEv)
                else:
                    t.wait_ge(self.sem["aC"], S["aC"])
                    t.wait_ge(self.sem["vC"], S["vC"])
                pb = self.bank(o % 4)[:, :TS]
                wbuf = self.wsb[:, (o % 2) * DC * 128:(o % 2 + 1) * DC * 128]
                n = 2 * DC
                k = 0
                ins = None
                for i in range(DC):
                    for l in range(2):
                        lim = self.inl[:, (l * DC + i) * TS:(l * DC + i + 1) * TS]
                        ins = t.matmul(pb, wbuf[:, i * 128:(i + 1) * 128], lim,
                                       start=(k == 0), stop=(k == n - 1))
                        k += 1
                self.tinc(ins)

        @b.scalar
        def _(sc):
            if nEa == 0:
                return
            for o in range(OC):
                sc.wait_ge(self.sem["tM"], S["tM"] + o + 1)
                if o >= 2:
                    sc.wait_ge(self.sem["vC"], S["vC"] + (o - 1) * nEv)
                else:
                    sc.wait_ge(self.sem["vC"], S["vC"])
                    self.wait_stores(sc, S)
                pb = self.bank(o % 4)[:, :TS]
                tm = self.tmp32[:, (o % 2) * TS:(o % 2 + 1) * TS]
                if evac == "relu":
                    scv = self.scal(scname)
                    self.A(sc, lambda tm=tm, pb=pb, scv=scv:
                           sc.activation(tm, pb, AF.Relu, scale=scv))
                else:
                    scv = self.sct[:, sccol_base + o:sccol_base + o + 1]
                    self.A(sc, lambda tm=tm, pb=pb, scv=scv: sc.mul(tm, pb, scv))

        @b.gpsimd
        def _(gp):
            for o in range(OC):
                gp.wait_ge(self.sem["vC"], S["vC"] + (o + 1) * nEv)
                tm = self.tmp32[:, (o % 2) * TS:(o % 2 + 1) * TS]
                lr = self.ltmp[:, (o % 2) * 2 * TS:(o % 2) * 2 * TS + TS]
                le = self.ltmp[:, (o % 2) * 2 * TS + TS:(o % 2 + 1) * 2 * TS]
                if evac == "resid":
                    self.dma(gp, stsem[o % 2],
                             out_dst[o, :, ts * TS:(ts + 1) * TS], tm)
                elif evac == "relu":
                    self.dma(gp, stsem[o % 2], out_dst[0, o, :, :], lr)
                    self.dma(gp, stsem[o % 2], out_dst[1, o, :, :], le)
                else:
                    tgt, rb = out_dst
                    for l, til in ((0, lr), (1, le)):
                        if rb is None:
                            dd = tgt[l, o, :, ts * TS:(ts + 1) * TS]
                        else:
                            r0 = rb + l * (OC * 128) + o * 128
                            dd = tgt[r0:r0 + 128, ts * TS:(ts + 1) * TS]
                        self.dma(gp, stsem[o % 2], dd, til)

        @b.vector
        def _(vec):
            for o in range(OC):
                if nEa:
                    vec.wait_ge(self.sem["aC"], S["aC"] + o + 1)
                else:
                    vec.wait_ge(self.sem["tM"], S["tM"] + o + 1)
                if o >= 2:
                    vec.wait_ge(self.sem[stsem[o % 2]],
                                S[stsem[o % 2]] + 16 * nStW * (o // 2))
                else:
                    self.wait_stores(vec, S)
                tm = self.tmp32[:, (o % 2) * TS:(o % 2 + 1) * TS]
                lr = self.ltmp[:, (o % 2) * 2 * TS:(o % 2) * 2 * TS + TS]
                le = self.ltmp[:, (o % 2) * 2 * TS + TS:(o % 2 + 1) * 2 * TS]
                if evac == "resid":
                    pb = self.bank(o % 4)[:, :TS]
                    xc = self.xck[:, (o % 2) * TS:(o % 2 + 1) * TS]
                    vec.wait_ge(self.sem[xsem[o % 2]],
                                S[xsem[o % 2]] + 16 * (o // 2 + 1))
                    scv = (self.scal(scname) if sccol_base is None
                           else self.sct[:, sccol_base + o:sccol_base + o + 1])
                    self.V(vec, lambda tm=tm, pb=pb, xc=xc, scv=scv:
                           vec.scalar_tensor_tensor(tm, pb, scv, xc,
                                                    ALU.mult, ALU.add))
                else:
                    self.V(vec, lambda lr=lr, tm=tm: vec.tensor_copy(lr, tm))
                    self.V(vec, lambda le=le, tm=tm, lr=lr: vec.tensor_sub(le, tm, lr))

    def ph_qkv(self, src):
        c = self.c
        for ts in range(c.NTS):
            self.ph_ln(src, ts)
            self.ph_mm(self.wqs, c.DC, in_src="sbuf", evac="qk",
                       out_dst=(self.qLd, None), ts=ts, scale_col=("", 0))
            self.ph_mm(self.wks, c.KVC, in_src="sbuf", evac="qk",
                       out_dst=(self.ccink, 0), ts=ts, scale_col=("", 2 * c.DC))
            if ts == c.NTS - 1:
                self.ph_exchange_one(self.ccink, self.ccoutk)
            self.ph_vproj(ts)
        self.ph_exchange_one(self.ccinv, self.ccoutv)

    def ph_exchange_one(self, cin, cout):
        b = self.block
        S = dict(self.ct)

        @b.gpsimd
        def _(gp):
            self.wait_stores(gp, S)
            gp.collective_compute(
                "AllGather", ALU.bypass,
                replica_groups=[[0, 1], [2, 3], [4, 5], [6, 7]],
                ins=[cin.opt()], outs=[cout.opt()],
            ).then_inc(self.sem["CC"], 1)
            self.ct["CC"] += 1

    def ph_vproj(self, ts):
        b, c = self.block, self.c
        DC, TS, KVD = c.DC, c.TS, c.KVD
        TC = c.TC
        VT = max(KVD // 512, 1)
        NTILE = min(512, KVD)
        assert TC <= 4
        wsem = ("wA", "wB")
        stsem = ("stA", "stB")
        nkv = NTILE // 128
        nV = 3

        for vh in range(VT):
            S = dict(self.ct)

            @b.sync
            def _(sync):
                self.wait_stores(sync, S)
                for i in range(DC):
                    sync.wait_ge(self.sem["tM"],
                                 S["tM"] + (1 + i - 1 if i >= 2 else 0))
                    self.dma(sync, wsem[i % 2],
                             self.wsb[:, (i % 2) * DC * 128:
                                      (i % 2) * DC * 128 + NTILE],
                             self.wvs[i, :, vh * 512:vh * 512 + NTILE])

            @b.tensor
            def _(t):
                t.wait_ge(self.sem["vC"], S["vC"])
                t.wait_ge(self.sem["aC"], S["aC"])
                self.tinc(t.matmul(self.bank(5)[:, :NTILE], self.onesf[:],
                                   self.rows[:, 2 * c.T + vh * 512:
                                             2 * c.T + vh * 512 + NTILE],
                                   start=True, stop=True))
                for i in range(DC):
                    t.wait_ge(self.sem[wsem[i % 2]],
                              S[wsem[i % 2]] + 16 * (i // 2 + 1))
                    if i == 0:
                        t.wait_ge(self.sem["aC"], S["aC"] + 1)
                    wbuf = self.wsb[:, (i % 2) * DC * 128:
                                    (i % 2) * DC * 128 + NTILE]
                    ins = None
                    for tc in range(TC):
                        for l in range(2):
                            lim = self.inl[:, (l * DC + i) * TS + tc * 128:
                                           (l * DC + i) * TS + (tc + 1) * 128]
                            ins = t.matmul(self.bank(tc)[:, :NTILE], lim, wbuf,
                                           start=(i == 0 and l == 0),
                                           stop=(i == DC - 1 and l == 1))
                    self.tinc(ins)
            tM_acc = self.ct["tM"]

            @b.scalar
            def _(sc):
                sc.wait_ge(self.sem["tM"], S["tM"] + 1)
                sc.wait_ge(self.sem["vC"], S["vC"])
                vb = self.bcast[:, 0:NTILE]
                self.A(sc, lambda vb=vb: sc.copy(vb, self.bank(5)[:, :NTILE]))

            @b.vector
            def _(vec):
                vec.wait_ge(self.sem["tM"], tM_acc)
                for tc in range(TC):
                    bidx = tc
                    pbk = self.bank(tc)[:, :NTILE]
                    tm = self.tmp32[:, (bidx % 2) * TS:(bidx % 2) * TS + NTILE]
                    lr = self.ltmp[:, (bidx % 2) * 2 * TS:
                                   (bidx % 2) * 2 * TS + NTILE]
                    le = self.ltmp[:, ((bidx % 2) * 2 + 1) * TS:
                                   ((bidx % 2) * 2 + 1) * TS + NTILE]
                    vb = self.bcast[:, 0:NTILE]
                    if bidx >= 2:
                        vec.wait_ge(self.sem[stsem[bidx % 2]],
                                    S[stsem[bidx % 2]] +
                                    16 * 2 * nkv * (bidx // 2))
                    else:
                        self.wait_stores(vec, S)
                    self.V(vec, lambda tm=tm, pbk=pbk, vb=vb:
                           vec.tensor_mul(tm, pbk, vb))
                    self.V(vec, lambda lr=lr, tm=tm: vec.tensor_copy(lr, tm))
                    self.V(vec, lambda le=le, tm=tm, lr=lr:
                           vec.tensor_sub(le, tm, lr))

            @b.gpsimd
            def _(gp):
                for tc in range(TC):
                    bidx = tc
                    gp.wait_ge(self.sem["vC"], S["vC"] + (bidx + 1) * nV)
                    lr = self.ltmp[:, (bidx % 2) * 2 * TS:
                                   (bidx % 2) * 2 * TS + NTILE]
                    le = self.ltmp[:, ((bidx % 2) * 2 + 1) * TS:
                                   ((bidx % 2) * 2 + 1) * TS + NTILE]
                    kcg = (ts * TS) // 128 + tc
                    for j in range(nkv):
                        kvc = vh * 4 + j if NTILE >= 512 else j
                        for l, til in ((0, lr), (1, le)):
                            self.dma(gp, stsem[bidx % 2],
                                     self.ccinv[(kvc * 2 + l) * 128:
                                                (kvc * 2 + l) * 128 + 128,
                                                kcg * 128:(kcg + 1) * 128],
                                     til[:, j * 128:(j + 1) * 128])

    def att_views(self):
        c = self.c
        T, TS, KC = c.T, c.TS, c.KC
        inl = self.inl
        off = 0
        qA = inl[:, off:off + T]; off += T
        qB = inl[:, off:off + T]; off += T
        kA = inl[:, off:off + 2 * T]; off += 2 * T
        kB = inl[:, off:off + 2 * T]; off += 2 * T
        vA = inl[:, off:off + KC * 128]; off += KC * 128
        vB = inl[:, off:off + KC * 128]; off += KC * 128
        pA = inl[:, off:off + KC * TS]; off += KC * TS
        pB = inl[:, off:off + KC * TS]; off += KC * TS
        assert off <= c.INLW, (off, c.INLW)
        p0 = self.xck[:, 0:TS]
        p1 = self.xck[:, TS:2 * TS]
        return qA, qB, kA, kB, vA, vB, pA, pB, (p0, p1)

    def ph_att_head(self, h):
        b, c = self.block, self.c
        T, TS, KC, QC, KT = c.T, c.TS, c.KC, c.QC, c.KT
        S = dict(self.ct)
        qA, qB, kA, kB, vA, vB, pA, pB, ptmp = self.att_views()
        kvh = h // c.NREP
        KR = c.KVC * 128
        kcpr = T // 128

        @b.sync
        def _(sync):
            if h == 0:
                sync.wait_ge(self.sem["CC"], 2)
            sync.wait_ge(self.sem["tM"], S["tM"])
            sync.wait_ge(self.sem["vC"], S["vC"])
            self.wait_stores(sync, S)
            self.dma(sync, "blk", qA, self.qLd[0, h, :, :])
            self.dma(sync, "blk", qB, self.qLd[1, h, :, :])
            for r in range(2):
                self.dma(sync, "blk", kA[:, r * T:(r + 1) * T],
                         self.ccoutk[r * 2 * KR + kvh * 128:
                                     r * 2 * KR + (kvh + 1) * 128, :])
                self.dma(sync, "blk", kB[:, r * T:(r + 1) * T],
                         self.ccoutk[r * 2 * KR + KR + kvh * 128:
                                     r * 2 * KR + KR + (kvh + 1) * 128, :])
                for l, vt in ((0, vA), (1, vB)):
                    vr0 = r * (c.KVC * 2 * 128) + (kvh * 2 + l) * 128
                    self.dma(sync, "blk",
                             vt[:, r * kcpr * 128:(r + 1) * kcpr * 128],
                             self.ccoutv[vr0:vr0 + 128, :])
        ld_head = self.ct["blk"]

        nPQ = 2 * KT

        @b.tensor
        def _(t):
            t.wait_ge(self.sem["blk"], ld_head)
            t.wait_ge(self.sem["aC"], S["aC"])
            for qc in range(QC):
                for kt in range(KT):
                    j = qc * KT + kt
                    if j >= 4:
                        qc2, kt2 = divmod(j - 4, KT)
                        t.wait_ge(self.sem["vC"], S["vC"] + qc2 * nPQ + kt2 + 1)
                    self.tinc(t.matmul(self.bank(j % 4)[:, :TS],
                                       qA[:, qc * 128:(qc + 1) * 128],
                                       kA[:, kt * TS:(kt + 1) * TS],
                                       start=True, stop=True))

        @b.vector
        def _(vec):
            vec.wait_ge(self.sem["aC"], S["aC"])
            for qc in range(QC):
                for kt in range(KT):
                    vec.wait_ge(self.sem["tM"], S["tM"] + qc * KT + kt + 1)
                    bidx = (qc * KT + kt) % 4
                    mx = self.bcast[:, kt:kt + 1]
                    self.V(vec, lambda mx=mx, bidx=bidx: vec.reduce_max(
                        mx, self.bank(bidx)[:, :TS], axis=AX))
                mcol = self.bcast[:, 8 + qc:8 + qc + 1]
                self.V(vec, lambda mcol=mcol:
                       vec.tensor_copy(mcol, self.bcast[:, 0:1]))
                for kt in range(1, KT):
                    self.V(vec, lambda mcol=mcol, kt=kt: vec.tensor_max(
                        mcol, mcol, self.bcast[:, kt:kt + 1]))
        vC_stats = self.ct["vC"]

        @b.gpsimd
        def _(gp):
            gp.wait_ge(self.sem["vC"], vC_stats)
            for qc in range(QC):
                self.dma(gp, "stO", self.mrow_d[h, qc].unsqueeze(1),
                         self.bcast[:, 8 + qc:8 + qc + 1])
        stO_mst = self.ct["stO"]

        @b.sync
        def _(sync):
            sync.wait_ge(self.sem["stO"], stO_mst)
            self.dma(sync, "blk", self.rows[:, 0:T],
                     self.mrow_d[h].rearrange("a b -> (a b)").unsqueeze(0))
        mrow_ld = self.ct["blk"]

        @b.vector
        def _(vec):
            vec.wait_ge(self.sem["blk"], mrow_ld)
            self.V(vec, lambda: vec.tensor_copy(self.rowsR[:], self.rows[:, 0:T]))
        vC_mrow = self.ct["vC"]

        for qt in range(c.NTS):
            self.att_qt(h, qt, qA, qB, kA, kB, vA, vB, pA, pB, ptmp, vC_mrow)

    def att_qt(self, h, qt, qA, qB, kA, kB, vA, vB, pA, pB, ptmp, vC_mrow):
        b, c = self.block, self.c
        T, TS, KC = c.T, c.TS, c.KC
        S = dict(self.ct)
        nEvK = 2

        @b.tensor
        def _(t):
            t.wait_ge(self.sem["vC"], max(S["vC"], vC_mrow))
            for kc in range(KC):
                if kc >= 4:
                    t.wait_ge(self.sem["vC"], S["vC"] + (kc - 3) * nEvK)
                pbk = self.bank(kc % 4)[:, :TS]
                q0 = qA[:, qt * TS:(qt + 1) * TS]
                q1 = qB[:, qt * TS:(qt + 1) * TS]
                kk0 = kA[:, kc * 128:(kc + 1) * 128]
                kk1 = kB[:, kc * 128:(kc + 1) * 128]
                t.matmul(pbk, kk0, q0, start=True, stop=False)
                t.matmul(pbk, kk1, q0, start=False, stop=False)
                t.matmul(pbk, kk0, q1, start=False, stop=False)
                self.tinc(t.matmul(pbk, self.negrow[:],
                                   self.rowsR[:, qt * TS:(qt + 1) * TS],
                                   start=False, stop=True))
            t.wait_ge(self.sem["vC"], S["vC"] + KC * nEvK)
            psz = self.bank(5)[0:1, :TS]
            n = 2 * KC
            k = 0
            ins = None
            for kc in range(KC):
                for pl in (pA, pB):
                    ins = t.matmul(psz, self.onescr[:],
                                   pl[:, kc * TS:(kc + 1) * TS],
                                   start=(k == 0), stop=(k == n - 1))
                    k += 1
            self.tinc(ins)
            t.wait_ge(self.sem["vC"], S["vC"] + KC * nEvK + 1)
            self.tinc(t.matmul(self.bank(6)[:, :TS], self.onesf[:],
                               self.rows[:, T:T + TS], start=True, stop=True))
            psav = self.bank(7)[:, :TS]
            n = 3 * KC
            k = 0
            for kc in range(KC):
                for (vt, pl) in ((vA, pA), (vA, pB), (vB, pA)):
                    ins = t.matmul(psav, vt[:, kc * 128:(kc + 1) * 128],
                                   pl[:, kc * TS:(kc + 1) * TS],
                                   start=(k == 0), stop=(k == n - 1))
                    k += 1
            self.tinc(ins)

        @b.scalar
        def _(sc):
            for kc in range(KC):
                sc.wait_ge(self.sem["tM"], S["tM"] + kc + 1)
                if kc >= 2:
                    sc.wait_ge(self.sem["vC"], S["vC"] + (kc - 1) * nEvK)
                else:
                    sc.wait_ge(self.sem["vC"], S["vC"])
                pt = ptmp[kc % 2]
                pbk = self.bank(kc % 4)[:, :TS]
                self.A(sc, lambda pt=pt, pbk=pbk:
                       sc.activation(pt, pbk, AF.Exp))
            sc.wait_ge(self.sem["tM"], S["tM"] + KC + 2)
            sc.wait_ge(self.sem["vC"], S["vC"] + KC * nEvK + 1)
            self.A(sc, lambda: sc.copy(self.bcast[:, TS:2 * TS],
                                       self.bank(6)[:, :TS]))

        @b.vector
        def _(vec):
            for kc in range(KC):
                vec.wait_ge(self.sem["aC"], S["aC"] + kc + 1)
                pt = ptmp[kc % 2]
                pr = pA[:, kc * TS:(kc + 1) * TS]
                pe = pB[:, kc * TS:(kc + 1) * TS]
                self.V(vec, lambda pr=pr, pt=pt: vec.tensor_copy(pr, pt))
                self.V(vec, lambda pe=pe, pt=pt, pr=pr: vec.tensor_sub(pe, pt, pr))
            vec.wait_ge(self.sem["tM"], S["tM"] + KC + 1)
            rz = self.rows[:, T:T + TS]
            self.V(vec, lambda rz=rz: vec.reciprocal(rz, self.bank(5)[0:1, :TS]))
            vec.wait_ge(self.sem["tM"], S["tM"] + KC + 3)
            vec.wait_ge(self.sem["aC"], S["aC"] + KC + 1)
            self.wait_stores(vec, S)
            tm = self.tmp32[:, 0:TS]
            lr = self.ltmp[:, 0:TS]
            le = self.ltmp[:, TS:2 * TS]
            self.V(vec, lambda tm=tm: vec.tensor_mul(
                tm, self.bank(7)[:, :TS], self.bcast[:, TS:2 * TS]))
            self.V(vec, lambda lr=lr, tm=tm: vec.tensor_copy(lr, tm))
            self.V(vec, lambda le=le, tm=tm, lr=lr: vec.tensor_sub(le, tm, lr))

        @b.gpsimd
        def _(gp):
            gp.wait_ge(self.sem["vC"], self.ct["vC"])
            lr = self.ltmp[:, 0:TS]
            le = self.ltmp[:, TS:2 * TS]
            self.dma(gp, "stO", self.attnL[0, h, :, qt * TS:(qt + 1) * TS], lr)
            self.dma(gp, "stO", self.attnL[1, h, :, qt * TS:(qt + 1) * TS], le)

    def ph_end(self):
        b = self.block
        S = dict(self.ct)

        @b.gpsimd
        def _(gp):
            for s in ALL_SEMS:
                if S[s] > 0:
                    gp.wait_ge(self.sem[s], S[s])


def make_inputs_per_core(cfg, core, full):
    c = cfg
    seq = core // 2
    half = core % 2
    x = np.asarray(full["L0_attn_hidden_states"])
    xs = np.ascontiguousarray(
        x[seq, half * c.T:(half + 1) * c.T, :].T).reshape(c.DC, 128, c.T)

    def u8(a):
        a = np.asarray(a)
        return a.view(np.uint8) if a.dtype == np.bool_ else a.astype(np.uint8)

    cei = np.ones(c.D, dtype=np.float32)
    cei[c.NE:] = np.float32(c.EI_NEG)
    return {
        "x0": xs.astype(np.float32),
        "wq_n": np.asarray(full["L0_attn_q_weight"], dtype=np.float32),
        "wk_n": np.asarray(full["L0_attn_k_weight"], dtype=np.float32),
        "wv_n": np.asarray(full["L0_attn_v_weight"], dtype=np.float32),
        "wo_n": np.asarray(full["L0_attn_o_weight"], dtype=np.float32),
        "qsc_in": np.asarray(full["L0_attn_q_scale"], dtype=np.float32),
        "ksc_in": np.asarray(full["L0_attn_k_scale"], dtype=np.float32),
        "vsc_in": np.asarray(full["L0_attn_v_scale"], dtype=np.float32),
        "osc_in": np.asarray(full["L0_attn_o_scale"], dtype=np.float32),
        "wqT": np.ascontiguousarray(np.asarray(full["L0_attn_q_weight"], dtype=np.float32).T),
        "wkT": np.ascontiguousarray(np.asarray(full["L0_attn_k_weight"], dtype=np.float32).T),
        "wvT": np.ascontiguousarray(np.asarray(full["L0_attn_v_weight"], dtype=np.float32).T),
        "woT": np.ascontiguousarray(np.asarray(full["L0_attn_o_weight"], dtype=np.float32).T),
        "k0u": u8(full["L0_ffn_k0"]),
        "k1u": u8(full["L0_ffn_k1"]),
        "k2u": u8(full["L0_ffn_k2"]),
        "s0": np.asarray(full["L0_ffn_s0"], dtype=np.float32).reshape(1, 1),
        "s1": np.asarray(full["L0_ffn_s1"], dtype=np.float32).reshape(1, 1),
        "s2": np.asarray(full["L0_ffn_s2"], dtype=np.float32).reshape(1, 1),
        "cei": np.ascontiguousarray(cei.reshape(c.DC, 128).T),
        "cei2": np.ascontiguousarray((2.0 * cei).reshape(c.DC, 128).T),
    }


def build_program(cfg, name="mkmac1", phase_limit=None):
    nc = bass.Bass("TRN2", target_bir_lowering=False, name=name)
    Build(nc, cfg, phase_limit=phase_limit).build()
    return nc


def kernel(**inputs):
    cfg = CFG()
    nc = build_program(cfg, name="mkmacaron_v2")
    in_maps = [make_inputs_per_core(cfg, core, inputs) for core in range(8)]
    res = run_bass_kernel_spmd(nc, in_maps, core_ids=list(range(8)))
    out = np.empty((4, 2048, 4096), dtype=np.float32)
    for core in range(8):
        seq, half = core // 2, core % 2
        y = res.results[core]["yout"].reshape(cfg.D, cfg.T)
        out[seq, half * cfg.T:(half + 1) * cfg.T, :] = y.T
    return out


# version 13
# speedup vs baseline: 1.3569x; 1.0000x over previous
import numpy as np
import concourse.bass as bass
import concourse.mybir as mybir
from concourse.bass_utils import run_bass_kernel_spmd
from contextlib import ExitStack

F32 = mybir.dt.float32
F32R = mybir.dt.float32r
U8 = mybir.dt.uint8
AF = mybir.ActivationFunctionType
ALU = mybir.AluOpType
AX = mybir.AxisListType.X

LOAD_SEMS = ["blk", "wA", "wB", "xA", "xB", "kA", "kB"]
STORE_SEMS = ["stA", "stB", "stC", "stD", "stO"]
ALL_SEMS = LOAD_SEMS + STORE_SEMS + ["tM", "vC", "aC", "gC", "CC"]


class CFG:
    def __init__(self, D=4096, T=1024, HD=128, KVH=8, P_EI=0.8):
        self.D = D
        self.T = T
        self.HD = HD
        self.DC = D // 128
        self.HEADS = D // HD
        self.KVH = KVH
        self.KVD = KVH * HD
        self.KVC = self.KVD // 128
        self.NREP = self.HEADS // self.KVH
        self.TS = T // 2
        self.NTS = 2
        self.TC = self.TS // 128
        self.SEQK = 2 * T
        self.KC = self.SEQK // 128
        self.QC = T // 128
        self.KT = self.SEQK // self.TS
        self.NE = round(D * P_EI)
        self.EI_NEG = -P_EI / (1.0 - P_EI)
        att_need = (2 * self.T + 2 * 2 * self.T + 2 * self.KC * 128 +
                    2 * self.KC * self.TS + 2 * self.TS)
        self.INLW = max(2 * self.DC * self.TS, att_need, 2 * D)
        assert self.TS <= 512 and HD == 128
        assert self.DC % 4 == 0


class Build:
    def __init__(self, nc, cfg, phase_limit=None):
        self.nc = nc
        self.c = cfg
        self.phase_limit = phase_limit
        self.ct = {s: 0 for s in ALL_SEMS}

    def V(self, vec, make):
        vec.wait_ge(self.sem["vC"], self.ct["vC"])
        make().then_inc(self.sem["vC"], 1)
        self.ct["vC"] += 1

    def A(self, sc, make):
        sc.wait_ge(self.sem["aC"], self.ct["aC"])
        make().then_inc(self.sem["aC"], 1)
        self.ct["aC"] += 1

    def G(self, gp, make):
        gp.wait_ge(self.sem["gC"], self.ct["gC"])
        make().then_inc(self.sem["gC"], 1)
        self.ct["gC"] += 1

    def tinc(self, ins):
        ins.then_inc(self.sem["tM"], 1)
        self.ct["tM"] += 1

    def dma(self, eng, stream, out, in_, **kw):
        eng.dma_start(out=out, in_=in_, **kw).then_inc(self.sem[stream], 16)
        self.ct[stream] += 16

    def wait_stores(self, eng, S):
        for s in STORE_SEMS:
            if S[s] > 0:
                eng.wait_ge(self.sem[s], S[s])

    def build(self):
        nc, c = self.nc, self.c
        D, T, TS, DC = c.D, c.T, c.TS, c.DC
        dt = nc.dram_tensor
        self.x0 = dt("x0", [DC, 128, T], F32, kind="ExternalInput").ap()
        self.qsc_in = dt("qsc_in", [D], F32, kind="ExternalInput").ap()
        self.ksc_in = dt("ksc_in", [c.KVD], F32, kind="ExternalInput").ap()
        self.vsc_in = dt("vsc_in", [c.KVD], F32, kind="ExternalInput").ap()
        self.osc_in = dt("osc_in", [D], F32, kind="ExternalInput").ap()
        self.k0u = dt("k0u", [D, D], U8, kind="ExternalInput").ap()
        self.k1u = dt("k1u", [D, D], U8, kind="ExternalInput").ap()
        self.k2u = dt("k2u", [D, D], U8, kind="ExternalInput").ap()
        self.s0 = dt("s0", [1, 1], F32, kind="ExternalInput").ap()
        self.s1 = dt("s1", [1, 1], F32, kind="ExternalInput").ap()
        self.s2 = dt("s2", [1, 1], F32, kind="ExternalInput").ap()
        self.wqT = dt("wqT", [D, D], F32, kind="ExternalInput").ap()
        self.wkT = dt("wkT", [D, c.KVD], F32, kind="ExternalInput").ap()
        self.wvT = dt("wvT", [D, c.KVD], F32, kind="ExternalInput").ap()
        self.woT = dt("woT", [D, D], F32, kind="ExternalInput").ap()
        self.cei = dt("cei", [128, DC], F32, kind="ExternalInput").ap()
        self.cei2 = dt("cei2", [128, DC], F32, kind="ExternalInput").ap()
        self.yout = dt("yout", [DC, 128, T], F32, kind="ExternalOutput").ap()

        self.xcur = dt("xcur", [DC, 128, T], F32).ap()
        self.w0s = dt("w0s", [DC, 128, DC * 128], F32R).ap()
        self.w1s = dt("w1s", [DC, 128, DC * 128], F32R).ap()
        self.w2s = dt("w2s", [DC, 128, DC * 128], F32R).ap()
        self.wqs = dt("wqs", [DC, 128, DC * 128], F32R).ap()
        self.wks = dt("wks", [c.KVC, 128, DC * 128], F32R).ap()
        self.wos = dt("wos", [DC, 128, DC * 128], F32R).ap()
        self.wvs = dt("wvs", [DC, 128, c.KVD], F32R).ap()
        self.qLd = dt("qLd", [2, DC, 128, T], F32R).ap()
        self.hLd = dt("hLd", [2, DC, 128, TS], F32R).ap()
        self.ccink = dt("ccink", [2 * c.KVC * 128, T], F32R).ap()
        self.ccoutk = dt("ccoutk", [2 * 2 * c.KVC * 128, T], F32R).ap()
        self.ccinv = dt("ccinv", [c.KVC * 2 * 128, T], F32R).ap()
        self.ccoutv = dt("ccoutv", [2 * c.KVC * 2 * 128, T], F32R).ap()
        self.attnL = dt("attnL", [2, DC, 128, T], F32R).ap()
        self.mrow_d = dt("mrow_d", [c.HEADS, c.QC, 128], F32).ap()
        self.srow_d = dt("srow_d", [D], F32).ap()

        es = ExitStack()
        sb, ps = nc.sbuf_tensor, nc.psum_tensor
        E = es.enter_context
        self.inl = E(sb("inl", [128, c.INLW], F32R))
        self.wsb = E(sb("wsb", [128, 2 * DC * 128], F32R))
        self.tmp32 = E(sb("tmp32", [128, 2 * TS], F32))
        self.ltmp = E(sb("ltmp", [128, 4 * TS], F32R))
        self.xck = E(sb("xck", [128, 2 * TS], F32))
        self.bcast = E(sb("bcast", [128, 2 * TS], F32))
        self.rows = E(sb("rows", [1, max(3 * T, 1024)], F32))
        self.crow = E(sb("crow", [1, 1024], F32))
        self.sct = E(sb("sct", [128, 2 * DC + c.KVC + 8], F32))
        self.ceis = E(sb("ceis", [128, 2 * DC], F32))
        self.onesf = E(sb("onesf", [1, 128], F32))
        self.onescf = E(sb("onescf", [128, 1], F32))
        self.onescr = E(sb("onescr", [128, 1], F32R))
        self.negrow = E(sb("negrow", [1, 128], F32R))
        self.rowsR = E(sb("rowsR", [1, c.T], F32R))
        self.psb = E(ps("psb", [128, 4096], F32))

        self.sem = {}
        for s in ALL_SEMS:
            self.sem[s] = E(nc.semaphore(s))
        self.block = E(nc.Block())
        self.emit_all(phase_limit=self.phase_limit)
        es.close()
        return nc

    def bank(self, i):
        return self.psb[:, i * 512:(i + 1) * 512]

    def scal(self, name):
        c = self.c
        base = 2 * c.DC + c.KVC
        idx = {"s0": base, "s1": base + 1, "s2": base + 2,
               "rs0": base + 3, "rs1": base + 4, "hs2": base + 5}[name]
        return self.sct[:, idx:idx + 1]

    def emit_all(self, phase_limit=None):
        c = self.c
        phases = [self.ph_consts,
                  lambda: self.ph_conv_ffn(self.k0u, self.w0s, use_c=False),
                  lambda: self.ph_conv_ffn(self.k1u, self.w1s, use_c=True),
                  lambda: self.ph_conv_ffn(self.k2u, self.w2s, use_c=True),
                  self.ph_conv_att_q,
                  self.ph_conv_att_k,
                  self.ph_conv_att_v,
                  self.ph_conv_att_o,
                  lambda: self.ph_ffn(src=self.x0, dst=self.xcur),
                  lambda: self.ph_qkv(src=self.xcur)]
        phases += [(lambda h=h: self.ph_att_head(h)) for h in range(c.HEADS)]
        phases += [(lambda ts=ts: self.ph_mm(
            self.wos, c.DC, in_src="attnL", evac="resid",
            resid_src=self.xcur, out_dst=self.xcur, ts=ts,
            scale_col=(None, c.DC))) for ts in range(c.NTS)]
        phases += [lambda: self.ph_ffn(src=self.xcur, dst=self.yout)]
        if phase_limit is None:
            phase_limit = len(phases)
        for ph in phases[:phase_limit]:
            ph()
        self.ph_end()

    def ph_consts(self):
        b, c = self.block, self.c

        @b.sync
        def _(sync):
            self.dma(sync, "blk", self.ceis[:, 0:c.DC], self.cei)
            self.dma(sync, "blk", self.ceis[:, c.DC:2 * c.DC], self.cei2)
            base = 2 * c.DC + c.KVC
            for j, s in enumerate([self.s0, self.s1, self.s2]):
                self.dma(sync, "blk", self.sct[:, base + j:base + j + 1],
                         bass.AP(s.tensor, 0, [[0, 128], [1, 1]]),
                         allow_slow_non_contiguous=True)
        blk_done = self.ct["blk"]

        @b.vector
        def _(vec):
            vec.wait_ge(self.sem["blk"], blk_done)
            self.V(vec, lambda: vec.memset(self.onesf[:], 1.0))
            self.V(vec, lambda: vec.memset(self.onescf[:], 1.0))
            self.V(vec, lambda: vec.tensor_copy(self.onescr[:], self.onescf[:]))
            self.V(vec, lambda: vec.tensor_scalar_mul(self.negrow[:], self.onesf[:], -1.0))
            base = 2 * c.DC + c.KVC
            self.V(vec, lambda: vec.tensor_scalar_mul(
                self.sct[:, base + 3:base + 4], self.sct[:, base:base + 1], 2.0))
            self.V(vec, lambda: vec.tensor_scalar_mul(
                self.sct[:, base + 4:base + 5], self.sct[:, base + 1:base + 2], 2.0))
            self.V(vec, lambda: vec.tensor_scalar_mul(
                self.sct[:, base + 5:base + 6], self.sct[:, base + 2:base + 3], 0.5))

    def ph_conv_ffn(self, ksrc, wdst, use_c):
        b, c = self.block, self.c
        DC = c.DC
        kslot = [self.xck[:, :].bitcast(U8), self.bcast[:, :].bitcast(U8)]
        S = dict(self.ct)
        nEv = 1
        GRP = 4
        nStW = DC // GRP
        ksem = ("kA", "kB")
        stsem = ("stA", "stB")

        @b.sync
        def _(sync):
            for i in range(DC):
                sync.wait_ge(self.sem["vC"],
                             S["vC"] + ((i - 1) * nEv if i >= 2 else 0))
                self.dma(sync, ksem[i % 2], kslot[i % 2][:, 0:c.D],
                         ksrc[i * 128:(i + 1) * 128, :])

        @b.vector
        def _(vec):
            for i in range(DC):
                vec.wait_ge(self.sem[ksem[i % 2]],
                            S[ksem[i % 2]] + 16 * (i // 2 + 1))
                if i >= 2:
                    vec.wait_ge(self.sem[stsem[i % 2]],
                                S[stsem[i % 2]] + 16 * nStW * (i // 2))
                else:
                    self.wait_stores(vec, S)
                    vec.wait_ge(self.sem["tM"], S["tM"])
                    vec.wait_ge(self.sem["aC"], S["aC"])
                sgn = self.inl[:, (i % 2) * c.D:(i % 2 + 1) * c.D]
                ksl = kslot[i % 2][:, 0:c.D]
                if use_c:
                    cc = self.ceis[:, i:i + 1]
                    c2 = self.ceis[:, DC + i:DC + i + 1]
                    self.V(vec, lambda sgn=sgn, ksl=ksl, cc=cc, c2=c2:
                           vec.tensor_scalar(sgn, ksl, c2, cc, ALU.mult, ALU.subtract))
                else:
                    self.V(vec, lambda sgn=sgn, ksl=ksl:
                           vec.tensor_scalar(sgn, ksl, 2.0, 1.0, ALU.mult, ALU.subtract))

        @b.gpsimd
        def _(gp):
            for i in range(DC):
                gp.wait_ge(self.sem["vC"], S["vC"] + (i + 1) * nEv)
                sgn = self.inl[:, (i % 2) * c.D:(i % 2 + 1) * c.D]
                for g in range(DC // GRP):
                    oc0 = g * GRP
                    self.dma(gp, stsem[i % 2],
                             wdst[oc0:oc0 + GRP, :, i * 128:(i + 1) * 128]
                             .transpose([1, 0, 2]),
                             sgn[:, oc0 * 128:(oc0 + GRP) * 128]
                             .rearrange("p (g o) -> p g o", o=128))

    def ph_conv_att_q(self):
        c = self.c
        rsq = float(np.float32(1.0) / np.float32(np.sqrt(np.float64(c.HD))))
        self.ph_sign(self.wqT, self.wqs, c.DC, whole=False,
                     scin=self.qsc_in, sccol=0, xsc=rsq)

    def ph_conv_att_k(self):
        c = self.c
        self.ph_sign(self.wkT, self.wks, c.KVC, whole=False,
                     scin=self.ksc_in, sccol=2 * c.DC, xsc=1.0)

    def ph_conv_att_v(self):
        c = self.c
        self.ph_sign(self.wvT, self.wvs, c.KVC, whole=True,
                     scin=self.vsc_in, sccol=-1, xsc=1.0)

    def ph_conv_att_o(self):
        c = self.c
        self.ph_sign(self.woT, self.wos, c.DC, whole=False,
                     scin=self.osc_in, sccol=c.DC, xsc=1.0)

    def ph_sign(self, srcT, wdst, OC, whole, scin, sccol, xsc):
        b, c = self.block, self.c
        DC = c.DC
        S = dict(self.ct)
        W = OC * 128
        nEa = 2
        GRP = 4 if OC % 4 == 0 else 1
        nStW = 1 if whole else OC // GRP
        NSG = min(512, W)
        nseg = W // NSG
        nR = 2
        wsem = ("wA", "wB")
        xsem = ("xA", "xB")
        stsem = ("stA", "stB")

        @b.sync
        def _(sync):
            for i in range(DC):
                if i >= 2:
                    sync.wait_ge(self.sem["aC"], S["aC"] + (i - 1) * nEa)
                else:
                    sync.wait_ge(self.sem["aC"], S["aC"])
                    sync.wait_ge(self.sem["vC"], S["vC"])
                    sync.wait_ge(self.sem["tM"], S["tM"])
                    self.wait_stores(sync, S)
                slab = self.inl[:, 2 * c.D + (i % 2) * c.D:
                                2 * c.D + (i % 2) * c.D + W]
                self.dma(sync, wsem[i % 2], slab,
                         srcT[i * 128:(i + 1) * 128, :].bitcast(F32R))
            for sg in range(nseg):
                sync.wait_ge(self.sem["vC"],
                             S["vC"] + max(sg - 1, 0) * nR)
                self.dma(sync, xsem[sg % 2],
                         self.rows[:, (sg % 2) * NSG:(sg % 2) * NSG + NSG],
                         bass.AP(scin.tensor, sg * NSG, [[NSG, 1], [1, NSG]]))

        @b.scalar
        def _(sc):
            for i in range(DC):
                sc.wait_ge(self.sem[wsem[i % 2]],
                           S[wsem[i % 2]] + 16 * (i // 2 + 1))
                if i >= 2:
                    sc.wait_ge(self.sem[stsem[i % 2]],
                               S[stsem[i % 2]] + 16 * nStW * (i // 2))
                    sc.wait_ge(self.sem["tM"], S["tM"] + i - 1)
                else:
                    self.wait_stores(sc, S)
                wslab = self.inl[:, 2 * c.D + (i % 2) * c.D:
                                 2 * c.D + (i % 2) * c.D + W].bitcast(F32)
                sgn = self.inl[:, (i % 2) * c.D:(i % 2) * c.D + W]
                absl = self.inl[:, 4 * c.D + (i % 2) * c.D:
                                4 * c.D + (i % 2) * c.D + W]
                self.A(sc, lambda sgn=sgn, w=wslab: sc.activation(sgn, w, AF.Sign))
                self.A(sc, lambda absl=absl, w=wslab:
                       sc.activation(absl, w, AF.Abs))

        @b.tensor
        def _(t):
            t.wait_ge(self.sem["vC"], S["vC"])
            t.wait_ge(self.sem["aC"], S["aC"])
            for i in range(DC):
                t.wait_ge(self.sem["aC"], S["aC"] + i * nEa + 2)
                absl = self.inl[:, 4 * c.D + (i % 2) * c.D:
                                4 * c.D + (i % 2) * c.D + W]
                ins = None
                for sg in range(nseg):
                    ins = t.matmul(self.bank(sg % 8)[0:1, :NSG], self.onescr[:],
                                   absl[:, sg * NSG:(sg + 1) * NSG],
                                   start=(i == 0), stop=(i == DC - 1))
                self.tinc(ins)

        @b.vector
        def _(vec):
            vec.wait_ge(self.sem["tM"], S["tM"] + DC)
            for sg in range(nseg):
                vec.wait_ge(self.sem[xsem[sg % 2]],
                            S[xsem[sg % 2]] + 16 * (sg // 2 + 1))
                seg = self.crow[:, (sg % 2) * 512:(sg % 2) * 512 + NSG]
                qseg = self.rows[:, (sg % 2) * NSG:(sg % 2) * NSG + NSG]
                if sg >= 2:
                    ssm = ("stC", "stD")[sg % 2]
                    vec.wait_ge(self.sem[ssm], S[ssm] + 16 * (sg // 2))
                self.V(vec, lambda seg=seg, sg=sg: vec.tensor_scalar(
                    seg, self.bank(sg % 8)[0:1, :NSG], 1.0 / c.D, 1e-8,
                    ALU.mult, ALU.max))
                if whole:
                    out = self.rows[:, 2 * c.T + sg * NSG:2 * c.T + (sg + 1) * NSG]
                else:
                    out = seg
                self.V(vec, lambda out=out, seg=seg, qseg=qseg:
                       vec.scalar_tensor_tensor(out, seg, xsc, qseg,
                                                ALU.mult, ALU.mult))

        @b.gpsimd
        def _(gp):
            for i in range(DC):
                gp.wait_ge(self.sem["aC"], S["aC"] + i * nEa + 1)
                sgn = self.inl[:, (i % 2) * c.D:(i % 2) * c.D + W]
                if whole:
                    self.dma(gp, stsem[i % 2], wdst[i, :, :], sgn)
                else:
                    for g in range(OC // GRP):
                        oc0 = g * GRP
                        if GRP == 1:
                            self.dma(gp, stsem[i % 2],
                                     wdst[oc0, :, i * 128:(i + 1) * 128],
                                     sgn[:, oc0 * 128:(oc0 + 1) * 128])
                        else:
                            self.dma(gp, stsem[i % 2],
                                     wdst[oc0:oc0 + GRP, :, i * 128:(i + 1) * 128]
                                     .transpose([1, 0, 2]),
                                     sgn[:, oc0 * 128:(oc0 + GRP) * 128]
                                     .rearrange("p (g o) -> p g o", o=128))
            if not whole:
                gp.wait_ge(self.sem["blk"], S["blk"])
                for sg in range(nseg):
                    gp.wait_ge(self.sem["vC"], S["vC"] + (sg + 1) * nR)
                    seg = self.crow[:, (sg % 2) * 512:(sg % 2) * 512 + NSG]
                    self.dma(gp, ("stC", "stD")[sg % 2],
                             bass.AP(self.srow_d.tensor, sg * NSG,
                                     [[NSG, 1], [1, NSG]]), seg)
        stC_seg = self.ct["stC"]
        stD_seg = self.ct["stD"]

        if not whole:
            @b.sync
            def _(sync):
                sync.wait_ge(self.sem["stC"], stC_seg)
                sync.wait_ge(self.sem["stD"], stD_seg)
                for oc in range(OC):
                    self.dma(sync, "blk", self.sct[:, sccol + oc:sccol + oc + 1],
                             bass.AP(self.srow_d.tensor, oc * 128,
                                     [[1, 128], [1, 1]]),
                             allow_slow_non_contiguous=True)

    def ph_ffn(self, src, dst):
        c = self.c
        for ts in range(c.NTS):
            self.ph_ln(src, ts)
            self.ph_mm(self.w0s, c.DC, in_src="sbuf", evac="relu",
                       out_dst=self.hLd, ts=ts, scale_col=("rs0", None))
            self.ph_mm(self.w1s, c.DC, in_src="hLd", evac="relu",
                       out_dst=self.hLd, ts=ts, scale_col=("rs1", None))
            self.ph_mm(self.w2s, c.DC, in_src="hLd", evac="resid",
                       resid_src=src, out_dst=dst, ts=ts, scale_col=("hs2", None))

    def ph_ln(self, src, ts):
        b, c = self.block, self.c
        DC, TS = c.DC, c.TS
        S = dict(self.ct)
        nV1, nV2, nR = 5, 4, 6
        xsem = ("xA", "xB")

        @b.sync
        def _(sync):
            self.wait_stores(sync, S)
            for i in range(DC):
                sync.wait_ge(self.sem["vC"],
                             S["vC"] + ((i - 1) * nV1 if i >= 2 else 0))
                self.dma(sync, xsem[i % 2],
                         self.xck[:, (i % 2) * TS:(i % 2 + 1) * TS],
                         src[i, :, ts * TS:(ts + 1) * TS])
            for i in range(DC):
                sync.wait_ge(self.sem["vC"], S["vC"] + DC * nV1 + nR +
                             max(i - 1, 0) * nV2)
                self.dma(sync, xsem[i % 2],
                         self.xck[:, (i % 2) * TS:(i % 2 + 1) * TS],
                         src[i, :, ts * TS:(ts + 1) * TS])

        @b.tensor
        def _(t):
            t.wait_ge(self.sem["aC"], S["aC"])
            for i in range(DC):
                t.wait_ge(self.sem["vC"], S["vC"] + i * nV1 + 2)
                A = self.ltmp[:, (i % 2) * 2 * TS:(i % 2) * 2 * TS + TS]
                B = self.ltmp[:, (i % 2) * 2 * TS + TS:(i % 2 + 1) * 2 * TS]
                psx = self.bank(4)[0:1, :TS]
                psq = self.bank(5)[0:1, :TS]
                t.matmul(psx, self.onescr[:], A, start=(i == 0), stop=False)
                self.tinc(t.matmul(psx, self.onescr[:], B, start=False,
                                   stop=(i == DC - 1)))
                t.wait_ge(self.sem["vC"], S["vC"] + i * nV1 + 5)
                t.matmul(psq, self.onescr[:], A, start=(i == 0), stop=False)
                self.tinc(t.matmul(psq, self.onescr[:], B, start=False,
                                   stop=(i == DC - 1)))
            t.wait_ge(self.sem["vC"], S["vC"] + DC * nV1 + nR)
            self.tinc(t.matmul(self.bank(2)[:, :TS], self.onesf[:],
                               self.rows[:, 0:TS], start=True, stop=True))
            self.tinc(t.matmul(self.bank(3)[:, :TS], self.onesf[:],
                               self.rows[:, TS:2 * TS], start=True, stop=True))
        tM_bc = self.ct["tM"]

        @b.vector
        def _(vec):
            for i in range(DC):
                vec.wait_ge(self.sem[xsem[i % 2]], S[xsem[i % 2]] + 16 * (i // 2 + 1))
                if i >= 2:
                    vec.wait_ge(self.sem["tM"], S["tM"] + (i - 1) * 2)
                else:
                    vec.wait_ge(self.sem["tM"], S["tM"])
                    self.wait_stores(vec, S)
                xc = self.xck[:, (i % 2) * TS:(i % 2 + 1) * TS]
                A = self.ltmp[:, (i % 2) * 2 * TS:(i % 2) * 2 * TS + TS]
                B = self.ltmp[:, (i % 2) * 2 * TS + TS:(i % 2 + 1) * 2 * TS]
                tm = self.tmp32[:, (i % 2) * TS:(i % 2 + 1) * TS]
                self.V(vec, lambda A=A, xc=xc: vec.tensor_copy(A, xc))
                self.V(vec, lambda B=B, xc=xc, A=A: vec.tensor_sub(B, xc, A))
                self.V(vec, lambda tm=tm, xc=xc: vec.tensor_mul(tm, xc, xc))
                vec.wait_ge(self.sem["tM"], S["tM"] + i * 2 + 1)
                self.V(vec, lambda A=A, tm=tm: vec.tensor_copy(A, tm))
                self.V(vec, lambda B=B, tm=tm, A=A: vec.tensor_sub(B, tm, A))
            vec.wait_ge(self.sem["tM"], S["tM"] + DC * 2)
            m = self.rows[:, 0:TS]
            r = self.rows[:, TS:2 * TS]
            v2 = self.rows[:, 2 * TS:3 * TS]
            psx = self.bank(4)[0:1, :TS]
            psq = self.bank(5)[0:1, :TS]
            self.V(vec, lambda: vec.tensor_scalar_mul(m, psx, 1.0 / c.D))
            self.V(vec, lambda: vec.tensor_scalar_mul(v2, psq, 1.0 / c.D))
            self.V(vec, lambda: vec.tensor_mul(r, m, m))
            self.V(vec, lambda: vec.tensor_sub(v2, v2, r))
            self.V(vec, lambda: vec.tensor_scalar_add(v2, v2, 1e-5))
            vec.wait_ge(self.sem["aC"], S["aC"] + 1)
            self.V(vec, lambda: vec.reciprocal(r, v2))
            for i in range(DC):
                vec.wait_ge(self.sem[xsem[i % 2]],
                            S[xsem[i % 2]] + 16 * (DC // 2 + i // 2 + 1))
                vec.wait_ge(self.sem["tM"], tM_bc)
                vec.wait_ge(self.sem["aC"], S["aC"] + 3)
                xc = self.xck[:, (i % 2) * TS:(i % 2 + 1) * TS]
                tm = self.tmp32[:, (i % 2) * TS:(i % 2 + 1) * TS]
                ur = self.inl[:, i * TS:(i + 1) * TS]
                ue = self.inl[:, (DC + i) * TS:(DC + i + 1) * TS]
                mb = self.bcast[:, 0:TS]
                rb = self.bcast[:, TS:2 * TS]
                self.V(vec, lambda tm=tm, xc=xc, mb=mb: vec.tensor_sub(tm, xc, mb))
                self.V(vec, lambda tm=tm, rb=rb: vec.tensor_mul(tm, tm, rb))
                self.V(vec, lambda ur=ur, tm=tm: vec.tensor_copy(ur, tm))
                self.V(vec, lambda ue=ue, tm=tm, ur=ur: vec.tensor_sub(ue, tm, ur))

        @b.scalar
        def _(sc):
            sc.wait_ge(self.sem["vC"], S["vC"] + DC * nV1 + 5)
            v2 = self.rows[:, 2 * TS:3 * TS]
            self.A(sc, lambda: sc.activation(v2, v2, AF.Sqrt))
            sc.wait_ge(self.sem["tM"], tM_bc)
            sc.wait_ge(self.sem["vC"], S["vC"] + DC * nV1 + nR)
            self.A(sc, lambda: sc.copy(self.bcast[:, 0:TS], self.bank(2)[:, :TS]))
            self.A(sc, lambda: sc.copy(self.bcast[:, TS:2 * TS], self.bank(3)[:, :TS]))

    def ph_mm(self, wscr, OC, in_src, evac, out_dst, ts, scale_col,
             resid_src=None):
        b, c = self.block, self.c
        DC, TS = c.DC, c.TS
        S = dict(self.ct)
        load_limbs = in_src in ("hLd", "attnL")
        n_in = 2 * DC if load_limbs else 0
        nEv = {"relu": 2, "resid": 1, "qk": 2}[evac]
        nEa = {"relu": 1, "resid": 0, "qk": 1}[evac]
        nStW = {"relu": 2, "resid": 1, "qk": 2}[evac]
        scname, sccol_base = scale_col
        wsem = ("wA", "wB")
        xsem = ("xA", "xB")
        stsem = ("stA", "stB")

        @b.sync
        def _(sync):
            self.wait_stores(sync, S)
            if load_limbs:
                sync.wait_ge(self.sem["tM"], S["tM"])
                srcT = self.hLd if in_src == "hLd" else self.attnL
                for l in range(2):
                    for i in range(DC):
                        ap = (srcT[l, i, :, :] if in_src == "hLd"
                              else srcT[l, i, :, ts * TS:(ts + 1) * TS])
                        dd = self.inl[:, (l * DC + i) * TS:(l * DC + i + 1) * TS]
                        self.dma(sync, "blk", dd, ap)
            for o in range(OC):
                sync.wait_ge(self.sem["tM"],
                             S["tM"] + (o - 1 if o >= 2 else 0))
                self.dma(sync, wsem[o % 2],
                         self.wsb[:, (o % 2) * DC * 128:(o % 2 + 1) * DC * 128],
                         wscr[o, :, :])
                if evac == "resid":
                    sync.wait_ge(self.sem["vC"],
                                 S["vC"] + ((o - 1) * nEv if o >= 2 else 0))
                    self.dma(sync, xsem[o % 2],
                             self.xck[:, (o % 2) * TS:(o % 2 + 1) * TS],
                             resid_src[o, :, ts * TS:(ts + 1) * TS])

        @b.tensor
        def _(t):
            for o in range(OC):
                t.wait_ge(self.sem[wsem[o % 2]], S[wsem[o % 2]] + 16 * (o // 2 + 1))
                if o == 0:
                    if load_limbs:
                        t.wait_ge(self.sem["blk"], S["blk"] + n_in * 16)
                    else:
                        t.wait_ge(self.sem["vC"], S["vC"])
                if o >= 4:
                    if nEa:
                        t.wait_ge(self.sem["aC"], S["aC"] + (o - 3) * nEa)
                    t.wait_ge(self.sem["vC"], S["vC"] + (o - 3) * nEv)
                else:
                    t.wait_ge(self.sem["aC"], S["aC"])
                    t.wait_ge(self.sem["vC"], S["vC"])
                pb = self.bank(o % 4)[:, :TS]
                wbuf = self.wsb[:, (o % 2) * DC * 128:(o % 2 + 1) * DC * 128]
                n = 2 * DC
                k = 0
                ins = None
                for i in range(DC):
                    for l in range(2):
                        lim = self.inl[:, (l * DC + i) * TS:(l * DC + i + 1) * TS]
                        ins = t.matmul(pb, wbuf[:, i * 128:(i + 1) * 128], lim,
                                       start=(k == 0), stop=(k == n - 1))
                        k += 1
                self.tinc(ins)

        @b.scalar
        def _(sc):
            if nEa == 0:
                return
            for o in range(OC):
                sc.wait_ge(self.sem["tM"], S["tM"] + o + 1)
                if o >= 2:
                    sc.wait_ge(self.sem["vC"], S["vC"] + (o - 1) * nEv)
                else:
                    sc.wait_ge(self.sem["vC"], S["vC"])
                    sc.wait_ge(self.sem["blk"], S["blk"])
                    self.wait_stores(sc, S)
                pb = self.bank(o % 4)[:, :TS]
                tm = self.tmp32[:, (o % 2) * TS:(o % 2 + 1) * TS]
                if evac == "relu":
                    scv = self.scal(scname)
                    self.A(sc, lambda tm=tm, pb=pb, scv=scv:
                           sc.activation(tm, pb, AF.Relu, scale=scv))
                else:
                    scv = self.sct[:, sccol_base + o:sccol_base + o + 1]
                    self.A(sc, lambda tm=tm, pb=pb, scv=scv: sc.mul(tm, pb, scv))

        @b.gpsimd
        def _(gp):
            for o in range(OC):
                gp.wait_ge(self.sem["vC"], S["vC"] + (o + 1) * nEv)
                tm = self.tmp32[:, (o % 2) * TS:(o % 2 + 1) * TS]
                lr = self.ltmp[:, (o % 2) * 2 * TS:(o % 2) * 2 * TS + TS]
                le = self.ltmp[:, (o % 2) * 2 * TS + TS:(o % 2 + 1) * 2 * TS]
                if evac == "resid":
                    self.dma(gp, stsem[o % 2],
                             out_dst[o, :, ts * TS:(ts + 1) * TS], tm)
                elif evac == "relu":
                    self.dma(gp, stsem[o % 2], out_dst[0, o, :, :], lr)
                    self.dma(gp, stsem[o % 2], out_dst[1, o, :, :], le)
                else:
                    tgt, rb = out_dst
                    for l, til in ((0, lr), (1, le)):
                        if rb is None:
                            dd = tgt[l, o, :, ts * TS:(ts + 1) * TS]
                        else:
                            r0 = rb + l * (OC * 128) + o * 128
                            dd = tgt[r0:r0 + 128, ts * TS:(ts + 1) * TS]
                        self.dma(gp, stsem[o % 2], dd, til)

        @b.vector
        def _(vec):
            for o in range(OC):
                if nEa:
                    vec.wait_ge(self.sem["aC"], S["aC"] + o + 1)
                else:
                    vec.wait_ge(self.sem["tM"], S["tM"] + o + 1)
                if o >= 2:
                    vec.wait_ge(self.sem[stsem[o % 2]],
                                S[stsem[o % 2]] + 16 * nStW * (o // 2))
                else:
                    self.wait_stores(vec, S)
                tm = self.tmp32[:, (o % 2) * TS:(o % 2 + 1) * TS]
                lr = self.ltmp[:, (o % 2) * 2 * TS:(o % 2) * 2 * TS + TS]
                le = self.ltmp[:, (o % 2) * 2 * TS + TS:(o % 2 + 1) * 2 * TS]
                if evac == "resid":
                    pb = self.bank(o % 4)[:, :TS]
                    xc = self.xck[:, (o % 2) * TS:(o % 2 + 1) * TS]
                    vec.wait_ge(self.sem[xsem[o % 2]],
                                S[xsem[o % 2]] + 16 * (o // 2 + 1))
                    scv = (self.scal(scname) if sccol_base is None
                           else self.sct[:, sccol_base + o:sccol_base + o + 1])
                    self.V(vec, lambda tm=tm, pb=pb, xc=xc, scv=scv:
                           vec.scalar_tensor_tensor(tm, pb, scv, xc,
                                                    ALU.mult, ALU.add))
                else:
                    self.V(vec, lambda lr=lr, tm=tm: vec.tensor_copy(lr, tm))
                    self.V(vec, lambda le=le, tm=tm, lr=lr: vec.tensor_sub(le, tm, lr))

    def ph_qkv(self, src):
        c = self.c
        for ts in range(c.NTS):
            self.ph_ln(src, ts)
            self.ph_mm(self.wqs, c.DC, in_src="sbuf", evac="qk",
                       out_dst=(self.qLd, None), ts=ts, scale_col=("", 0))
            self.ph_mm(self.wks, c.KVC, in_src="sbuf", evac="qk",
                       out_dst=(self.ccink, 0), ts=ts, scale_col=("", 2 * c.DC))
            if ts == c.NTS - 1:
                self.ph_exchange_one(self.ccink, self.ccoutk)
            self.ph_vproj(ts)
        self.ph_exchange_one(self.ccinv, self.ccoutv)

    def ph_exchange_one(self, cin, cout):
        b = self.block
        S = dict(self.ct)

        @b.gpsimd
        def _(gp):
            self.wait_stores(gp, S)
            gp.collective_compute(
                "AllGather", ALU.bypass,
                replica_groups=[[0, 1], [2, 3], [4, 5], [6, 7]],
                ins=[cin.opt()], outs=[cout.opt()],
            ).then_inc(self.sem["CC"], 1)
            self.ct["CC"] += 1

    def ph_vproj(self, ts):
        b, c = self.block, self.c
        DC, TS, KVD = c.DC, c.TS, c.KVD
        TC = c.TC
        VT = max(KVD // 512, 1)
        NTILE = min(512, KVD)
        assert TC <= 4
        wsem = ("wA", "wB")
        stsem = ("stA", "stB")
        nkv = NTILE // 128
        nV = 3

        for vh in range(VT):
            S = dict(self.ct)

            @b.sync
            def _(sync):
                self.wait_stores(sync, S)
                for i in range(DC):
                    sync.wait_ge(self.sem["tM"],
                                 S["tM"] + (1 + i - 1 if i >= 2 else 0))
                    self.dma(sync, wsem[i % 2],
                             self.wsb[:, (i % 2) * DC * 128:
                                      (i % 2) * DC * 128 + NTILE],
                             self.wvs[i, :, vh * 512:vh * 512 + NTILE])

            @b.tensor
            def _(t):
                t.wait_ge(self.sem["vC"], S["vC"])
                t.wait_ge(self.sem["aC"], S["aC"])
                self.tinc(t.matmul(self.bank(5)[:, :NTILE], self.onesf[:],
                                   self.rows[:, 2 * c.T + vh * 512:
                                             2 * c.T + vh * 512 + NTILE],
                                   start=True, stop=True))
                for i in range(DC):
                    t.wait_ge(self.sem[wsem[i % 2]],
                              S[wsem[i % 2]] + 16 * (i // 2 + 1))
                    if i == 0:
                        t.wait_ge(self.sem["aC"], S["aC"] + 1)
                    wbuf = self.wsb[:, (i % 2) * DC * 128:
                                    (i % 2) * DC * 128 + NTILE]
                    ins = None
                    for tc in range(TC):
                        for l in range(2):
                            lim = self.inl[:, (l * DC + i) * TS + tc * 128:
                                           (l * DC + i) * TS + (tc + 1) * 128]
                            ins = t.matmul(self.bank(tc)[:, :NTILE], lim, wbuf,
                                           start=(i == 0 and l == 0),
                                           stop=(i == DC - 1 and l == 1))
                    self.tinc(ins)
            tM_acc = self.ct["tM"]

            @b.scalar
            def _(sc):
                sc.wait_ge(self.sem["tM"], S["tM"] + 1)
                sc.wait_ge(self.sem["vC"], S["vC"])
                vb = self.bcast[:, 0:NTILE]
                self.A(sc, lambda vb=vb: sc.copy(vb, self.bank(5)[:, :NTILE]))

            @b.vector
            def _(vec):
                vec.wait_ge(self.sem["tM"], tM_acc)
                for tc in range(TC):
                    bidx = tc
                    pbk = self.bank(tc)[:, :NTILE]
                    tm = self.tmp32[:, (bidx % 2) * TS:(bidx % 2) * TS + NTILE]
                    lr = self.ltmp[:, (bidx % 2) * 2 * TS:
                                   (bidx % 2) * 2 * TS + NTILE]
                    le = self.ltmp[:, ((bidx % 2) * 2 + 1) * TS:
                                   ((bidx % 2) * 2 + 1) * TS + NTILE]
                    vb = self.bcast[:, 0:NTILE]
                    if bidx >= 2:
                        vec.wait_ge(self.sem[stsem[bidx % 2]],
                                    S[stsem[bidx % 2]] +
                                    16 * 2 * nkv * (bidx // 2))
                    else:
                        self.wait_stores(vec, S)
                    self.V(vec, lambda tm=tm, pbk=pbk, vb=vb:
                           vec.tensor_mul(tm, pbk, vb))
                    self.V(vec, lambda lr=lr, tm=tm: vec.tensor_copy(lr, tm))
                    self.V(vec, lambda le=le, tm=tm, lr=lr:
                           vec.tensor_sub(le, tm, lr))

            @b.gpsimd
            def _(gp):
                for tc in range(TC):
                    bidx = tc
                    gp.wait_ge(self.sem["vC"], S["vC"] + (bidx + 1) * nV)
                    lr = self.ltmp[:, (bidx % 2) * 2 * TS:
                                   (bidx % 2) * 2 * TS + NTILE]
                    le = self.ltmp[:, ((bidx % 2) * 2 + 1) * TS:
                                   ((bidx % 2) * 2 + 1) * TS + NTILE]
                    kcg = (ts * TS) // 128 + tc
                    for j in range(nkv):
                        kvc = vh * 4 + j if NTILE >= 512 else j
                        for l, til in ((0, lr), (1, le)):
                            self.dma(gp, stsem[bidx % 2],
                                     self.ccinv[(kvc * 2 + l) * 128:
                                                (kvc * 2 + l) * 128 + 128,
                                                kcg * 128:(kcg + 1) * 128],
                                     til[:, j * 128:(j + 1) * 128])

    def att_views(self):
        c = self.c
        T, TS, KC = c.T, c.TS, c.KC
        inl = self.inl
        off = 0
        qA = inl[:, off:off + T]; off += T
        qB = inl[:, off:off + T]; off += T
        kA = inl[:, off:off + 2 * T]; off += 2 * T
        kB = inl[:, off:off + 2 * T]; off += 2 * T
        vA = inl[:, off:off + KC * 128]; off += KC * 128
        vB = inl[:, off:off + KC * 128]; off += KC * 128
        pA = inl[:, off:off + KC * TS]; off += KC * TS
        pB = inl[:, off:off + KC * TS]; off += KC * TS
        assert off <= c.INLW, (off, c.INLW)
        p0 = self.xck[:, 0:TS]
        p1 = self.xck[:, TS:2 * TS]
        return qA, qB, kA, kB, vA, vB, pA, pB, (p0, p1)

    def ph_att_head(self, h):
        b, c = self.block, self.c
        T, TS, KC, QC, KT = c.T, c.TS, c.KC, c.QC, c.KT
        S = dict(self.ct)
        qA, qB, kA, kB, vA, vB, pA, pB, ptmp = self.att_views()
        kvh = h // c.NREP
        KR = c.KVC * 128
        kcpr = T // 128

        @b.sync
        def _(sync):
            if h == 0:
                sync.wait_ge(self.sem["CC"], 2)
            sync.wait_ge(self.sem["tM"], S["tM"])
            sync.wait_ge(self.sem["vC"], S["vC"])
            self.wait_stores(sync, S)
            self.dma(sync, "blk", qA, self.qLd[0, h, :, :])
            self.dma(sync, "blk", qB, self.qLd[1, h, :, :])
            for r in range(2):
                self.dma(sync, "blk", kA[:, r * T:(r + 1) * T],
                         self.ccoutk[r * 2 * KR + kvh * 128:
                                     r * 2 * KR + (kvh + 1) * 128, :])
                self.dma(sync, "blk", kB[:, r * T:(r + 1) * T],
                         self.ccoutk[r * 2 * KR + KR + kvh * 128:
                                     r * 2 * KR + KR + (kvh + 1) * 128, :])
                for l, vt in ((0, vA), (1, vB)):
                    vr0 = r * (c.KVC * 2 * 128) + (kvh * 2 + l) * 128
                    self.dma(sync, "blk",
                             vt[:, r * kcpr * 128:(r + 1) * kcpr * 128],
                             self.ccoutv[vr0:vr0 + 128, :])
        ld_head = self.ct["blk"]

        nPQ = 2 * KT

        @b.tensor
        def _(t):
            t.wait_ge(self.sem["blk"], ld_head)
            t.wait_ge(self.sem["aC"], S["aC"])
            for qc in range(QC):
                for kt in range(KT):
                    j = qc * KT + kt
                    if j >= 4:
                        qc2, kt2 = divmod(j - 4, KT)
                        t.wait_ge(self.sem["vC"], S["vC"] + qc2 * nPQ + kt2 + 1)
                    self.tinc(t.matmul(self.bank(j % 4)[:, :TS],
                                       qA[:, qc * 128:(qc + 1) * 128],
                                       kA[:, kt * TS:(kt + 1) * TS],
                                       start=True, stop=True))

        @b.vector
        def _(vec):
            vec.wait_ge(self.sem["aC"], S["aC"])
            for qc in range(QC):
                for kt in range(KT):
                    vec.wait_ge(self.sem["tM"], S["tM"] + qc * KT + kt + 1)
                    bidx = (qc * KT + kt) % 4
                    mx = self.bcast[:, kt:kt + 1]
                    self.V(vec, lambda mx=mx, bidx=bidx: vec.reduce_max(
                        mx, self.bank(bidx)[:, :TS], axis=AX))
                mcol = self.bcast[:, 8 + qc:8 + qc + 1]
                self.V(vec, lambda mcol=mcol:
                       vec.tensor_copy(mcol, self.bcast[:, 0:1]))
                for kt in range(1, KT):
                    self.V(vec, lambda mcol=mcol, kt=kt: vec.tensor_max(
                        mcol, mcol, self.bcast[:, kt:kt + 1]))
        vC_stats = self.ct["vC"]

        @b.gpsimd
        def _(gp):
            gp.wait_ge(self.sem["vC"], vC_stats)
            for qc in range(QC):
                self.dma(gp, "stO", self.mrow_d[h, qc].unsqueeze(1),
                         self.bcast[:, 8 + qc:8 + qc + 1])
        stO_mst = self.ct["stO"]

        @b.sync
        def _(sync):
            sync.wait_ge(self.sem["stO"], stO_mst)
            self.dma(sync, "blk", self.rows[:, 0:T],
                     self.mrow_d[h].rearrange("a b -> (a b)").unsqueeze(0))
        mrow_ld = self.ct["blk"]

        @b.vector
        def _(vec):
            vec.wait_ge(self.sem["blk"], mrow_ld)
            self.V(vec, lambda: vec.tensor_copy(self.rowsR[:], self.rows[:, 0:T]))
        vC_mrow = self.ct["vC"]

        for qt in range(c.NTS):
            self.att_qt(h, qt, qA, qB, kA, kB, vA, vB, pA, pB, ptmp, vC_mrow)

    def att_qt(self, h, qt, qA, qB, kA, kB, vA, vB, pA, pB, ptmp, vC_mrow):
        b, c = self.block, self.c
        T, TS, KC = c.T, c.TS, c.KC
        S = dict(self.ct)
        nEvK = 2

        @b.tensor
        def _(t):
            t.wait_ge(self.sem["vC"], max(S["vC"], vC_mrow))
            for kc in range(KC):
                if kc >= 4:
                    t.wait_ge(self.sem["vC"], S["vC"] + (kc - 3) * nEvK)
                pbk = self.bank(kc % 4)[:, :TS]
                q0 = qA[:, qt * TS:(qt + 1) * TS]
                q1 = qB[:, qt * TS:(qt + 1) * TS]
                kk0 = kA[:, kc * 128:(kc + 1) * 128]
                kk1 = kB[:, kc * 128:(kc + 1) * 128]
                t.matmul(pbk, kk0, q0, start=True, stop=False)
                t.matmul(pbk, kk1, q0, start=False, stop=False)
                t.matmul(pbk, kk0, q1, start=False, stop=False)
                self.tinc(t.matmul(pbk, self.negrow[:],
                                   self.rowsR[:, qt * TS:(qt + 1) * TS],
                                   start=False, stop=True))
            t.wait_ge(self.sem["vC"], S["vC"] + KC * nEvK)
            psz = self.bank(5)[0:1, :TS]
            n = 2 * KC
            k = 0
            ins = None
            for kc in range(KC):
                for pl in (pA, pB):
                    ins = t.matmul(psz, self.onescr[:],
                                   pl[:, kc * TS:(kc + 1) * TS],
                                   start=(k == 0), stop=(k == n - 1))
                    k += 1
            self.tinc(ins)
            t.wait_ge(self.sem["vC"], S["vC"] + KC * nEvK + 1)
            self.tinc(t.matmul(self.bank(6)[:, :TS], self.onesf[:],
                               self.rows[:, T:T + TS], start=True, stop=True))
            psav = self.bank(7)[:, :TS]
            n = 3 * KC
            k = 0
            for kc in range(KC):
                for (vt, pl) in ((vA, pA), (vA, pB), (vB, pA)):
                    ins = t.matmul(psav, vt[:, kc * 128:(kc + 1) * 128],
                                   pl[:, kc * TS:(kc + 1) * TS],
                                   start=(k == 0), stop=(k == n - 1))
                    k += 1
            self.tinc(ins)

        @b.scalar
        def _(sc):
            for kc in range(KC):
                sc.wait_ge(self.sem["tM"], S["tM"] + kc + 1)
                if kc >= 2:
                    sc.wait_ge(self.sem["vC"], S["vC"] + (kc - 1) * nEvK)
                else:
                    sc.wait_ge(self.sem["vC"], S["vC"])
                pt = ptmp[kc % 2]
                pbk = self.bank(kc % 4)[:, :TS]
                self.A(sc, lambda pt=pt, pbk=pbk:
                       sc.activation(pt, pbk, AF.Exp))
            sc.wait_ge(self.sem["tM"], S["tM"] + KC + 2)
            sc.wait_ge(self.sem["vC"], S["vC"] + KC * nEvK + 1)
            self.A(sc, lambda: sc.copy(self.bcast[:, TS:2 * TS],
                                       self.bank(6)[:, :TS]))

        @b.vector
        def _(vec):
            for kc in range(KC):
                vec.wait_ge(self.sem["aC"], S["aC"] + kc + 1)
                pt = ptmp[kc % 2]
                pr = pA[:, kc * TS:(kc + 1) * TS]
                pe = pB[:, kc * TS:(kc + 1) * TS]
                self.V(vec, lambda pr=pr, pt=pt: vec.tensor_copy(pr, pt))
                self.V(vec, lambda pe=pe, pt=pt, pr=pr: vec.tensor_sub(pe, pt, pr))
            vec.wait_ge(self.sem["tM"], S["tM"] + KC + 1)
            rz = self.rows[:, T:T + TS]
            self.V(vec, lambda rz=rz: vec.reciprocal(rz, self.bank(5)[0:1, :TS]))
            vec.wait_ge(self.sem["tM"], S["tM"] + KC + 3)
            vec.wait_ge(self.sem["aC"], S["aC"] + KC + 1)
            self.wait_stores(vec, S)
            tm = self.tmp32[:, 0:TS]
            lr = self.ltmp[:, 0:TS]
            le = self.ltmp[:, TS:2 * TS]
            self.V(vec, lambda tm=tm: vec.tensor_mul(
                tm, self.bank(7)[:, :TS], self.bcast[:, TS:2 * TS]))
            self.V(vec, lambda lr=lr, tm=tm: vec.tensor_copy(lr, tm))
            self.V(vec, lambda le=le, tm=tm, lr=lr: vec.tensor_sub(le, tm, lr))

        @b.gpsimd
        def _(gp):
            gp.wait_ge(self.sem["vC"], self.ct["vC"])
            lr = self.ltmp[:, 0:TS]
            le = self.ltmp[:, TS:2 * TS]
            self.dma(gp, "stO", self.attnL[0, h, :, qt * TS:(qt + 1) * TS], lr)
            self.dma(gp, "stO", self.attnL[1, h, :, qt * TS:(qt + 1) * TS], le)

    def ph_end(self):
        b = self.block
        S = dict(self.ct)

        @b.gpsimd
        def _(gp):
            for s in ALL_SEMS:
                if S[s] > 0:
                    gp.wait_ge(self.sem[s], S[s])


def make_inputs_per_core(cfg, core, full):
    c = cfg
    seq = core // 2
    half = core % 2
    x = np.asarray(full["L0_attn_hidden_states"])
    xs = np.ascontiguousarray(
        x[seq, half * c.T:(half + 1) * c.T, :].T).reshape(c.DC, 128, c.T)

    def u8(a):
        a = np.asarray(a)
        return a.view(np.uint8) if a.dtype == np.bool_ else a.astype(np.uint8)

    cei = np.ones(c.D, dtype=np.float32)
    cei[c.NE:] = np.float32(c.EI_NEG)
    return {
        "x0": xs.astype(np.float32),
        "qsc_in": np.asarray(full["L0_attn_q_scale"], dtype=np.float32),
        "ksc_in": np.asarray(full["L0_attn_k_scale"], dtype=np.float32),
        "vsc_in": np.asarray(full["L0_attn_v_scale"], dtype=np.float32),
        "osc_in": np.asarray(full["L0_attn_o_scale"], dtype=np.float32),
        "wqT": np.ascontiguousarray(np.asarray(full["L0_attn_q_weight"], dtype=np.float32).T),
        "wkT": np.ascontiguousarray(np.asarray(full["L0_attn_k_weight"], dtype=np.float32).T),
        "wvT": np.ascontiguousarray(np.asarray(full["L0_attn_v_weight"], dtype=np.float32).T),
        "woT": np.ascontiguousarray(np.asarray(full["L0_attn_o_weight"], dtype=np.float32).T),
        "k0u": u8(full["L0_ffn_k0"]),
        "k1u": u8(full["L0_ffn_k1"]),
        "k2u": u8(full["L0_ffn_k2"]),
        "s0": np.asarray(full["L0_ffn_s0"], dtype=np.float32).reshape(1, 1),
        "s1": np.asarray(full["L0_ffn_s1"], dtype=np.float32).reshape(1, 1),
        "s2": np.asarray(full["L0_ffn_s2"], dtype=np.float32).reshape(1, 1),
        "cei": np.ascontiguousarray(cei.reshape(c.DC, 128).T),
        "cei2": np.ascontiguousarray((2.0 * cei).reshape(c.DC, 128).T),
    }


def build_program(cfg, name="mkmac1", phase_limit=None):
    nc = bass.Bass("TRN2", target_bir_lowering=False, name=name)
    Build(nc, cfg, phase_limit=phase_limit).build()
    return nc


def kernel(**inputs):
    cfg = CFG()
    nc = build_program(cfg, name="mkmacaron_v2")
    in_maps = [make_inputs_per_core(cfg, core, inputs) for core in range(8)]
    res = run_bass_kernel_spmd(nc, in_maps, core_ids=list(range(8)))
    out = np.empty((4, 2048, 4096), dtype=np.float32)
    for core in range(8):
        seq, half = core // 2, core % 2
        y = res.results[core]["yout"].reshape(cfg.D, cfg.T)
        out[seq, half * cfg.T:(half + 1) * cfg.T, :] = y.T
    return out


# version 14
# speedup vs baseline: 1.4333x; 1.0563x over previous
import numpy as np
import concourse.bass as bass
import concourse.mybir as mybir
from concourse.bass_utils import run_bass_kernel_spmd
from contextlib import ExitStack

F32 = mybir.dt.float32
F32R = mybir.dt.float32r
U8 = mybir.dt.uint8
AF = mybir.ActivationFunctionType
ALU = mybir.AluOpType
AX = mybir.AxisListType.X

LOAD_SEMS = ["blk", "wA", "wB", "xA", "xB", "kA", "kB"]
STORE_SEMS = ["stA", "stB", "stC", "stD", "stO"]
ALL_SEMS = LOAD_SEMS + STORE_SEMS + ["tM", "vC", "aC", "gC", "CC"]


class CFG:
    def __init__(self, D=4096, T=1024, HD=128, KVH=8, P_EI=0.8):
        self.D = D
        self.T = T
        self.HD = HD
        self.DC = D // 128
        self.HEADS = D // HD
        self.KVH = KVH
        self.KVD = KVH * HD
        self.KVC = self.KVD // 128
        self.NREP = self.HEADS // self.KVH
        self.TS = T // 2
        self.NTS = 2
        self.TC = self.TS // 128
        self.SEQK = 2 * T
        self.KC = self.SEQK // 128
        self.QC = T // 128
        self.KT = self.SEQK // self.TS
        self.NE = round(D * P_EI)
        self.EI_NEG = -P_EI / (1.0 - P_EI)
        att_need = (4 * self.T + 8 * self.T + 2 * self.KC * 128 +
                    2 * self.KC * self.TS)
        self.INLW = max(2 * self.DC * self.TS, att_need, 6 * D)
        assert self.TS <= 512 and HD == 128
        assert self.DC % 4 == 0


class Build:
    def __init__(self, nc, cfg, phase_limit=None):
        self.nc = nc
        self.c = cfg
        self.phase_limit = phase_limit
        self.ct = {s: 0 for s in ALL_SEMS}
        self.qk_last_tM = {}
        self.av_tM = {}
        self.mrow_copy_vC = {}
        self.stats_end = None

    def V(self, vec, make):
        vec.wait_ge(self.sem["vC"], self.ct["vC"])
        make().then_inc(self.sem["vC"], 1)
        self.ct["vC"] += 1

    def A(self, sc, make):
        sc.wait_ge(self.sem["aC"], self.ct["aC"])
        make().then_inc(self.sem["aC"], 1)
        self.ct["aC"] += 1

    def G(self, gp, make):
        gp.wait_ge(self.sem["gC"], self.ct["gC"])
        make().then_inc(self.sem["gC"], 1)
        self.ct["gC"] += 1

    def tinc(self, ins):
        ins.then_inc(self.sem["tM"], 1)
        self.ct["tM"] += 1

    def dma(self, eng, stream, out, in_, **kw):
        eng.dma_start(out=out, in_=in_, **kw).then_inc(self.sem[stream], 16)
        self.ct[stream] += 16

    def wait_stores(self, eng, S):
        for s in STORE_SEMS:
            if S[s] > 0:
                eng.wait_ge(self.sem[s], S[s])

    def build(self):
        nc, c = self.nc, self.c
        D, T, TS, DC = c.D, c.T, c.TS, c.DC
        dt = nc.dram_tensor
        self.x0 = dt("x0", [DC, 128, T], F32, kind="ExternalInput").ap()
        self.qsc_in = dt("qsc_in", [D], F32, kind="ExternalInput").ap()
        self.ksc_in = dt("ksc_in", [c.KVD], F32, kind="ExternalInput").ap()
        self.vsc_in = dt("vsc_in", [c.KVD], F32, kind="ExternalInput").ap()
        self.osc_in = dt("osc_in", [D], F32, kind="ExternalInput").ap()
        self.k0u = dt("k0u", [D, D], U8, kind="ExternalInput").ap()
        self.k1u = dt("k1u", [D, D], U8, kind="ExternalInput").ap()
        self.k2u = dt("k2u", [D, D], U8, kind="ExternalInput").ap()
        self.s0 = dt("s0", [1, 1], F32, kind="ExternalInput").ap()
        self.s1 = dt("s1", [1, 1], F32, kind="ExternalInput").ap()
        self.s2 = dt("s2", [1, 1], F32, kind="ExternalInput").ap()
        self.wqT = dt("wqT", [D, D], F32, kind="ExternalInput").ap()
        self.wkT = dt("wkT", [D, c.KVD], F32, kind="ExternalInput").ap()
        self.wvT = dt("wvT", [D, c.KVD], F32, kind="ExternalInput").ap()
        self.woT = dt("woT", [D, D], F32, kind="ExternalInput").ap()
        self.cei = dt("cei", [128, DC], F32, kind="ExternalInput").ap()
        self.cei2 = dt("cei2", [128, DC], F32, kind="ExternalInput").ap()
        self.yout = dt("yout", [DC, 128, T], F32, kind="ExternalOutput").ap()

        self.xcur = dt("xcur", [DC, 128, T], F32).ap()
        self.w0s = dt("w0s", [DC, 128, DC * 128], F32R).ap()
        self.w1s = dt("w1s", [DC, 128, DC * 128], F32R).ap()
        self.w2s = dt("w2s", [DC, 128, DC * 128], F32R).ap()
        self.wqs = dt("wqs", [DC, 128, DC * 128], F32R).ap()
        self.wks = dt("wks", [c.KVC, 128, DC * 128], F32R).ap()
        self.wos = dt("wos", [DC, 128, DC * 128], F32R).ap()
        self.wvs = dt("wvs", [DC, 128, c.KVD], F32R).ap()
        self.qLd = dt("qLd", [2, DC, 128, T], F32R).ap()
        self.hLd = dt("hLd", [2, DC, 128, TS], F32R).ap()
        self.ccink = dt("ccink", [2 * c.KVC * 128, T], F32R).ap()
        self.ccoutk = dt("ccoutk", [2 * 2 * c.KVC * 128, T], F32R).ap()
        self.ccinv = dt("ccinv", [c.KVC * 2 * 128, T], F32R).ap()
        self.ccoutv = dt("ccoutv", [2 * c.KVC * 2 * 128, T], F32R).ap()
        self.attnL = dt("attnL", [2, DC, 128, T], F32R).ap()
        self.mrow_d = dt("mrow_d", [c.HEADS, c.QC, 128], F32).ap()
        self.srow_d = dt("srow_d", [D], F32).ap()

        es = ExitStack()
        sb, ps = nc.sbuf_tensor, nc.psum_tensor
        E = es.enter_context
        self.inl = E(sb("inl", [128, c.INLW], F32R))
        self.wsb = E(sb("wsb", [128, 2 * DC * 128], F32R))
        self.tmp32 = E(sb("tmp32", [128, 2 * TS], F32))
        self.ltmp = E(sb("ltmp", [128, 4 * TS], F32R))
        self.xck = E(sb("xck", [128, 2 * TS], F32))
        self.bcast = E(sb("bcast", [128, 2 * TS], F32))
        self.rows = E(sb("rows", [1, max(3 * T, 1024)], F32))
        self.crow = E(sb("crow", [1, 1024], F32))
        self.sct = E(sb("sct", [128, 2 * DC + c.KVC + 8], F32))
        self.ceis = E(sb("ceis", [128, 2 * DC], F32))
        self.onesf = E(sb("onesf", [1, 128], F32))
        self.onescf = E(sb("onescf", [128, 1], F32))
        self.onescr = E(sb("onescr", [128, 1], F32R))
        self.negrow = E(sb("negrow", [1, 128], F32R))
        self.rowsR = E(sb("rowsR", [1, c.T], F32R))
        self.psb = E(ps("psb", [128, 4096], F32))

        self.sem = {}
        for s in ALL_SEMS:
            self.sem[s] = E(nc.semaphore(s))
        self.block = E(nc.Block())
        self.emit_all(phase_limit=self.phase_limit)
        es.close()
        return nc

    def bank(self, i):
        return self.psb[:, i * 512:(i + 1) * 512]

    def scal(self, name):
        c = self.c
        base = 2 * c.DC + c.KVC
        idx = {"s0": base, "s1": base + 1, "s2": base + 2,
               "rs0": base + 3, "rs1": base + 4, "hs2": base + 5}[name]
        return self.sct[:, idx:idx + 1]

    def emit_all(self, phase_limit=None):
        c = self.c
        phases = [self.ph_consts,
                  lambda: self.ph_conv_ffn(self.k0u, self.w0s, use_c=False),
                  lambda: self.ph_conv_ffn(self.k1u, self.w1s, use_c=True),
                  lambda: self.ph_conv_ffn(self.k2u, self.w2s, use_c=True),
                  self.ph_conv_att_q,
                  self.ph_conv_att_k,
                  self.ph_conv_att_v,
                  self.ph_conv_att_o,
                  lambda: self.ph_ffn(src=self.x0, dst=self.xcur),
                  lambda: self.ph_qkv(src=self.xcur)]
        phases += [self.ph_att_stats]
        phases += [(lambda h=h: self.ph_att_head(h)) for h in range(c.HEADS)]
        phases += [(lambda ts=ts: self.ph_mm(
            self.wos, c.DC, in_src="attnL", evac="resid",
            resid_src=self.xcur, out_dst=self.xcur, ts=ts,
            scale_col=(None, c.DC))) for ts in range(c.NTS)]
        phases += [lambda: self.ph_ffn(src=self.xcur, dst=self.yout)]
        if phase_limit is None:
            phase_limit = len(phases)
        for ph in phases[:phase_limit]:
            ph()
        self.ph_end()

    def ph_consts(self):
        b, c = self.block, self.c

        @b.sync
        def _(sync):
            self.dma(sync, "blk", self.ceis[:, 0:c.DC], self.cei)
            self.dma(sync, "blk", self.ceis[:, c.DC:2 * c.DC], self.cei2)
            base = 2 * c.DC + c.KVC
            for j, s in enumerate([self.s0, self.s1, self.s2]):
                self.dma(sync, "blk", self.sct[:, base + j:base + j + 1],
                         bass.AP(s.tensor, 0, [[0, 128], [1, 1]]),
                         allow_slow_non_contiguous=True)
        blk_done = self.ct["blk"]

        @b.vector
        def _(vec):
            vec.wait_ge(self.sem["blk"], blk_done)
            self.V(vec, lambda: vec.memset(self.onesf[:], 1.0))
            self.V(vec, lambda: vec.memset(self.onescf[:], 1.0))
            self.V(vec, lambda: vec.tensor_copy(self.onescr[:], self.onescf[:]))
            self.V(vec, lambda: vec.tensor_scalar_mul(self.negrow[:], self.onesf[:], -1.0))
            base = 2 * c.DC + c.KVC
            self.V(vec, lambda: vec.tensor_scalar_mul(
                self.sct[:, base + 3:base + 4], self.sct[:, base:base + 1], 2.0))
            self.V(vec, lambda: vec.tensor_scalar_mul(
                self.sct[:, base + 4:base + 5], self.sct[:, base + 1:base + 2], 2.0))
            self.V(vec, lambda: vec.tensor_scalar_mul(
                self.sct[:, base + 5:base + 6], self.sct[:, base + 2:base + 3], 0.5))

    def ph_conv_ffn(self, ksrc, wdst, use_c):
        b, c = self.block, self.c
        DC = c.DC
        kslot = [self.xck[:, :].bitcast(U8), self.bcast[:, :].bitcast(U8)]
        S = dict(self.ct)
        nEv = 1
        GRP = 4
        nStW = DC // GRP
        ksem = ("kA", "kB")
        stsem = ("stA", "stB")

        @b.sync
        def _(sync):
            for i in range(DC):
                sync.wait_ge(self.sem["vC"],
                             S["vC"] + ((i - 1) * nEv if i >= 2 else 0))
                self.dma(sync, ksem[i % 2], kslot[i % 2][:, 0:c.D],
                         ksrc[i * 128:(i + 1) * 128, :])

        @b.vector
        def _(vec):
            for i in range(DC):
                vec.wait_ge(self.sem[ksem[i % 2]],
                            S[ksem[i % 2]] + 16 * (i // 2 + 1))
                if i >= 2:
                    vec.wait_ge(self.sem[stsem[i % 2]],
                                S[stsem[i % 2]] + 16 * nStW * (i // 2))
                else:
                    self.wait_stores(vec, S)
                    vec.wait_ge(self.sem["tM"], S["tM"])
                    vec.wait_ge(self.sem["aC"], S["aC"])
                sgn = self.inl[:, (i % 2) * c.D:(i % 2 + 1) * c.D]
                ksl = kslot[i % 2][:, 0:c.D]
                if use_c:
                    cc = self.ceis[:, i:i + 1]
                    c2 = self.ceis[:, DC + i:DC + i + 1]
                    self.V(vec, lambda sgn=sgn, ksl=ksl, cc=cc, c2=c2:
                           vec.tensor_scalar(sgn, ksl, c2, cc, ALU.mult, ALU.subtract))
                else:
                    self.V(vec, lambda sgn=sgn, ksl=ksl:
                           vec.tensor_scalar(sgn, ksl, 2.0, 1.0, ALU.mult, ALU.subtract))

        @b.gpsimd
        def _(gp):
            for i in range(DC):
                gp.wait_ge(self.sem["vC"], S["vC"] + (i + 1) * nEv)
                sgn = self.inl[:, (i % 2) * c.D:(i % 2 + 1) * c.D]
                for g in range(DC // GRP):
                    oc0 = g * GRP
                    self.dma(gp, stsem[i % 2],
                             wdst[oc0:oc0 + GRP, :, i * 128:(i + 1) * 128]
                             .transpose([1, 0, 2]),
                             sgn[:, oc0 * 128:(oc0 + GRP) * 128]
                             .rearrange("p (g o) -> p g o", o=128))

    def ph_conv_att_q(self):
        c = self.c
        rsq = float(np.float32(1.0) / np.float32(np.sqrt(np.float64(c.HD))))
        self.ph_sign(self.wqT, self.wqs, c.DC, whole=False,
                     scin=self.qsc_in, sccol=0, xsc=rsq)

    def ph_conv_att_k(self):
        c = self.c
        self.ph_sign(self.wkT, self.wks, c.KVC, whole=False,
                     scin=self.ksc_in, sccol=2 * c.DC, xsc=1.0)

    def ph_conv_att_v(self):
        c = self.c
        self.ph_sign(self.wvT, self.wvs, c.KVC, whole=True,
                     scin=self.vsc_in, sccol=-1, xsc=1.0)

    def ph_conv_att_o(self):
        c = self.c
        self.ph_sign(self.woT, self.wos, c.DC, whole=False,
                     scin=self.osc_in, sccol=c.DC, xsc=1.0)

    def ph_sign(self, srcT, wdst, OC, whole, scin, sccol, xsc):
        b, c = self.block, self.c
        DC = c.DC
        S = dict(self.ct)
        W = OC * 128
        nEa = 2
        GRP = 4 if OC % 4 == 0 else 1
        nStW = 1 if whole else OC // GRP
        NSG = min(512, W)
        nseg = W // NSG
        nR = 2
        wsem = ("wA", "wB")
        xsem = ("xA", "xB")
        stsem = ("stA", "stB")

        @b.sync
        def _(sync):
            for i in range(DC):
                if i >= 2:
                    sync.wait_ge(self.sem["aC"], S["aC"] + (i - 1) * nEa)
                else:
                    sync.wait_ge(self.sem["aC"], S["aC"])
                    sync.wait_ge(self.sem["vC"], S["vC"])
                    sync.wait_ge(self.sem["tM"], S["tM"])
                    self.wait_stores(sync, S)
                slab = self.inl[:, 2 * c.D + (i % 2) * c.D:
                                2 * c.D + (i % 2) * c.D + W]
                self.dma(sync, wsem[i % 2], slab,
                         srcT[i * 128:(i + 1) * 128, :].bitcast(F32R))
            for sg in range(nseg):
                sync.wait_ge(self.sem["vC"],
                             S["vC"] + max(sg - 1, 0) * nR)
                self.dma(sync, xsem[sg % 2],
                         self.rows[:, (sg % 2) * NSG:(sg % 2) * NSG + NSG],
                         bass.AP(scin.tensor, sg * NSG, [[NSG, 1], [1, NSG]]))

        @b.scalar
        def _(sc):
            for i in range(DC):
                sc.wait_ge(self.sem[wsem[i % 2]],
                           S[wsem[i % 2]] + 16 * (i // 2 + 1))
                if i >= 2:
                    sc.wait_ge(self.sem[stsem[i % 2]],
                               S[stsem[i % 2]] + 16 * nStW * (i // 2))
                    sc.wait_ge(self.sem["tM"], S["tM"] + i - 1)
                else:
                    self.wait_stores(sc, S)
                wslab = self.inl[:, 2 * c.D + (i % 2) * c.D:
                                 2 * c.D + (i % 2) * c.D + W].bitcast(F32)
                sgn = self.inl[:, (i % 2) * c.D:(i % 2) * c.D + W]
                absl = self.inl[:, 4 * c.D + (i % 2) * c.D:
                                4 * c.D + (i % 2) * c.D + W]
                self.A(sc, lambda sgn=sgn, w=wslab: sc.activation(sgn, w, AF.Sign))
                self.A(sc, lambda absl=absl, w=wslab:
                       sc.activation(absl, w, AF.Abs))

        @b.tensor
        def _(t):
            t.wait_ge(self.sem["vC"], S["vC"])
            t.wait_ge(self.sem["aC"], S["aC"])
            for i in range(DC):
                t.wait_ge(self.sem["aC"], S["aC"] + i * nEa + 2)
                absl = self.inl[:, 4 * c.D + (i % 2) * c.D:
                                4 * c.D + (i % 2) * c.D + W]
                ins = None
                for sg in range(nseg):
                    ins = t.matmul(self.bank(sg % 8)[0:1, :NSG], self.onescr[:],
                                   absl[:, sg * NSG:(sg + 1) * NSG],
                                   start=(i == 0), stop=(i == DC - 1))
                self.tinc(ins)

        @b.vector
        def _(vec):
            vec.wait_ge(self.sem["tM"], S["tM"] + DC)
            for sg in range(nseg):
                vec.wait_ge(self.sem[xsem[sg % 2]],
                            S[xsem[sg % 2]] + 16 * (sg // 2 + 1))
                seg = self.crow[:, (sg % 2) * 512:(sg % 2) * 512 + NSG]
                qseg = self.rows[:, (sg % 2) * NSG:(sg % 2) * NSG + NSG]
                if sg >= 2:
                    ssm = ("stC", "stD")[sg % 2]
                    vec.wait_ge(self.sem[ssm], S[ssm] + 16 * (sg // 2))
                self.V(vec, lambda seg=seg, sg=sg: vec.tensor_scalar(
                    seg, self.bank(sg % 8)[0:1, :NSG], 1.0 / c.D, 1e-8,
                    ALU.mult, ALU.max))
                if whole:
                    out = self.rows[:, 2 * c.T + sg * NSG:2 * c.T + (sg + 1) * NSG]
                else:
                    out = seg
                self.V(vec, lambda out=out, seg=seg, qseg=qseg:
                       vec.scalar_tensor_tensor(out, seg, xsc, qseg,
                                                ALU.mult, ALU.mult))

        @b.gpsimd
        def _(gp):
            for i in range(DC):
                gp.wait_ge(self.sem["aC"], S["aC"] + i * nEa + 1)
                sgn = self.inl[:, (i % 2) * c.D:(i % 2) * c.D + W]
                if whole:
                    self.dma(gp, stsem[i % 2], wdst[i, :, :], sgn)
                else:
                    for g in range(OC // GRP):
                        oc0 = g * GRP
                        if GRP == 1:
                            self.dma(gp, stsem[i % 2],
                                     wdst[oc0, :, i * 128:(i + 1) * 128],
                                     sgn[:, oc0 * 128:(oc0 + 1) * 128])
                        else:
                            self.dma(gp, stsem[i % 2],
                                     wdst[oc0:oc0 + GRP, :, i * 128:(i + 1) * 128]
                                     .transpose([1, 0, 2]),
                                     sgn[:, oc0 * 128:(oc0 + GRP) * 128]
                                     .rearrange("p (g o) -> p g o", o=128))
            if not whole:
                gp.wait_ge(self.sem["blk"], S["blk"])
                for sg in range(nseg):
                    gp.wait_ge(self.sem["vC"], S["vC"] + (sg + 1) * nR)
                    seg = self.crow[:, (sg % 2) * 512:(sg % 2) * 512 + NSG]
                    self.dma(gp, ("stC", "stD")[sg % 2],
                             bass.AP(self.srow_d.tensor, sg * NSG,
                                     [[NSG, 1], [1, NSG]]), seg)
        stC_seg = self.ct["stC"]
        stD_seg = self.ct["stD"]

        if not whole:
            @b.sync
            def _(sync):
                sync.wait_ge(self.sem["stC"], stC_seg)
                sync.wait_ge(self.sem["stD"], stD_seg)
                for oc in range(OC):
                    self.dma(sync, "blk", self.sct[:, sccol + oc:sccol + oc + 1],
                             bass.AP(self.srow_d.tensor, oc * 128,
                                     [[1, 128], [1, 1]]),
                             allow_slow_non_contiguous=True)

    def ph_ffn(self, src, dst):
        c = self.c
        for ts in range(c.NTS):
            self.ph_ln(src, ts)
            self.ph_mm(self.w0s, c.DC, in_src="sbuf", evac="relu",
                       out_dst=self.hLd, ts=ts, scale_col=("rs0", None))
            self.ph_mm(self.w1s, c.DC, in_src="hLd", evac="relu",
                       out_dst=self.hLd, ts=ts, scale_col=("rs1", None))
            self.ph_mm(self.w2s, c.DC, in_src="hLd", evac="resid",
                       resid_src=src, out_dst=dst, ts=ts, scale_col=("hs2", None))

    def ph_ln(self, src, ts):
        b, c = self.block, self.c
        DC, TS = c.DC, c.TS
        S = dict(self.ct)
        nV1, nV2, nR = 5, 4, 6
        xsem = ("xA", "xB")

        @b.sync
        def _(sync):
            self.wait_stores(sync, S)
            for i in range(DC):
                sync.wait_ge(self.sem["vC"],
                             S["vC"] + ((i - 1) * nV1 if i >= 2 else 0))
                self.dma(sync, xsem[i % 2],
                         self.xck[:, (i % 2) * TS:(i % 2 + 1) * TS],
                         src[i, :, ts * TS:(ts + 1) * TS])
            for i in range(DC):
                sync.wait_ge(self.sem["vC"], S["vC"] + DC * nV1 + nR +
                             max(i - 1, 0) * nV2)
                self.dma(sync, xsem[i % 2],
                         self.xck[:, (i % 2) * TS:(i % 2 + 1) * TS],
                         src[i, :, ts * TS:(ts + 1) * TS])

        @b.tensor
        def _(t):
            t.wait_ge(self.sem["aC"], S["aC"])
            for i in range(DC):
                t.wait_ge(self.sem["vC"], S["vC"] + i * nV1 + 2)
                A = self.ltmp[:, (i % 2) * 2 * TS:(i % 2) * 2 * TS + TS]
                B = self.ltmp[:, (i % 2) * 2 * TS + TS:(i % 2 + 1) * 2 * TS]
                psx = self.bank(4)[0:1, :TS]
                psq = self.bank(5)[0:1, :TS]
                t.matmul(psx, self.onescr[:], A, start=(i == 0), stop=False)
                self.tinc(t.matmul(psx, self.onescr[:], B, start=False,
                                   stop=(i == DC - 1)))
                t.wait_ge(self.sem["vC"], S["vC"] + i * nV1 + 5)
                t.matmul(psq, self.onescr[:], A, start=(i == 0), stop=False)
                self.tinc(t.matmul(psq, self.onescr[:], B, start=False,
                                   stop=(i == DC - 1)))
            t.wait_ge(self.sem["vC"], S["vC"] + DC * nV1 + nR)
            self.tinc(t.matmul(self.bank(2)[:, :TS], self.onesf[:],
                               self.rows[:, 0:TS], start=True, stop=True))
            self.tinc(t.matmul(self.bank(3)[:, :TS], self.onesf[:],
                               self.rows[:, TS:2 * TS], start=True, stop=True))
        tM_bc = self.ct["tM"]

        @b.vector
        def _(vec):
            for i in range(DC):
                vec.wait_ge(self.sem[xsem[i % 2]], S[xsem[i % 2]] + 16 * (i // 2 + 1))
                if i >= 2:
                    vec.wait_ge(self.sem["tM"], S["tM"] + (i - 1) * 2)
                else:
                    vec.wait_ge(self.sem["tM"], S["tM"])
                    self.wait_stores(vec, S)
                xc = self.xck[:, (i % 2) * TS:(i % 2 + 1) * TS]
                A = self.ltmp[:, (i % 2) * 2 * TS:(i % 2) * 2 * TS + TS]
                B = self.ltmp[:, (i % 2) * 2 * TS + TS:(i % 2 + 1) * 2 * TS]
                tm = self.tmp32[:, (i % 2) * TS:(i % 2 + 1) * TS]
                self.V(vec, lambda A=A, xc=xc: vec.tensor_copy(A, xc))
                self.V(vec, lambda B=B, xc=xc, A=A: vec.tensor_sub(B, xc, A))
                self.V(vec, lambda tm=tm, xc=xc: vec.tensor_mul(tm, xc, xc))
                vec.wait_ge(self.sem["tM"], S["tM"] + i * 2 + 1)
                self.V(vec, lambda A=A, tm=tm: vec.tensor_copy(A, tm))
                self.V(vec, lambda B=B, tm=tm, A=A: vec.tensor_sub(B, tm, A))
            vec.wait_ge(self.sem["tM"], S["tM"] + DC * 2)
            m = self.rows[:, 0:TS]
            r = self.rows[:, TS:2 * TS]
            v2 = self.rows[:, 2 * TS:3 * TS]
            psx = self.bank(4)[0:1, :TS]
            psq = self.bank(5)[0:1, :TS]
            self.V(vec, lambda: vec.tensor_scalar_mul(m, psx, 1.0 / c.D))
            self.V(vec, lambda: vec.tensor_scalar_mul(v2, psq, 1.0 / c.D))
            self.V(vec, lambda: vec.tensor_mul(r, m, m))
            self.V(vec, lambda: vec.tensor_sub(v2, v2, r))
            self.V(vec, lambda: vec.tensor_scalar_add(v2, v2, 1e-5))
            vec.wait_ge(self.sem["aC"], S["aC"] + 1)
            self.V(vec, lambda: vec.reciprocal(r, v2))
            for i in range(DC):
                vec.wait_ge(self.sem[xsem[i % 2]],
                            S[xsem[i % 2]] + 16 * (DC // 2 + i // 2 + 1))
                vec.wait_ge(self.sem["tM"], tM_bc)
                vec.wait_ge(self.sem["aC"], S["aC"] + 3)
                xc = self.xck[:, (i % 2) * TS:(i % 2 + 1) * TS]
                tm = self.tmp32[:, (i % 2) * TS:(i % 2 + 1) * TS]
                ur = self.inl[:, i * TS:(i + 1) * TS]
                ue = self.inl[:, (DC + i) * TS:(DC + i + 1) * TS]
                mb = self.bcast[:, 0:TS]
                rb = self.bcast[:, TS:2 * TS]
                self.V(vec, lambda tm=tm, xc=xc, mb=mb: vec.tensor_sub(tm, xc, mb))
                self.V(vec, lambda tm=tm, rb=rb: vec.tensor_mul(tm, tm, rb))
                self.V(vec, lambda ur=ur, tm=tm: vec.tensor_copy(ur, tm))
                self.V(vec, lambda ue=ue, tm=tm, ur=ur: vec.tensor_sub(ue, tm, ur))

        @b.scalar
        def _(sc):
            sc.wait_ge(self.sem["vC"], S["vC"] + DC * nV1 + 5)
            v2 = self.rows[:, 2 * TS:3 * TS]
            self.A(sc, lambda: sc.activation(v2, v2, AF.Sqrt))
            sc.wait_ge(self.sem["tM"], tM_bc)
            sc.wait_ge(self.sem["vC"], S["vC"] + DC * nV1 + nR)
            self.A(sc, lambda: sc.copy(self.bcast[:, 0:TS], self.bank(2)[:, :TS]))
            self.A(sc, lambda: sc.copy(self.bcast[:, TS:2 * TS], self.bank(3)[:, :TS]))

    def ph_mm(self, wscr, OC, in_src, evac, out_dst, ts, scale_col,
             resid_src=None):
        b, c = self.block, self.c
        DC, TS = c.DC, c.TS
        S = dict(self.ct)
        load_limbs = in_src in ("hLd", "attnL")
        n_in = 2 * DC if load_limbs else 0
        nEv = {"relu": 2, "resid": 1, "qk": 2}[evac]
        nEa = {"relu": 1, "resid": 0, "qk": 1}[evac]
        nStW = {"relu": 2, "resid": 1, "qk": 2}[evac]
        scname, sccol_base = scale_col
        wsem = ("wA", "wB")
        xsem = ("xA", "xB")
        stsem = ("stA", "stB")

        @b.sync
        def _(sync):
            self.wait_stores(sync, S)
            if load_limbs:
                sync.wait_ge(self.sem["tM"], S["tM"])
                srcT = self.hLd if in_src == "hLd" else self.attnL
                for l in range(2):
                    for i in range(DC):
                        ap = (srcT[l, i, :, :] if in_src == "hLd"
                              else srcT[l, i, :, ts * TS:(ts + 1) * TS])
                        dd = self.inl[:, (l * DC + i) * TS:(l * DC + i + 1) * TS]
                        self.dma(sync, "blk", dd, ap)
            for o in range(OC):
                sync.wait_ge(self.sem["tM"],
                             S["tM"] + (o - 1 if o >= 2 else 0))
                self.dma(sync, wsem[o % 2],
                         self.wsb[:, (o % 2) * DC * 128:(o % 2 + 1) * DC * 128],
                         wscr[o, :, :])
                if evac == "resid":
                    sync.wait_ge(self.sem["vC"],
                                 S["vC"] + ((o - 1) * nEv if o >= 2 else 0))
                    self.dma(sync, xsem[o % 2],
                             self.xck[:, (o % 2) * TS:(o % 2 + 1) * TS],
                             resid_src[o, :, ts * TS:(ts + 1) * TS])

        @b.tensor
        def _(t):
            for o in range(OC):
                t.wait_ge(self.sem[wsem[o % 2]], S[wsem[o % 2]] + 16 * (o // 2 + 1))
                if o == 0:
                    if load_limbs:
                        t.wait_ge(self.sem["blk"], S["blk"] + n_in * 16)
                    else:
                        t.wait_ge(self.sem["vC"], S["vC"])
                if o >= 4:
                    if nEa:
                        t.wait_ge(self.sem["aC"], S["aC"] + (o - 3) * nEa)
                    t.wait_ge(self.sem["vC"], S["vC"] + (o - 3) * nEv)
                else:
                    t.wait_ge(self.sem["aC"], S["aC"])
                    t.wait_ge(self.sem["vC"], S["vC"])
                pb = self.bank(o % 4)[:, :TS]
                wbuf = self.wsb[:, (o % 2) * DC * 128:(o % 2 + 1) * DC * 128]
                n = 2 * DC
                k = 0
                ins = None
                for i in range(DC):
                    for l in range(2):
                        lim = self.inl[:, (l * DC + i) * TS:(l * DC + i + 1) * TS]
                        ins = t.matmul(pb, wbuf[:, i * 128:(i + 1) * 128], lim,
                                       start=(k == 0), stop=(k == n - 1))
                        k += 1
                self.tinc(ins)

        @b.scalar
        def _(sc):
            if nEa == 0:
                return
            for o in range(OC):
                sc.wait_ge(self.sem["tM"], S["tM"] + o + 1)
                if o >= 2:
                    sc.wait_ge(self.sem["vC"], S["vC"] + (o - 1) * nEv)
                else:
                    sc.wait_ge(self.sem["vC"], S["vC"])
                    sc.wait_ge(self.sem["blk"], S["blk"])
                    self.wait_stores(sc, S)
                pb = self.bank(o % 4)[:, :TS]
                tm = self.tmp32[:, (o % 2) * TS:(o % 2 + 1) * TS]
                if evac == "relu":
                    scv = self.scal(scname)
                    self.A(sc, lambda tm=tm, pb=pb, scv=scv:
                           sc.activation(tm, pb, AF.Relu, scale=scv))
                else:
                    scv = self.sct[:, sccol_base + o:sccol_base + o + 1]
                    self.A(sc, lambda tm=tm, pb=pb, scv=scv: sc.mul(tm, pb, scv))

        @b.gpsimd
        def _(gp):
            for o in range(OC):
                gp.wait_ge(self.sem["vC"], S["vC"] + (o + 1) * nEv)
                tm = self.tmp32[:, (o % 2) * TS:(o % 2 + 1) * TS]
                lr = self.ltmp[:, (o % 2) * 2 * TS:(o % 2) * 2 * TS + TS]
                le = self.ltmp[:, (o % 2) * 2 * TS + TS:(o % 2 + 1) * 2 * TS]
                if evac == "resid":
                    self.dma(gp, stsem[o % 2],
                             out_dst[o, :, ts * TS:(ts + 1) * TS], tm)
                elif evac == "relu":
                    self.dma(gp, stsem[o % 2], out_dst[0, o, :, :], lr)
                    self.dma(gp, stsem[o % 2], out_dst[1, o, :, :], le)
                else:
                    tgt, rb = out_dst
                    for l, til in ((0, lr), (1, le)):
                        if rb is None:
                            dd = tgt[l, o, :, ts * TS:(ts + 1) * TS]
                        else:
                            r0 = rb + l * (OC * 128) + o * 128
                            dd = tgt[r0:r0 + 128, ts * TS:(ts + 1) * TS]
                        self.dma(gp, stsem[o % 2], dd, til)

        @b.vector
        def _(vec):
            for o in range(OC):
                if nEa:
                    vec.wait_ge(self.sem["aC"], S["aC"] + o + 1)
                else:
                    vec.wait_ge(self.sem["tM"], S["tM"] + o + 1)
                if o >= 2:
                    vec.wait_ge(self.sem[stsem[o % 2]],
                                S[stsem[o % 2]] + 16 * nStW * (o // 2))
                else:
                    self.wait_stores(vec, S)
                tm = self.tmp32[:, (o % 2) * TS:(o % 2 + 1) * TS]
                lr = self.ltmp[:, (o % 2) * 2 * TS:(o % 2) * 2 * TS + TS]
                le = self.ltmp[:, (o % 2) * 2 * TS + TS:(o % 2 + 1) * 2 * TS]
                if evac == "resid":
                    pb = self.bank(o % 4)[:, :TS]
                    xc = self.xck[:, (o % 2) * TS:(o % 2 + 1) * TS]
                    vec.wait_ge(self.sem[xsem[o % 2]],
                                S[xsem[o % 2]] + 16 * (o // 2 + 1))
                    scv = (self.scal(scname) if sccol_base is None
                           else self.sct[:, sccol_base + o:sccol_base + o + 1])
                    self.V(vec, lambda tm=tm, pb=pb, xc=xc, scv=scv:
                           vec.scalar_tensor_tensor(tm, pb, scv, xc,
                                                    ALU.mult, ALU.add))
                else:
                    self.V(vec, lambda lr=lr, tm=tm: vec.tensor_copy(lr, tm))
                    self.V(vec, lambda le=le, tm=tm, lr=lr: vec.tensor_sub(le, tm, lr))

    def ph_qkv(self, src):
        c = self.c
        for ts in range(c.NTS):
            self.ph_ln(src, ts)
            self.ph_mm(self.wqs, c.DC, in_src="sbuf", evac="qk",
                       out_dst=(self.qLd, None), ts=ts, scale_col=("", 0))
            self.ph_mm(self.wks, c.KVC, in_src="sbuf", evac="qk",
                       out_dst=(self.ccink, 0), ts=ts, scale_col=("", 2 * c.DC))
            if ts == c.NTS - 1:
                self.ph_exchange_one(self.ccink, self.ccoutk)
            self.ph_vproj(ts)
        self.ph_exchange_one(self.ccinv, self.ccoutv)

    def ph_exchange_one(self, cin, cout):
        b = self.block
        S = dict(self.ct)

        @b.gpsimd
        def _(gp):
            self.wait_stores(gp, S)
            gp.collective_compute(
                "AllGather", ALU.bypass,
                replica_groups=[[0, 1], [2, 3], [4, 5], [6, 7]],
                ins=[cin.opt()], outs=[cout.opt()],
            ).then_inc(self.sem["CC"], 1)
            self.ct["CC"] += 1

    def ph_vproj(self, ts):
        b, c = self.block, self.c
        DC, TS, KVD = c.DC, c.TS, c.KVD
        TC = c.TC
        VT = max(KVD // 512, 1)
        NTILE = min(512, KVD)
        assert TC <= 4
        wsem = ("wA", "wB")
        stsem = ("stA", "stB")
        nkv = NTILE // 128
        nV = 3

        for vh in range(VT):
            S = dict(self.ct)

            @b.sync
            def _(sync):
                self.wait_stores(sync, S)
                for i in range(DC):
                    sync.wait_ge(self.sem["tM"],
                                 S["tM"] + (1 + i - 1 if i >= 2 else 0))
                    self.dma(sync, wsem[i % 2],
                             self.wsb[:, (i % 2) * DC * 128:
                                      (i % 2) * DC * 128 + NTILE],
                             self.wvs[i, :, vh * 512:vh * 512 + NTILE])

            @b.tensor
            def _(t):
                t.wait_ge(self.sem["vC"], S["vC"])
                t.wait_ge(self.sem["aC"], S["aC"])
                self.tinc(t.matmul(self.bank(5)[:, :NTILE], self.onesf[:],
                                   self.rows[:, 2 * c.T + vh * 512:
                                             2 * c.T + vh * 512 + NTILE],
                                   start=True, stop=True))
                for i in range(DC):
                    t.wait_ge(self.sem[wsem[i % 2]],
                              S[wsem[i % 2]] + 16 * (i // 2 + 1))
                    if i == 0:
                        t.wait_ge(self.sem["aC"], S["aC"] + 1)
                    wbuf = self.wsb[:, (i % 2) * DC * 128:
                                    (i % 2) * DC * 128 + NTILE]
                    ins = None
                    for tc in range(TC):
                        for l in range(2):
                            lim = self.inl[:, (l * DC + i) * TS + tc * 128:
                                           (l * DC + i) * TS + (tc + 1) * 128]
                            ins = t.matmul(self.bank(tc)[:, :NTILE], lim, wbuf,
                                           start=(i == 0 and l == 0),
                                           stop=(i == DC - 1 and l == 1))
                    self.tinc(ins)
            tM_acc = self.ct["tM"]

            @b.scalar
            def _(sc):
                sc.wait_ge(self.sem["tM"], S["tM"] + 1)
                sc.wait_ge(self.sem["vC"], S["vC"])
                vb = self.bcast[:, 0:NTILE]
                self.A(sc, lambda vb=vb: sc.copy(vb, self.bank(5)[:, :NTILE]))

            @b.vector
            def _(vec):
                vec.wait_ge(self.sem["tM"], tM_acc)
                for tc in range(TC):
                    bidx = tc
                    pbk = self.bank(tc)[:, :NTILE]
                    tm = self.tmp32[:, (bidx % 2) * TS:(bidx % 2) * TS + NTILE]
                    lr = self.ltmp[:, (bidx % 2) * 2 * TS:
                                   (bidx % 2) * 2 * TS + NTILE]
                    le = self.ltmp[:, ((bidx % 2) * 2 + 1) * TS:
                                   ((bidx % 2) * 2 + 1) * TS + NTILE]
                    vb = self.bcast[:, 0:NTILE]
                    if bidx >= 2:
                        vec.wait_ge(self.sem[stsem[bidx % 2]],
                                    S[stsem[bidx % 2]] +
                                    16 * 2 * nkv * (bidx // 2))
                    else:
                        self.wait_stores(vec, S)
                    self.V(vec, lambda tm=tm, pbk=pbk, vb=vb:
                           vec.tensor_mul(tm, pbk, vb))
                    self.V(vec, lambda lr=lr, tm=tm: vec.tensor_copy(lr, tm))
                    self.V(vec, lambda le=le, tm=tm, lr=lr:
                           vec.tensor_sub(le, tm, lr))

            @b.gpsimd
            def _(gp):
                for tc in range(TC):
                    bidx = tc
                    gp.wait_ge(self.sem["vC"], S["vC"] + (bidx + 1) * nV)
                    lr = self.ltmp[:, (bidx % 2) * 2 * TS:
                                   (bidx % 2) * 2 * TS + NTILE]
                    le = self.ltmp[:, ((bidx % 2) * 2 + 1) * TS:
                                   ((bidx % 2) * 2 + 1) * TS + NTILE]
                    kcg = (ts * TS) // 128 + tc
                    for j in range(nkv):
                        kvc = vh * 4 + j if NTILE >= 512 else j
                        for l, til in ((0, lr), (1, le)):
                            self.dma(gp, stsem[bidx % 2],
                                     self.ccinv[(kvc * 2 + l) * 128:
                                                (kvc * 2 + l) * 128 + 128,
                                                kcg * 128:(kcg + 1) * 128],
                                     til[:, j * 128:(j + 1) * 128])

    def att_views(self, h):
        c = self.c
        T, TS, KC = c.T, c.TS, c.KC
        inl = self.inl
        p = h % 2
        off = 0
        qs = []
        ks = []
        for sl in range(2):
            qs.append((inl[:, off:off + T], inl[:, off + T:off + 2 * T]))
            off += 2 * T
        for sl in range(2):
            ks.append((inl[:, off:off + 2 * T], inl[:, off + 2 * T:off + 4 * T]))
            off += 4 * T
        vA = inl[:, off:off + KC * 128]; off += KC * 128
        vB = inl[:, off:off + KC * 128]; off += KC * 128
        pA = inl[:, off:off + KC * TS]; off += KC * TS
        pB = inl[:, off:off + KC * TS]; off += KC * TS
        assert off <= c.INLW, (off, c.INLW)
        p0 = self.xck[:, 0:TS]
        p1 = self.xck[:, TS:2 * TS]
        qA, qB = qs[p]
        kA, kB = ks[p]
        return qA, qB, kA, kB, vA, vB, pA, pB, (p0, p1)

    def ph_att_stats(self):
        b, c = self.block, self.c
        T, TS, KC, QC, KT = c.T, c.TS, c.KC, c.QC, c.KT
        S = dict(self.ct)
        nPQ = 2 * KT
        DH = QC * nPQ
        MH = QC * KT
        KR = c.KVC * 128

        ld_marks = []

        @b.sync
        def _(sync):
            for h in range(c.HEADS):
                kvh = h // c.NREP
                qA = self.att_views(h)[0]
                kA = self.att_views(h)[2]
                xs = ("xA", "xB")[h % 2]
                if h < 2:
                    if h == 0:
                        sync.wait_ge(self.sem["CC"], 2)
                    sync.wait_ge(self.sem["tM"], S["tM"])
                    sync.wait_ge(self.sem["vC"], S["vC"])
                    self.wait_stores(sync, S)
                else:
                    sync.wait_ge(self.sem["tM"], S["tM"] + (h - 1) * MH)
                self.dma(sync, xs, qA, self.qLd[0, h, :, :])
                for r in range(2):
                    self.dma(sync, xs, kA[:, r * T:(r + 1) * T],
                             self.ccoutk[r * 2 * KR + kvh * 128:
                                         r * 2 * KR + (kvh + 1) * 128, :])
                ld_marks.append((xs, self.ct[xs]))

        @b.tensor
        def _(t):
            t.wait_ge(self.sem["aC"], S["aC"])
            t.wait_ge(self.sem["vC"], S["vC"])
            for h in range(c.HEADS):
                qA = self.att_views(h)[0]
                kA = self.att_views(h)[2]
                xs, xv = ld_marks[h]
                t.wait_ge(self.sem[xs], xv)
                for qc in range(QC):
                    for kt in range(KT):
                        jj = h * MH + qc * KT + kt
                        if jj >= 4:
                            h2, r2 = divmod(jj - 4, MH)
                            qc2, kt2 = divmod(r2, KT)
                            t.wait_ge(self.sem["vC"],
                                      S["vC"] + h2 * DH + qc2 * nPQ + kt2 + 1)
                        self.tinc(t.matmul(self.bank(jj % 4)[:, :TS],
                                           qA[:, qc * 128:(qc + 1) * 128],
                                           kA[:, kt * TS:(kt + 1) * TS],
                                           start=True, stop=True))

        @b.vector
        def _(vec):
            for h in range(c.HEADS):
                if h >= 1:
                    vec.wait_ge(self.sem["stO"], S["stO"] + h * QC * 16)
                for qc in range(QC):
                    for kt in range(KT):
                        vec.wait_ge(self.sem["tM"],
                                    S["tM"] + h * MH + qc * KT + kt + 1)
                        bidx = (h * MH + qc * KT + kt) % 4
                        mx = self.bcast[:, kt:kt + 1]
                        self.V(vec, lambda mx=mx, bidx=bidx: vec.reduce_max(
                            mx, self.bank(bidx)[:, :TS], axis=AX))
                    mcol = self.bcast[:, 8 + qc:8 + qc + 1]
                    self.V(vec, lambda mcol=mcol:
                           vec.tensor_copy(mcol, self.bcast[:, 0:1]))
                    for kt in range(1, KT):
                        self.V(vec, lambda mcol=mcol, kt=kt: vec.tensor_max(
                            mcol, mcol, self.bcast[:, kt:kt + 1]))

        @b.gpsimd
        def _(gp):
            for h in range(c.HEADS):
                gp.wait_ge(self.sem["vC"], S["vC"] + (h + 1) * DH)
                for qc in range(QC):
                    self.dma(gp, "stO", self.mrow_d[h, qc].unsqueeze(1),
                             self.bcast[:, 8 + qc:8 + qc + 1])
        self.stats_end = dict(self.ct)

    def ph_att_head(self, h):
        b, c = self.block, self.c
        T, TS, KC, QC, KT = c.T, c.TS, c.KC, c.QC, c.KT
        S = dict(self.ct)
        qA, qB, kA, kB, vA, vB, pA, pB, ptmp = self.att_views(h)
        kvh = h // c.NREP
        KR = c.KVC * 128

        @b.sync
        def _(sync):
            xs = ("xA", "xB")[h % 2]
            if h >= 2:
                sync.wait_ge(self.sem["tM"], self.qk_last_tM[h - 2])
            else:
                sync.wait_ge(self.sem["tM"], self.stats_end["tM"])
                sync.wait_ge(self.sem["vC"], self.stats_end["vC"])
            self.dma(sync, xs, qA, self.qLd[0, h, :, :])
            self.dma(sync, xs, qB, self.qLd[1, h, :, :])
            for r in range(2):
                self.dma(sync, xs, kA[:, r * T:(r + 1) * T],
                         self.ccoutk[r * 2 * KR + kvh * 128:
                                     r * 2 * KR + (kvh + 1) * 128, :])
                self.dma(sync, xs, kB[:, r * T:(r + 1) * T],
                         self.ccoutk[r * 2 * KR + KR + kvh * 128:
                                     r * 2 * KR + KR + (kvh + 1) * 128, :])
            if h >= 1:
                sync.wait_ge(self.sem["vC"], self.mrow_copy_vC[h - 1])
            self.dma(sync, xs, self.rows[:, 0:T],
                     self.mrow_d[h].rearrange("a b -> (a b)").unsqueeze(0))
            if h >= 1:
                sync.wait_ge(self.sem["tM"], self.av_tM[h - 1])
            for r in range(2):
                for l, vt in ((0, vA), (1, vB)):
                    vr0 = r * (c.KVC * 2 * 128) + (kvh * 2 + l) * 128
                    self.dma(sync, xs,
                             vt[:, r * (T // 128) * 128:(r + 1) * (T // 128) * 128],
                             self.ccoutv[vr0:vr0 + 128, :])
        xs_h = ("xA", "xB")[h % 2]
        ld_head = self.ct[xs_h]

        @b.vector
        def _(vec):
            vec.wait_ge(self.sem[xs_h], ld_head)
            if h >= 1:
                vec.wait_ge(self.sem["tM"], self.qk_last_tM[h - 1])
            self.V(vec, lambda: vec.tensor_copy(self.rowsR[:], self.rows[:, 0:T]))
        vC_mrow = self.ct["vC"]
        self.mrow_copy_vC[h] = vC_mrow

        for qt in range(c.NTS):
            self.att_qt(h, qt, qA, qB, kA, kB, vA, vB, pA, pB, ptmp, vC_mrow)
        self.qk_last_tM[h] = self.qk_tM_mark
        self.av_tM[h] = self.av_tM_mark

    def att_qt(self, h, qt, qA, qB, kA, kB, vA, vB, pA, pB, ptmp, vC_mrow):
        b, c = self.block, self.c
        T, TS, KC = c.T, c.TS, c.KC
        S = dict(self.ct)
        nEvK = 2
        xs_h = ("xA", "xB")[h % 2]
        ld_h = self.ct[xs_h]

        @b.tensor
        def _(t):
            t.wait_ge(self.sem[xs_h], ld_h)
            t.wait_ge(self.sem["vC"], max(S["vC"], vC_mrow))
            for kc in range(KC):
                if kc >= 4:
                    t.wait_ge(self.sem["vC"], S["vC"] + (kc - 3) * nEvK)
                pbk = self.bank(kc % 4)[:, :TS]
                q0 = qA[:, qt * TS:(qt + 1) * TS]
                q1 = qB[:, qt * TS:(qt + 1) * TS]
                kk0 = kA[:, kc * 128:(kc + 1) * 128]
                kk1 = kB[:, kc * 128:(kc + 1) * 128]
                t.matmul(pbk, kk0, q0, start=True, stop=False)
                t.matmul(pbk, kk1, q0, start=False, stop=False)
                t.matmul(pbk, kk0, q1, start=False, stop=False)
                self.tinc(t.matmul(pbk, self.negrow[:],
                                   self.rowsR[:, qt * TS:(qt + 1) * TS],
                                   start=False, stop=True))
            self.qk_tM_mark = self.ct["tM"]
            t.wait_ge(self.sem["vC"], S["vC"] + KC * nEvK)
            psz = self.bank(5)[0:1, :TS]
            n = 2 * KC
            k = 0
            ins = None
            for kc in range(KC):
                for pl in (pA, pB):
                    ins = t.matmul(psz, self.onescr[:],
                                   pl[:, kc * TS:(kc + 1) * TS],
                                   start=(k == 0), stop=(k == n - 1))
                    k += 1
            self.tinc(ins)
            t.wait_ge(self.sem["vC"], S["vC"] + KC * nEvK + 1)
            self.tinc(t.matmul(self.bank(6)[:, :TS], self.onesf[:],
                               self.rows[:, T:T + TS], start=True, stop=True))
            psav = self.bank(7)[:, :TS]
            n = 3 * KC
            k = 0
            for kc in range(KC):
                for (vt, pl) in ((vA, pA), (vA, pB), (vB, pA)):
                    ins = t.matmul(psav, vt[:, kc * 128:(kc + 1) * 128],
                                   pl[:, kc * TS:(kc + 1) * TS],
                                   start=(k == 0), stop=(k == n - 1))
                    k += 1
            self.tinc(ins)
            self.av_tM_mark = self.ct["tM"]

        @b.scalar
        def _(sc):
            for kc in range(KC):
                sc.wait_ge(self.sem["tM"], S["tM"] + kc + 1)
                if kc >= 2:
                    sc.wait_ge(self.sem["vC"], S["vC"] + (kc - 1) * nEvK)
                else:
                    sc.wait_ge(self.sem["vC"], S["vC"])
                pt = ptmp[kc % 2]
                pbk = self.bank(kc % 4)[:, :TS]
                self.A(sc, lambda pt=pt, pbk=pbk:
                       sc.activation(pt, pbk, AF.Exp))
            sc.wait_ge(self.sem["tM"], S["tM"] + KC + 2)
            sc.wait_ge(self.sem["vC"], S["vC"] + KC * nEvK + 1)
            self.A(sc, lambda: sc.copy(self.bcast[:, TS:2 * TS],
                                       self.bank(6)[:, :TS]))

        @b.vector
        def _(vec):
            for kc in range(KC):
                vec.wait_ge(self.sem["aC"], S["aC"] + kc + 1)
                pt = ptmp[kc % 2]
                pr = pA[:, kc * TS:(kc + 1) * TS]
                pe = pB[:, kc * TS:(kc + 1) * TS]
                self.V(vec, lambda pr=pr, pt=pt: vec.tensor_copy(pr, pt))
                self.V(vec, lambda pe=pe, pt=pt, pr=pr: vec.tensor_sub(pe, pt, pr))
            vec.wait_ge(self.sem["tM"], S["tM"] + KC + 1)
            rz = self.rows[:, T:T + TS]
            self.V(vec, lambda rz=rz: vec.reciprocal(rz, self.bank(5)[0:1, :TS]))
            vec.wait_ge(self.sem["tM"], S["tM"] + KC + 3)
            vec.wait_ge(self.sem["aC"], S["aC"] + KC + 1)
            self.wait_stores(vec, S)
            tm = self.tmp32[:, 0:TS]
            lr = self.ltmp[:, 0:TS]
            le = self.ltmp[:, TS:2 * TS]
            self.V(vec, lambda tm=tm: vec.tensor_mul(
                tm, self.bank(7)[:, :TS], self.bcast[:, TS:2 * TS]))
            self.V(vec, lambda lr=lr, tm=tm: vec.tensor_copy(lr, tm))
            self.V(vec, lambda le=le, tm=tm, lr=lr: vec.tensor_sub(le, tm, lr))

        @b.gpsimd
        def _(gp):
            gp.wait_ge(self.sem["vC"], self.ct["vC"])
            lr = self.ltmp[:, 0:TS]
            le = self.ltmp[:, TS:2 * TS]
            self.dma(gp, "stO", self.attnL[0, h, :, qt * TS:(qt + 1) * TS], lr)
            self.dma(gp, "stO", self.attnL[1, h, :, qt * TS:(qt + 1) * TS], le)

    def ph_end(self):
        b = self.block
        S = dict(self.ct)

        @b.gpsimd
        def _(gp):
            for s in ALL_SEMS:
                if S[s] > 0:
                    gp.wait_ge(self.sem[s], S[s])


def make_inputs_per_core(cfg, core, full):
    c = cfg
    seq = core // 2
    half = core % 2
    x = np.asarray(full["L0_attn_hidden_states"])
    xs = np.ascontiguousarray(
        x[seq, half * c.T:(half + 1) * c.T, :].T).reshape(c.DC, 128, c.T)

    def u8(a):
        a = np.asarray(a)
        return a.view(np.uint8) if a.dtype == np.bool_ else a.astype(np.uint8)

    cei = np.ones(c.D, dtype=np.float32)
    cei[c.NE:] = np.float32(c.EI_NEG)
    return {
        "x0": xs.astype(np.float32),
        "qsc_in": np.asarray(full["L0_attn_q_scale"], dtype=np.float32),
        "ksc_in": np.asarray(full["L0_attn_k_scale"], dtype=np.float32),
        "vsc_in": np.asarray(full["L0_attn_v_scale"], dtype=np.float32),
        "osc_in": np.asarray(full["L0_attn_o_scale"], dtype=np.float32),
        "wqT": np.ascontiguousarray(np.asarray(full["L0_attn_q_weight"], dtype=np.float32).T),
        "wkT": np.ascontiguousarray(np.asarray(full["L0_attn_k_weight"], dtype=np.float32).T),
        "wvT": np.ascontiguousarray(np.asarray(full["L0_attn_v_weight"], dtype=np.float32).T),
        "woT": np.ascontiguousarray(np.asarray(full["L0_attn_o_weight"], dtype=np.float32).T),
        "k0u": u8(full["L0_ffn_k0"]),
        "k1u": u8(full["L0_ffn_k1"]),
        "k2u": u8(full["L0_ffn_k2"]),
        "s0": np.asarray(full["L0_ffn_s0"], dtype=np.float32).reshape(1, 1),
        "s1": np.asarray(full["L0_ffn_s1"], dtype=np.float32).reshape(1, 1),
        "s2": np.asarray(full["L0_ffn_s2"], dtype=np.float32).reshape(1, 1),
        "cei": np.ascontiguousarray(cei.reshape(c.DC, 128).T),
        "cei2": np.ascontiguousarray((2.0 * cei).reshape(c.DC, 128).T),
    }


def build_program(cfg, name="mkmac1", phase_limit=None):
    nc = bass.Bass("TRN2", target_bir_lowering=False, name=name)
    Build(nc, cfg, phase_limit=phase_limit).build()
    return nc


def kernel(**inputs):
    cfg = CFG()
    nc = build_program(cfg, name="mkmacaron_v2")
    in_maps = [make_inputs_per_core(cfg, core, inputs) for core in range(8)]
    res = run_bass_kernel_spmd(nc, in_maps, core_ids=list(range(8)))
    out = np.empty((4, 2048, 4096), dtype=np.float32)
    for core in range(8):
        seq, half = core // 2, core % 2
        y = res.results[core]["yout"].reshape(cfg.D, cfg.T)
        out[seq, half * cfg.T:(half + 1) * cfg.T, :] = y.T
    return out


# version 15
# speedup vs baseline: 1.4759x; 1.0297x over previous
import numpy as np
import concourse.bass as bass
import concourse.mybir as mybir
from concourse.bass_utils import run_bass_kernel_spmd
from contextlib import ExitStack

F32 = mybir.dt.float32
F32R = mybir.dt.float32r
U8 = mybir.dt.uint8
AF = mybir.ActivationFunctionType
ALU = mybir.AluOpType
AX = mybir.AxisListType.X

LOAD_SEMS = ["blk", "wA", "wB", "xA", "xB", "kA", "kB"]
STORE_SEMS = ["stA", "stB", "stC", "stD", "stO"]
ALL_SEMS = LOAD_SEMS + STORE_SEMS + ["tM", "vC", "aC", "gC", "CC"]


class CFG:
    def __init__(self, D=4096, T=1024, HD=128, KVH=8, P_EI=0.8):
        self.D = D
        self.T = T
        self.HD = HD
        self.DC = D // 128
        self.HEADS = D // HD
        self.KVH = KVH
        self.KVD = KVH * HD
        self.KVC = self.KVD // 128
        self.NREP = self.HEADS // self.KVH
        self.TS = T // 2
        self.NTS = 2
        self.TC = self.TS // 128
        self.SEQK = 2 * T
        self.KC = self.SEQK // 128
        self.QC = T // 128
        self.KT = self.SEQK // self.TS
        self.NE = round(D * P_EI)
        self.EI_NEG = -P_EI / (1.0 - P_EI)
        att_need = (4 * self.T + 8 * self.T + 2 * self.KC * 128 +
                    2 * self.KC * self.TS)
        self.INLW = max(2 * self.DC * self.TS, att_need, 6 * D)
        assert self.TS <= 512 and HD == 128
        assert self.DC % 4 == 0


class Build:
    def __init__(self, nc, cfg, phase_limit=None):
        self.nc = nc
        self.c = cfg
        self.phase_limit = phase_limit
        self.ct = {s: 0 for s in ALL_SEMS}
        self.qk_last_tM = {}
        self.av_tM = {}
        self.mrow_copy_vC = {}
        self.stats_end = None

    def V(self, vec, make):
        vec.wait_ge(self.sem["vC"], self.ct["vC"])
        make().then_inc(self.sem["vC"], 1)
        self.ct["vC"] += 1

    def A(self, sc, make):
        sc.wait_ge(self.sem["aC"], self.ct["aC"])
        make().then_inc(self.sem["aC"], 1)
        self.ct["aC"] += 1

    def G(self, gp, make):
        gp.wait_ge(self.sem["gC"], self.ct["gC"])
        make().then_inc(self.sem["gC"], 1)
        self.ct["gC"] += 1

    def tinc(self, ins):
        ins.then_inc(self.sem["tM"], 1)
        self.ct["tM"] += 1

    def dma(self, eng, stream, out, in_, **kw):
        eng.dma_start(out=out, in_=in_, **kw).then_inc(self.sem[stream], 16)
        self.ct[stream] += 16

    def wait_stores(self, eng, S):
        for s in STORE_SEMS:
            if S[s] > 0:
                eng.wait_ge(self.sem[s], S[s])

    def build(self):
        nc, c = self.nc, self.c
        D, T, TS, DC = c.D, c.T, c.TS, c.DC
        dt = nc.dram_tensor
        self.x0 = dt("x0", [DC, 128, T], F32, kind="ExternalInput").ap()
        self.qsc_in = dt("qsc_in", [D], F32, kind="ExternalInput").ap()
        self.ksc_in = dt("ksc_in", [c.KVD], F32, kind="ExternalInput").ap()
        self.vsc_in = dt("vsc_in", [c.KVD], F32, kind="ExternalInput").ap()
        self.osc_in = dt("osc_in", [D], F32, kind="ExternalInput").ap()
        self.k0u = dt("k0u", [D, D], U8, kind="ExternalInput").ap()
        self.k1u = dt("k1u", [D, D], U8, kind="ExternalInput").ap()
        self.k2u = dt("k2u", [D, D], U8, kind="ExternalInput").ap()
        self.s0 = dt("s0", [1, 1], F32, kind="ExternalInput").ap()
        self.s1 = dt("s1", [1, 1], F32, kind="ExternalInput").ap()
        self.s2 = dt("s2", [1, 1], F32, kind="ExternalInput").ap()
        self.wqT = dt("wqT", [D, D], F32, kind="ExternalInput").ap()
        self.wkT = dt("wkT", [D, c.KVD], F32, kind="ExternalInput").ap()
        self.wvT = dt("wvT", [D, c.KVD], F32, kind="ExternalInput").ap()
        self.woT = dt("woT", [D, D], F32, kind="ExternalInput").ap()
        self.cei = dt("cei", [128, DC], F32, kind="ExternalInput").ap()
        self.cei2 = dt("cei2", [128, DC], F32, kind="ExternalInput").ap()
        self.yout = dt("yout", [DC, 128, T], F32, kind="ExternalOutput").ap()

        self.xcur = dt("xcur", [DC, 128, T], F32).ap()
        self.w0s = dt("w0s", [DC, 128, DC * 128], F32R).ap()
        self.w1s = dt("w1s", [DC, 128, DC * 128], F32R).ap()
        self.w2s = dt("w2s", [DC, 128, DC * 128], F32R).ap()
        self.wqs = dt("wqs", [DC, 128, DC * 128], F32R).ap()
        self.wks = dt("wks", [c.KVC, 128, DC * 128], F32R).ap()
        self.wos = dt("wos", [DC, 128, DC * 128], F32R).ap()
        self.wvs = dt("wvs", [DC, 128, c.KVD], F32R).ap()
        self.qLd = dt("qLd", [2, DC, 128, T], F32R).ap()
        self.hLd = dt("hLd", [2, DC, 128, TS], F32R).ap()
        self.ccink = dt("ccink", [2 * c.KVC * 128, T], F32R).ap()
        self.ccoutk = dt("ccoutk", [2 * 2 * c.KVC * 128, T], F32R).ap()
        self.ccinv = dt("ccinv", [c.KVC * 2 * 128, T], F32R).ap()
        self.ccoutv = dt("ccoutv", [2 * c.KVC * 2 * 128, T], F32R).ap()
        self.attnL = dt("attnL", [2, DC, 128, T], F32R).ap()
        self.mrow_d = dt("mrow_d", [c.HEADS, c.QC, 128], F32).ap()
        self.srow_d = dt("srow_d", [D], F32).ap()

        es = ExitStack()
        sb, ps = nc.sbuf_tensor, nc.psum_tensor
        E = es.enter_context
        self.inl = E(sb("inl", [128, c.INLW], F32R))
        self.wsb = E(sb("wsb", [128, 2 * DC * 128], F32R))
        self.tmp32 = E(sb("tmp32", [128, 2 * TS], F32))
        self.ltmp = E(sb("ltmp", [128, 4 * TS], F32R))
        self.xck = E(sb("xck", [128, 2 * TS], F32))
        self.bcast = E(sb("bcast", [128, 2 * TS], F32))
        self.rows = E(sb("rows", [1, max(3 * T, 1024)], F32))
        self.crow = E(sb("crow", [1, 1024], F32))
        self.sct = E(sb("sct", [128, 2 * DC + c.KVC + 8], F32))
        self.ceis = E(sb("ceis", [128, 2 * DC], F32))
        self.onesf = E(sb("onesf", [1, 128], F32))
        self.onescf = E(sb("onescf", [128, 1], F32))
        self.onescr = E(sb("onescr", [128, 1], F32R))
        self.negrow = E(sb("negrow", [1, 128], F32R))
        self.rowsR = E(sb("rowsR", [1, c.T], F32R))
        self.psb = E(ps("psb", [128, 4096], F32))

        self.sem = {}
        for s in ALL_SEMS:
            self.sem[s] = E(nc.semaphore(s))
        self.block = E(nc.Block())
        self.emit_all(phase_limit=self.phase_limit)
        es.close()
        return nc

    def bank(self, i):
        return self.psb[:, i * 512:(i + 1) * 512]

    def scal(self, name):
        c = self.c
        base = 2 * c.DC + c.KVC
        idx = {"s0": base, "s1": base + 1, "s2": base + 2,
               "rs0": base + 3, "rs1": base + 4, "hs2": base + 5}[name]
        return self.sct[:, idx:idx + 1]

    def emit_all(self, phase_limit=None):
        c = self.c
        phases = [self.ph_consts,
                  lambda: self.ph_conv_ffn(self.k0u, self.w0s, use_c=False),
                  lambda: self.ph_conv_ffn(self.k1u, self.w1s, use_c=True),
                  lambda: self.ph_conv_ffn(self.k2u, self.w2s, use_c=True),
                  self.ph_conv_att_q,
                  self.ph_conv_att_k,
                  self.ph_conv_att_v,
                  self.ph_conv_att_o,
                  lambda: self.ph_ffn(src=self.x0, dst=self.xcur),
                  lambda: self.ph_qkv(src=self.xcur)]
        phases += [self.ph_att_stats]
        phases += [(lambda h=h: self.ph_att_head(h)) for h in range(c.HEADS)]
        phases += [(lambda ts=ts: self.ph_mm(
            self.wos, c.DC, in_src="attnL", evac="resid",
            resid_src=self.xcur, out_dst=self.xcur, ts=ts,
            scale_col=(None, c.DC))) for ts in range(c.NTS)]
        phases += [lambda: self.ph_ffn(src=self.xcur, dst=self.yout)]
        if phase_limit is None:
            phase_limit = len(phases)
        for ph in phases[:phase_limit]:
            ph()
        self.ph_end()

    def ph_consts(self):
        b, c = self.block, self.c

        @b.sync
        def _(sync):
            self.dma(sync, "blk", self.ceis[:, 0:c.DC], self.cei)
            self.dma(sync, "blk", self.ceis[:, c.DC:2 * c.DC], self.cei2)
            base = 2 * c.DC + c.KVC
            for j, s in enumerate([self.s0, self.s1, self.s2]):
                self.dma(sync, "blk", self.sct[:, base + j:base + j + 1],
                         bass.AP(s.tensor, 0, [[0, 128], [1, 1]]),
                         allow_slow_non_contiguous=True)
        blk_done = self.ct["blk"]

        @b.vector
        def _(vec):
            vec.wait_ge(self.sem["blk"], blk_done)
            self.V(vec, lambda: vec.memset(self.onesf[:], 1.0))
            self.V(vec, lambda: vec.memset(self.onescf[:], 1.0))
            self.V(vec, lambda: vec.tensor_copy(self.onescr[:], self.onescf[:]))
            self.V(vec, lambda: vec.tensor_scalar_mul(self.negrow[:], self.onesf[:], -1.0))
            base = 2 * c.DC + c.KVC
            self.V(vec, lambda: vec.tensor_scalar_mul(
                self.sct[:, base + 3:base + 4], self.sct[:, base:base + 1], 2.0))
            self.V(vec, lambda: vec.tensor_scalar_mul(
                self.sct[:, base + 4:base + 5], self.sct[:, base + 1:base + 2], 2.0))
            self.V(vec, lambda: vec.tensor_scalar_mul(
                self.sct[:, base + 5:base + 6], self.sct[:, base + 2:base + 3], 0.5))

    def ph_conv_ffn(self, ksrc, wdst, use_c):
        b, c = self.block, self.c
        DC = c.DC
        kslot = [self.xck[:, :].bitcast(U8), self.bcast[:, :].bitcast(U8)]
        S = dict(self.ct)
        nEv = 1
        GRP = 4
        nStW = DC // GRP
        ksem = ("kA", "kB")
        stsem = ("stA", "stB")

        @b.sync
        def _(sync):
            for i in range(DC):
                sync.wait_ge(self.sem["vC"],
                             S["vC"] + ((i - 1) * nEv if i >= 2 else 0))
                self.dma(sync, ksem[i % 2], kslot[i % 2][:, 0:c.D],
                         ksrc[i * 128:(i + 1) * 128, :])

        @b.vector
        def _(vec):
            for i in range(DC):
                vec.wait_ge(self.sem[ksem[i % 2]],
                            S[ksem[i % 2]] + 16 * (i // 2 + 1))
                if i >= 2:
                    vec.wait_ge(self.sem[stsem[i % 2]],
                                S[stsem[i % 2]] + 16 * nStW * (i // 2))
                else:
                    self.wait_stores(vec, S)
                    vec.wait_ge(self.sem["tM"], S["tM"])
                    vec.wait_ge(self.sem["aC"], S["aC"])
                sgn = self.inl[:, (i % 2) * c.D:(i % 2 + 1) * c.D]
                ksl = kslot[i % 2][:, 0:c.D]
                if use_c:
                    cc = self.ceis[:, i:i + 1]
                    c2 = self.ceis[:, DC + i:DC + i + 1]
                    self.V(vec, lambda sgn=sgn, ksl=ksl, cc=cc, c2=c2:
                           vec.tensor_scalar(sgn, ksl, c2, cc, ALU.mult, ALU.subtract))
                else:
                    self.V(vec, lambda sgn=sgn, ksl=ksl:
                           vec.tensor_scalar(sgn, ksl, 2.0, 1.0, ALU.mult, ALU.subtract))

        @b.gpsimd
        def _(gp):
            for i in range(DC):
                gp.wait_ge(self.sem["vC"], S["vC"] + (i + 1) * nEv)
                sgn = self.inl[:, (i % 2) * c.D:(i % 2 + 1) * c.D]
                for g in range(DC // GRP):
                    oc0 = g * GRP
                    self.dma(gp, stsem[i % 2],
                             wdst[oc0:oc0 + GRP, :, i * 128:(i + 1) * 128]
                             .transpose([1, 0, 2]),
                             sgn[:, oc0 * 128:(oc0 + GRP) * 128]
                             .rearrange("p (g o) -> p g o", o=128))

    def ph_conv_att_q(self):
        c = self.c
        rsq = float(np.float32(1.0) / np.float32(np.sqrt(np.float64(c.HD))))
        self.ph_sign(self.wqT, self.wqs, c.DC, whole=False,
                     scin=self.qsc_in, sccol=0, xsc=rsq)

    def ph_conv_att_k(self):
        c = self.c
        self.ph_sign(self.wkT, self.wks, c.KVC, whole=False,
                     scin=self.ksc_in, sccol=2 * c.DC, xsc=1.0)

    def ph_conv_att_v(self):
        c = self.c
        self.ph_sign(self.wvT, self.wvs, c.KVC, whole=True,
                     scin=self.vsc_in, sccol=-1, xsc=1.0)

    def ph_conv_att_o(self):
        c = self.c
        self.ph_sign(self.woT, self.wos, c.DC, whole=False,
                     scin=self.osc_in, sccol=c.DC, xsc=1.0)

    def ph_sign(self, srcT, wdst, OC, whole, scin, sccol, xsc):
        b, c = self.block, self.c
        DC = c.DC
        S = dict(self.ct)
        W = OC * 128
        nEa = 2
        GRP = 4 if OC % 4 == 0 else 1
        nStW = 1 if whole else OC // GRP
        NSG = min(512, W)
        nseg = W // NSG
        nR = 2
        wsem = ("wA", "wB")
        xsem = ("xA", "xB")
        stsem = ("stA", "stB")

        @b.sync
        def _(sync):
            for i in range(DC):
                if i >= 2:
                    sync.wait_ge(self.sem["aC"], S["aC"] + (i - 1) * nEa)
                else:
                    sync.wait_ge(self.sem["aC"], S["aC"])
                    sync.wait_ge(self.sem["vC"], S["vC"])
                    sync.wait_ge(self.sem["tM"], S["tM"])
                    self.wait_stores(sync, S)
                slab = self.inl[:, 2 * c.D + (i % 2) * c.D:
                                2 * c.D + (i % 2) * c.D + W]
                self.dma(sync, wsem[i % 2], slab,
                         srcT[i * 128:(i + 1) * 128, :].bitcast(F32R))
            for sg in range(nseg):
                sync.wait_ge(self.sem["vC"],
                             S["vC"] + max(sg - 1, 0) * nR)
                self.dma(sync, xsem[sg % 2],
                         self.rows[:, (sg % 2) * NSG:(sg % 2) * NSG + NSG],
                         bass.AP(scin.tensor, sg * NSG, [[NSG, 1], [1, NSG]]))

        @b.scalar
        def _(sc):
            for i in range(DC):
                sc.wait_ge(self.sem[wsem[i % 2]],
                           S[wsem[i % 2]] + 16 * (i // 2 + 1))
                if i >= 2:
                    sc.wait_ge(self.sem[stsem[i % 2]],
                               S[stsem[i % 2]] + 16 * nStW * (i // 2))
                    sc.wait_ge(self.sem["tM"], S["tM"] + i - 1)
                else:
                    self.wait_stores(sc, S)
                wslab = self.inl[:, 2 * c.D + (i % 2) * c.D:
                                 2 * c.D + (i % 2) * c.D + W].bitcast(F32)
                sgn = self.inl[:, (i % 2) * c.D:(i % 2) * c.D + W]
                absl = self.inl[:, 4 * c.D + (i % 2) * c.D:
                                4 * c.D + (i % 2) * c.D + W]
                self.A(sc, lambda sgn=sgn, w=wslab: sc.activation(sgn, w, AF.Sign))
                self.A(sc, lambda absl=absl, w=wslab:
                       sc.activation(absl, w, AF.Abs))

        @b.tensor
        def _(t):
            t.wait_ge(self.sem["vC"], S["vC"])
            t.wait_ge(self.sem["aC"], S["aC"])
            for i in range(DC):
                t.wait_ge(self.sem["aC"], S["aC"] + i * nEa + 2)
                absl = self.inl[:, 4 * c.D + (i % 2) * c.D:
                                4 * c.D + (i % 2) * c.D + W]
                ins = None
                for sg in range(nseg):
                    ins = t.matmul(self.bank(sg % 8)[0:1, :NSG], self.onescr[:],
                                   absl[:, sg * NSG:(sg + 1) * NSG],
                                   start=(i == 0), stop=(i == DC - 1))
                self.tinc(ins)

        @b.vector
        def _(vec):
            vec.wait_ge(self.sem["tM"], S["tM"] + DC)
            for sg in range(nseg):
                vec.wait_ge(self.sem[xsem[sg % 2]],
                            S[xsem[sg % 2]] + 16 * (sg // 2 + 1))
                seg = self.crow[:, (sg % 2) * 512:(sg % 2) * 512 + NSG]
                qseg = self.rows[:, (sg % 2) * NSG:(sg % 2) * NSG + NSG]
                if sg >= 2:
                    ssm = ("stC", "stD")[sg % 2]
                    vec.wait_ge(self.sem[ssm], S[ssm] + 16 * (sg // 2))
                self.V(vec, lambda seg=seg, sg=sg: vec.tensor_scalar(
                    seg, self.bank(sg % 8)[0:1, :NSG], 1.0 / c.D, 1e-8,
                    ALU.mult, ALU.max))
                if whole:
                    out = self.rows[:, 2 * c.T + sg * NSG:2 * c.T + (sg + 1) * NSG]
                else:
                    out = seg
                self.V(vec, lambda out=out, seg=seg, qseg=qseg:
                       vec.scalar_tensor_tensor(out, seg, xsc, qseg,
                                                ALU.mult, ALU.mult))

        @b.gpsimd
        def _(gp):
            for i in range(DC):
                gp.wait_ge(self.sem["aC"], S["aC"] + i * nEa + 1)
                sgn = self.inl[:, (i % 2) * c.D:(i % 2) * c.D + W]
                if whole:
                    self.dma(gp, stsem[i % 2], wdst[i, :, :], sgn)
                else:
                    for g in range(OC // GRP):
                        oc0 = g * GRP
                        if GRP == 1:
                            self.dma(gp, stsem[i % 2],
                                     wdst[oc0, :, i * 128:(i + 1) * 128],
                                     sgn[:, oc0 * 128:(oc0 + 1) * 128])
                        else:
                            self.dma(gp, stsem[i % 2],
                                     wdst[oc0:oc0 + GRP, :, i * 128:(i + 1) * 128]
                                     .transpose([1, 0, 2]),
                                     sgn[:, oc0 * 128:(oc0 + GRP) * 128]
                                     .rearrange("p (g o) -> p g o", o=128))
            if not whole:
                gp.wait_ge(self.sem["blk"], S["blk"])
                for sg in range(nseg):
                    gp.wait_ge(self.sem["vC"], S["vC"] + (sg + 1) * nR)
                    seg = self.crow[:, (sg % 2) * 512:(sg % 2) * 512 + NSG]
                    self.dma(gp, ("stC", "stD")[sg % 2],
                             bass.AP(self.srow_d.tensor, sg * NSG,
                                     [[NSG, 1], [1, NSG]]), seg)
        stC_seg = self.ct["stC"]
        stD_seg = self.ct["stD"]

        if not whole:
            @b.sync
            def _(sync):
                sync.wait_ge(self.sem["stC"], stC_seg)
                sync.wait_ge(self.sem["stD"], stD_seg)
                for oc in range(OC):
                    self.dma(sync, "blk", self.sct[:, sccol + oc:sccol + oc + 1],
                             bass.AP(self.srow_d.tensor, oc * 128,
                                     [[1, 128], [1, 1]]),
                             allow_slow_non_contiguous=True)

    def ph_ffn(self, src, dst):
        c = self.c
        for ts in range(c.NTS):
            self.ph_ln(src, ts)
            self.ph_mm(self.w0s, c.DC, in_src="sbuf", evac="relu",
                       out_dst=self.hLd, ts=ts, scale_col=("rs0", None))
            self.ph_mm(self.w1s, c.DC, in_src="hLd", evac="relu",
                       out_dst=self.hLd, ts=ts, scale_col=("rs1", None))
            self.ph_mm(self.w2s, c.DC, in_src="hLd", evac="resid",
                       resid_src=src, out_dst=dst, ts=ts, scale_col=("hs2", None))

    def ph_ln(self, src, ts):
        b, c = self.block, self.c
        DC, TS = c.DC, c.TS
        S = dict(self.ct)
        nV1, nV2, nR = 5, 4, 6
        xsem = ("xA", "xB")

        @b.sync
        def _(sync):
            self.wait_stores(sync, S)
            for i in range(DC):
                sync.wait_ge(self.sem["vC"],
                             S["vC"] + ((i - 1) * nV1 if i >= 2 else 0))
                self.dma(sync, xsem[i % 2],
                         self.xck[:, (i % 2) * TS:(i % 2 + 1) * TS],
                         src[i, :, ts * TS:(ts + 1) * TS])
            for i in range(DC):
                sync.wait_ge(self.sem["vC"], S["vC"] + DC * nV1 + nR +
                             max(i - 1, 0) * nV2)
                self.dma(sync, xsem[i % 2],
                         self.xck[:, (i % 2) * TS:(i % 2 + 1) * TS],
                         src[i, :, ts * TS:(ts + 1) * TS])

        @b.tensor
        def _(t):
            t.wait_ge(self.sem["aC"], S["aC"])
            for i in range(DC):
                t.wait_ge(self.sem["vC"], S["vC"] + i * nV1 + 2)
                A = self.ltmp[:, (i % 2) * 2 * TS:(i % 2) * 2 * TS + TS]
                B = self.ltmp[:, (i % 2) * 2 * TS + TS:(i % 2 + 1) * 2 * TS]
                psx = self.bank(4)[0:1, :TS]
                psq = self.bank(5)[0:1, :TS]
                t.matmul(psx, self.onescr[:], A, start=(i == 0), stop=False)
                self.tinc(t.matmul(psx, self.onescr[:], B, start=False,
                                   stop=(i == DC - 1)))
                t.wait_ge(self.sem["vC"], S["vC"] + i * nV1 + 5)
                t.matmul(psq, self.onescr[:], A, start=(i == 0), stop=False)
                self.tinc(t.matmul(psq, self.onescr[:], B, start=False,
                                   stop=(i == DC - 1)))
            t.wait_ge(self.sem["vC"], S["vC"] + DC * nV1 + nR)
            self.tinc(t.matmul(self.bank(2)[:, :TS], self.onesf[:],
                               self.rows[:, 0:TS], start=True, stop=True))
            self.tinc(t.matmul(self.bank(3)[:, :TS], self.onesf[:],
                               self.rows[:, TS:2 * TS], start=True, stop=True))
        tM_bc = self.ct["tM"]

        @b.vector
        def _(vec):
            for i in range(DC):
                vec.wait_ge(self.sem[xsem[i % 2]], S[xsem[i % 2]] + 16 * (i // 2 + 1))
                if i >= 2:
                    vec.wait_ge(self.sem["tM"], S["tM"] + (i - 1) * 2)
                else:
                    vec.wait_ge(self.sem["tM"], S["tM"])
                    self.wait_stores(vec, S)
                xc = self.xck[:, (i % 2) * TS:(i % 2 + 1) * TS]
                A = self.ltmp[:, (i % 2) * 2 * TS:(i % 2) * 2 * TS + TS]
                B = self.ltmp[:, (i % 2) * 2 * TS + TS:(i % 2 + 1) * 2 * TS]
                tm = self.tmp32[:, (i % 2) * TS:(i % 2 + 1) * TS]
                self.V(vec, lambda A=A, xc=xc: vec.tensor_copy(A, xc))
                self.V(vec, lambda B=B, xc=xc, A=A: vec.tensor_sub(B, xc, A))
                self.V(vec, lambda tm=tm, xc=xc: vec.tensor_mul(tm, xc, xc))
                vec.wait_ge(self.sem["tM"], S["tM"] + i * 2 + 1)
                self.V(vec, lambda A=A, tm=tm: vec.tensor_copy(A, tm))
                self.V(vec, lambda B=B, tm=tm, A=A: vec.tensor_sub(B, tm, A))
            vec.wait_ge(self.sem["tM"], S["tM"] + DC * 2)
            m = self.rows[:, 0:TS]
            r = self.rows[:, TS:2 * TS]
            v2 = self.rows[:, 2 * TS:3 * TS]
            psx = self.bank(4)[0:1, :TS]
            psq = self.bank(5)[0:1, :TS]
            self.V(vec, lambda: vec.tensor_scalar_mul(m, psx, 1.0 / c.D))
            self.V(vec, lambda: vec.tensor_scalar_mul(v2, psq, 1.0 / c.D))
            self.V(vec, lambda: vec.tensor_mul(r, m, m))
            self.V(vec, lambda: vec.tensor_sub(v2, v2, r))
            self.V(vec, lambda: vec.tensor_scalar_add(v2, v2, 1e-5))
            vec.wait_ge(self.sem["aC"], S["aC"] + 1)
            self.V(vec, lambda: vec.reciprocal(r, v2))
            for i in range(DC):
                vec.wait_ge(self.sem[xsem[i % 2]],
                            S[xsem[i % 2]] + 16 * (DC // 2 + i // 2 + 1))
                vec.wait_ge(self.sem["tM"], tM_bc)
                vec.wait_ge(self.sem["aC"], S["aC"] + 3)
                xc = self.xck[:, (i % 2) * TS:(i % 2 + 1) * TS]
                tm = self.tmp32[:, (i % 2) * TS:(i % 2 + 1) * TS]
                ur = self.inl[:, i * TS:(i + 1) * TS]
                ue = self.inl[:, (DC + i) * TS:(DC + i + 1) * TS]
                mb = self.bcast[:, 0:TS]
                rb = self.bcast[:, TS:2 * TS]
                self.V(vec, lambda tm=tm, xc=xc, mb=mb: vec.tensor_sub(tm, xc, mb))
                self.V(vec, lambda tm=tm, rb=rb: vec.tensor_mul(tm, tm, rb))
                self.V(vec, lambda ur=ur, tm=tm: vec.tensor_copy(ur, tm))
                self.V(vec, lambda ue=ue, tm=tm, ur=ur: vec.tensor_sub(ue, tm, ur))

        @b.scalar
        def _(sc):
            sc.wait_ge(self.sem["vC"], S["vC"] + DC * nV1 + 5)
            v2 = self.rows[:, 2 * TS:3 * TS]
            self.A(sc, lambda: sc.activation(v2, v2, AF.Sqrt))
            sc.wait_ge(self.sem["tM"], tM_bc)
            sc.wait_ge(self.sem["vC"], S["vC"] + DC * nV1 + nR)
            self.A(sc, lambda: sc.copy(self.bcast[:, 0:TS], self.bank(2)[:, :TS]))
            self.A(sc, lambda: sc.copy(self.bcast[:, TS:2 * TS], self.bank(3)[:, :TS]))

    def ph_mm(self, wscr, OC, in_src, evac, out_dst, ts, scale_col,
             resid_src=None):
        b, c = self.block, self.c
        DC, TS = c.DC, c.TS
        S = dict(self.ct)
        load_limbs = in_src in ("hLd", "attnL")
        n_in = 2 * DC if load_limbs else 0
        nEv = {"relu": 2, "resid": 1, "qk": 2}[evac]
        nEa = {"relu": 1, "resid": 0, "qk": 1}[evac]
        nStW = {"relu": 2, "resid": 1, "qk": 2}[evac]
        scname, sccol_base = scale_col
        wsem = ("wA", "wB")
        xsem = ("xA", "xB")
        stsem = ("stA", "stB")

        @b.sync
        def _(sync):
            self.wait_stores(sync, S)
            if load_limbs:
                sync.wait_ge(self.sem["tM"], S["tM"])
                srcT = self.hLd if in_src == "hLd" else self.attnL
                for l in range(2):
                    for i in range(DC):
                        ap = (srcT[l, i, :, :] if in_src == "hLd"
                              else srcT[l, i, :, ts * TS:(ts + 1) * TS])
                        dd = self.inl[:, (l * DC + i) * TS:(l * DC + i + 1) * TS]
                        self.dma(sync, "blk", dd, ap)
            for o in range(OC):
                sync.wait_ge(self.sem["tM"],
                             S["tM"] + (o - 1 if o >= 2 else 0))
                self.dma(sync, wsem[o % 2],
                         self.wsb[:, (o % 2) * DC * 128:(o % 2 + 1) * DC * 128],
                         wscr[o, :, :])
                if evac == "resid":
                    sync.wait_ge(self.sem["vC"],
                                 S["vC"] + ((o - 1) * nEv if o >= 2 else 0))
                    self.dma(sync, xsem[o % 2],
                             self.xck[:, (o % 2) * TS:(o % 2 + 1) * TS],
                             resid_src[o, :, ts * TS:(ts + 1) * TS])

        @b.tensor
        def _(t):
            for o in range(OC):
                t.wait_ge(self.sem[wsem[o % 2]], S[wsem[o % 2]] + 16 * (o // 2 + 1))
                if o == 0:
                    if load_limbs:
                        t.wait_ge(self.sem["blk"], S["blk"] + n_in * 16)
                    else:
                        t.wait_ge(self.sem["vC"], S["vC"])
                if o >= 4:
                    if nEa:
                        t.wait_ge(self.sem["aC"], S["aC"] + (o - 3) * nEa)
                    t.wait_ge(self.sem["vC"], S["vC"] + (o - 3) * nEv)
                else:
                    t.wait_ge(self.sem["aC"], S["aC"])
                    t.wait_ge(self.sem["vC"], S["vC"])
                pb = self.bank(o % 4)[:, :TS]
                wbuf = self.wsb[:, (o % 2) * DC * 128:(o % 2 + 1) * DC * 128]
                n = 2 * DC
                k = 0
                ins = None
                for i in range(DC):
                    for l in range(2):
                        lim = self.inl[:, (l * DC + i) * TS:(l * DC + i + 1) * TS]
                        ins = t.matmul(pb, wbuf[:, i * 128:(i + 1) * 128], lim,
                                       start=(k == 0), stop=(k == n - 1))
                        k += 1
                self.tinc(ins)

        @b.scalar
        def _(sc):
            if nEa == 0:
                return
            for o in range(OC):
                sc.wait_ge(self.sem["tM"], S["tM"] + o + 1)
                if o >= 2:
                    sc.wait_ge(self.sem["vC"], S["vC"] + (o - 1) * nEv)
                else:
                    sc.wait_ge(self.sem["vC"], S["vC"])
                    sc.wait_ge(self.sem["blk"], S["blk"])
                    self.wait_stores(sc, S)
                pb = self.bank(o % 4)[:, :TS]
                tm = self.tmp32[:, (o % 2) * TS:(o % 2 + 1) * TS]
                if evac == "relu":
                    scv = self.scal(scname)
                    self.A(sc, lambda tm=tm, pb=pb, scv=scv:
                           sc.activation(tm, pb, AF.Relu, scale=scv))
                else:
                    scv = self.sct[:, sccol_base + o:sccol_base + o + 1]
                    self.A(sc, lambda tm=tm, pb=pb, scv=scv: sc.mul(tm, pb, scv))

        @b.gpsimd
        def _(gp):
            for o in range(OC):
                gp.wait_ge(self.sem["vC"], S["vC"] + (o + 1) * nEv)
                tm = self.tmp32[:, (o % 2) * TS:(o % 2 + 1) * TS]
                lr = self.ltmp[:, (o % 2) * 2 * TS:(o % 2) * 2 * TS + TS]
                le = self.ltmp[:, (o % 2) * 2 * TS + TS:(o % 2 + 1) * 2 * TS]
                if evac == "resid":
                    self.dma(gp, stsem[o % 2],
                             out_dst[o, :, ts * TS:(ts + 1) * TS], tm)
                elif evac == "relu":
                    self.dma(gp, stsem[o % 2], out_dst[0, o, :, :], lr)
                    self.dma(gp, stsem[o % 2], out_dst[1, o, :, :], le)
                else:
                    tgt, rb = out_dst
                    for l, til in ((0, lr), (1, le)):
                        if rb is None:
                            dd = tgt[l, o, :, ts * TS:(ts + 1) * TS]
                        else:
                            r0 = rb + l * (OC * 128) + o * 128
                            dd = tgt[r0:r0 + 128, ts * TS:(ts + 1) * TS]
                        self.dma(gp, stsem[o % 2], dd, til)

        @b.vector
        def _(vec):
            for o in range(OC):
                if nEa:
                    vec.wait_ge(self.sem["aC"], S["aC"] + o + 1)
                else:
                    vec.wait_ge(self.sem["tM"], S["tM"] + o + 1)
                if o >= 2:
                    vec.wait_ge(self.sem[stsem[o % 2]],
                                S[stsem[o % 2]] + 16 * nStW * (o // 2))
                else:
                    self.wait_stores(vec, S)
                tm = self.tmp32[:, (o % 2) * TS:(o % 2 + 1) * TS]
                lr = self.ltmp[:, (o % 2) * 2 * TS:(o % 2) * 2 * TS + TS]
                le = self.ltmp[:, (o % 2) * 2 * TS + TS:(o % 2 + 1) * 2 * TS]
                if evac == "resid":
                    pb = self.bank(o % 4)[:, :TS]
                    xc = self.xck[:, (o % 2) * TS:(o % 2 + 1) * TS]
                    vec.wait_ge(self.sem[xsem[o % 2]],
                                S[xsem[o % 2]] + 16 * (o // 2 + 1))
                    scv = (self.scal(scname) if sccol_base is None
                           else self.sct[:, sccol_base + o:sccol_base + o + 1])
                    self.V(vec, lambda tm=tm, pb=pb, xc=xc, scv=scv:
                           vec.scalar_tensor_tensor(tm, pb, scv, xc,
                                                    ALU.mult, ALU.add))
                else:
                    self.V(vec, lambda lr=lr, tm=tm: vec.tensor_copy(lr, tm))
                    self.V(vec, lambda le=le, tm=tm, lr=lr: vec.tensor_sub(le, tm, lr))

    def ph_qkv(self, src):
        c = self.c
        for ts in range(c.NTS):
            self.ph_ln(src, ts)
            self.ph_mm(self.wqs, c.DC, in_src="sbuf", evac="qk",
                       out_dst=(self.qLd, None), ts=ts, scale_col=("", 0))
            self.ph_mm(self.wks, c.KVC, in_src="sbuf", evac="qk",
                       out_dst=(self.ccink, 0), ts=ts, scale_col=("", 2 * c.DC))
            if ts == c.NTS - 1:
                self.ph_exchange_one(self.ccink, self.ccoutk)
            self.ph_vproj(ts)
        self.ph_exchange_one(self.ccinv, self.ccoutv)

    def ph_exchange_one(self, cin, cout):
        b = self.block
        S = dict(self.ct)

        @b.gpsimd
        def _(gp):
            self.wait_stores(gp, S)
            gp.collective_compute(
                "AllGather", ALU.bypass,
                replica_groups=[[0, 1], [2, 3], [4, 5], [6, 7]],
                ins=[cin.opt()], outs=[cout.opt()],
            ).then_inc(self.sem["CC"], 1)
            self.ct["CC"] += 1

    def ph_vproj(self, ts):
        b, c = self.block, self.c
        DC, TS, KVD = c.DC, c.TS, c.KVD
        TC = c.TC
        VT = max(KVD // 512, 1)
        NTILE = min(512, KVD)
        assert TC <= 4
        wsem = ("wA", "wB")
        stsem = ("stA", "stB")
        nkv = NTILE // 128
        nV = 3

        for vh in range(VT):
            S = dict(self.ct)

            @b.sync
            def _(sync):
                self.wait_stores(sync, S)
                for i in range(DC):
                    sync.wait_ge(self.sem["tM"],
                                 S["tM"] + (1 + i - 1 if i >= 2 else 0))
                    self.dma(sync, wsem[i % 2],
                             self.wsb[:, (i % 2) * DC * 128:
                                      (i % 2) * DC * 128 + NTILE],
                             self.wvs[i, :, vh * 512:vh * 512 + NTILE])

            @b.tensor
            def _(t):
                t.wait_ge(self.sem["vC"], S["vC"])
                t.wait_ge(self.sem["aC"], S["aC"])
                self.tinc(t.matmul(self.bank(5)[:, :NTILE], self.onesf[:],
                                   self.rows[:, 2 * c.T + vh * 512:
                                             2 * c.T + vh * 512 + NTILE],
                                   start=True, stop=True))
                for i in range(DC):
                    t.wait_ge(self.sem[wsem[i % 2]],
                              S[wsem[i % 2]] + 16 * (i // 2 + 1))
                    if i == 0:
                        t.wait_ge(self.sem["aC"], S["aC"] + 1)
                    wbuf = self.wsb[:, (i % 2) * DC * 128:
                                    (i % 2) * DC * 128 + NTILE]
                    ins = None
                    for tc in range(TC):
                        for l in range(2):
                            lim = self.inl[:, (l * DC + i) * TS + tc * 128:
                                           (l * DC + i) * TS + (tc + 1) * 128]
                            ins = t.matmul(self.bank(tc)[:, :NTILE], lim, wbuf,
                                           start=(i == 0 and l == 0),
                                           stop=(i == DC - 1 and l == 1))
                    self.tinc(ins)
            tM_acc = self.ct["tM"]

            @b.scalar
            def _(sc):
                sc.wait_ge(self.sem["tM"], S["tM"] + 1)
                sc.wait_ge(self.sem["vC"], S["vC"])
                vb = self.bcast[:, 0:NTILE]
                self.A(sc, lambda vb=vb: sc.copy(vb, self.bank(5)[:, :NTILE]))

            @b.vector
            def _(vec):
                vec.wait_ge(self.sem["tM"], tM_acc)
                for tc in range(TC):
                    bidx = tc
                    pbk = self.bank(tc)[:, :NTILE]
                    tm = self.tmp32[:, (bidx % 2) * TS:(bidx % 2) * TS + NTILE]
                    lr = self.ltmp[:, (bidx % 2) * 2 * TS:
                                   (bidx % 2) * 2 * TS + NTILE]
                    le = self.ltmp[:, ((bidx % 2) * 2 + 1) * TS:
                                   ((bidx % 2) * 2 + 1) * TS + NTILE]
                    vb = self.bcast[:, 0:NTILE]
                    if bidx >= 2:
                        vec.wait_ge(self.sem[stsem[bidx % 2]],
                                    S[stsem[bidx % 2]] +
                                    16 * 2 * nkv * (bidx // 2))
                    else:
                        self.wait_stores(vec, S)
                    self.V(vec, lambda tm=tm, pbk=pbk, vb=vb:
                           vec.tensor_mul(tm, pbk, vb))
                    self.V(vec, lambda lr=lr, tm=tm: vec.tensor_copy(lr, tm))
                    self.V(vec, lambda le=le, tm=tm, lr=lr:
                           vec.tensor_sub(le, tm, lr))

            @b.gpsimd
            def _(gp):
                for tc in range(TC):
                    bidx = tc
                    gp.wait_ge(self.sem["vC"], S["vC"] + (bidx + 1) * nV)
                    lr = self.ltmp[:, (bidx % 2) * 2 * TS:
                                   (bidx % 2) * 2 * TS + NTILE]
                    le = self.ltmp[:, ((bidx % 2) * 2 + 1) * TS:
                                   ((bidx % 2) * 2 + 1) * TS + NTILE]
                    kcg = (ts * TS) // 128 + tc
                    for j in range(nkv):
                        kvc = vh * 4 + j if NTILE >= 512 else j
                        for l, til in ((0, lr), (1, le)):
                            self.dma(gp, stsem[bidx % 2],
                                     self.ccinv[(kvc * 2 + l) * 128:
                                                (kvc * 2 + l) * 128 + 128,
                                                kcg * 128:(kcg + 1) * 128],
                                     til[:, j * 128:(j + 1) * 128])

    def att_views(self, h):
        c = self.c
        T, TS, KC = c.T, c.TS, c.KC
        inl = self.inl
        p = h % 2
        off = 0
        qs = []
        ks = []
        for sl in range(2):
            qs.append((inl[:, off:off + T], inl[:, off + T:off + 2 * T]))
            off += 2 * T
        for sl in range(2):
            ks.append((inl[:, off:off + 2 * T], inl[:, off + 2 * T:off + 4 * T]))
            off += 4 * T
        vA = inl[:, off:off + KC * 128]; off += KC * 128
        vB = inl[:, off:off + KC * 128]; off += KC * 128
        pA = inl[:, off:off + KC * TS]; off += KC * TS
        pB = inl[:, off:off + KC * TS]; off += KC * TS
        assert off <= c.INLW, (off, c.INLW)
        p0 = self.xck[:, 0:TS]
        p1 = self.xck[:, TS:2 * TS]
        qA, qB = qs[p]
        kA, kB = ks[p]
        return qA, qB, kA, kB, vA, vB, pA, pB, (p0, p1)

    def ph_att_stats(self):
        b, c = self.block, self.c
        T, TS, KC, QC, KT = c.T, c.TS, c.KC, c.QC, c.KT
        S = dict(self.ct)
        nPQ = 2 * KT
        DH = QC * nPQ
        MH = QC * KT
        KR = c.KVC * 128

        ld_marks = []

        @b.sync
        def _(sync):
            for h in range(c.HEADS):
                kvh = h // c.NREP
                qA = self.att_views(h)[0]
                kA = self.att_views(h)[2]
                xs = ("xA", "xB")[h % 2]
                if h < 2:
                    if h == 0:
                        sync.wait_ge(self.sem["CC"], 1)
                    sync.wait_ge(self.sem["tM"], S["tM"])
                    sync.wait_ge(self.sem["vC"], S["vC"])
                    self.wait_stores(sync, S)
                else:
                    sync.wait_ge(self.sem["tM"], S["tM"] + (h - 1) * MH)
                self.dma(sync, xs, qA, self.qLd[0, h, :, :])
                for r in range(2):
                    self.dma(sync, xs, kA[:, r * T:(r + 1) * T],
                             self.ccoutk[r * 2 * KR + kvh * 128:
                                         r * 2 * KR + (kvh + 1) * 128, :])
                ld_marks.append((xs, self.ct[xs]))

        @b.tensor
        def _(t):
            t.wait_ge(self.sem["aC"], S["aC"])
            t.wait_ge(self.sem["vC"], S["vC"])
            for h in range(c.HEADS):
                qA = self.att_views(h)[0]
                kA = self.att_views(h)[2]
                xs, xv = ld_marks[h]
                t.wait_ge(self.sem[xs], xv)
                for qc in range(QC):
                    gq = h * QC + qc
                    if gq >= 2:
                        t.wait_ge(self.sem["vC"], S["vC"] + gq - 1)
                    for kt in range(KT):
                        self.tinc(t.matmul(
                            self.bank((qc % 2) * 4 + kt)[:, :TS],
                            qA[:, qc * 128:(qc + 1) * 128],
                            kA[:, kt * TS:(kt + 1) * TS],
                            start=True, stop=True))

        @b.vector
        def _(vec):
            for h in range(c.HEADS):
                if h >= 1:
                    vec.wait_ge(self.sem["stO"], S["stO"] + h * QC * 16)
                for qc in range(QC):
                    vec.wait_ge(self.sem["tM"],
                                S["tM"] + h * MH + (qc + 1) * KT)
                    mcol = self.bcast[:, 8 + qc:8 + qc + 1]
                    base = (qc % 2) * 4 * 512
                    wide = (self.psb[:, base:base + 4 * 512]
                            .rearrange("p (kt w) -> p kt w", w=512)[:, :, 0:TS])
                    self.V(vec, lambda mcol=mcol, wide=wide: vec.reduce_max(
                        mcol, wide, axis=mybir.AxisListType.XY))

        @b.gpsimd
        def _(gp):
            for h in range(c.HEADS):
                gp.wait_ge(self.sem["vC"], S["vC"] + (h + 1) * QC)
                for qc in range(QC):
                    self.dma(gp, "stO", self.mrow_d[h, qc].unsqueeze(1),
                             self.bcast[:, 8 + qc:8 + qc + 1])
        self.stats_end = dict(self.ct)

    def ph_att_head(self, h):
        b, c = self.block, self.c
        T, TS, KC, QC, KT = c.T, c.TS, c.KC, c.QC, c.KT
        S = dict(self.ct)
        qA, qB, kA, kB, vA, vB, pA, pB, ptmp = self.att_views(h)
        kvh = h // c.NREP
        KR = c.KVC * 128

        @b.sync
        def _(sync):
            xs = ("xA", "xB")[h % 2]
            if h >= 2:
                sync.wait_ge(self.sem["tM"], self.qk_last_tM[h - 2])
            else:
                sync.wait_ge(self.sem["tM"], self.stats_end["tM"])
                sync.wait_ge(self.sem["vC"], self.stats_end["vC"])
            self.dma(sync, xs, qA, self.qLd[0, h, :, :])
            self.dma(sync, xs, qB, self.qLd[1, h, :, :])
            for r in range(2):
                self.dma(sync, xs, kA[:, r * T:(r + 1) * T],
                         self.ccoutk[r * 2 * KR + kvh * 128:
                                     r * 2 * KR + (kvh + 1) * 128, :])
                self.dma(sync, xs, kB[:, r * T:(r + 1) * T],
                         self.ccoutk[r * 2 * KR + KR + kvh * 128:
                                     r * 2 * KR + KR + (kvh + 1) * 128, :])
            if h >= 1:
                sync.wait_ge(self.sem["vC"], self.mrow_copy_vC[h - 1])
            self.dma(sync, xs, self.rows[:, 0:T],
                     self.mrow_d[h].rearrange("a b -> (a b)").unsqueeze(0))
            if h == 0:
                sync.wait_ge(self.sem["CC"], 2)
            if h >= 1:
                sync.wait_ge(self.sem["tM"], self.av_tM[h - 1])
            for r in range(2):
                for l, vt in ((0, vA), (1, vB)):
                    vr0 = r * (c.KVC * 2 * 128) + (kvh * 2 + l) * 128
                    self.dma(sync, xs,
                             vt[:, r * (T // 128) * 128:(r + 1) * (T // 128) * 128],
                             self.ccoutv[vr0:vr0 + 128, :])
        xs_h = ("xA", "xB")[h % 2]
        ld_head = self.ct[xs_h]

        @b.vector
        def _(vec):
            vec.wait_ge(self.sem[xs_h], ld_head)
            if h >= 1:
                vec.wait_ge(self.sem["tM"], self.qk_last_tM[h - 1])
            self.V(vec, lambda: vec.tensor_copy(self.rowsR[:], self.rows[:, 0:T]))
        vC_mrow = self.ct["vC"]
        self.mrow_copy_vC[h] = vC_mrow

        for qt in range(c.NTS):
            self.att_qt(h, qt, qA, qB, kA, kB, vA, vB, pA, pB, ptmp, vC_mrow)
        self.qk_last_tM[h] = self.qk_tM_mark
        self.av_tM[h] = self.av_tM_mark

    def att_qt(self, h, qt, qA, qB, kA, kB, vA, vB, pA, pB, ptmp, vC_mrow):
        b, c = self.block, self.c
        T, TS, KC = c.T, c.TS, c.KC
        S = dict(self.ct)
        nEvK = 2
        xs_h = ("xA", "xB")[h % 2]
        ld_h = self.ct[xs_h]

        @b.tensor
        def _(t):
            t.wait_ge(self.sem[xs_h], ld_h)
            t.wait_ge(self.sem["vC"], max(S["vC"], vC_mrow))
            for kc in range(KC):
                if kc >= 4:
                    t.wait_ge(self.sem["vC"], S["vC"] + (kc - 3) * nEvK)
                pbk = self.bank(kc % 4)[:, :TS]
                q0 = qA[:, qt * TS:(qt + 1) * TS]
                q1 = qB[:, qt * TS:(qt + 1) * TS]
                kk0 = kA[:, kc * 128:(kc + 1) * 128]
                kk1 = kB[:, kc * 128:(kc + 1) * 128]
                t.matmul(pbk, kk0, q0, start=True, stop=False)
                t.matmul(pbk, kk1, q0, start=False, stop=False)
                t.matmul(pbk, kk0, q1, start=False, stop=False)
                self.tinc(t.matmul(pbk, self.negrow[:],
                                   self.rowsR[:, qt * TS:(qt + 1) * TS],
                                   start=False, stop=True))
            self.qk_tM_mark = self.ct["tM"]
            t.wait_ge(self.sem["vC"], S["vC"] + KC * nEvK)
            psz = self.bank(5)[0:1, :TS]
            n = 2 * KC
            k = 0
            ins = None
            for kc in range(KC):
                for pl in (pA, pB):
                    ins = t.matmul(psz, self.onescr[:],
                                   pl[:, kc * TS:(kc + 1) * TS],
                                   start=(k == 0), stop=(k == n - 1))
                    k += 1
            self.tinc(ins)
            t.wait_ge(self.sem["vC"], S["vC"] + KC * nEvK + 1)
            self.tinc(t.matmul(self.bank(6)[:, :TS], self.onesf[:],
                               self.rows[:, T:T + TS], start=True, stop=True))
            psav = self.bank(7)[:, :TS]
            n = 3 * KC
            k = 0
            for kc in range(KC):
                for (vt, pl) in ((vA, pA), (vA, pB), (vB, pA)):
                    ins = t.matmul(psav, vt[:, kc * 128:(kc + 1) * 128],
                                   pl[:, kc * TS:(kc + 1) * TS],
                                   start=(k == 0), stop=(k == n - 1))
                    k += 1
            self.tinc(ins)
            self.av_tM_mark = self.ct["tM"]

        @b.scalar
        def _(sc):
            for kc in range(KC):
                sc.wait_ge(self.sem["tM"], S["tM"] + kc + 1)
                if kc >= 2:
                    sc.wait_ge(self.sem["vC"], S["vC"] + (kc - 1) * nEvK)
                else:
                    sc.wait_ge(self.sem["vC"], S["vC"])
                pt = ptmp[kc % 2]
                pbk = self.bank(kc % 4)[:, :TS]
                self.A(sc, lambda pt=pt, pbk=pbk:
                       sc.activation(pt, pbk, AF.Exp))
            sc.wait_ge(self.sem["tM"], S["tM"] + KC + 2)
            sc.wait_ge(self.sem["vC"], S["vC"] + KC * nEvK + 1)
            self.A(sc, lambda: sc.copy(self.bcast[:, TS:2 * TS],
                                       self.bank(6)[:, :TS]))

        @b.vector
        def _(vec):
            for kc in range(KC):
                vec.wait_ge(self.sem["aC"], S["aC"] + kc + 1)
                pt = ptmp[kc % 2]
                pr = pA[:, kc * TS:(kc + 1) * TS]
                pe = pB[:, kc * TS:(kc + 1) * TS]
                self.V(vec, lambda pr=pr, pt=pt: vec.tensor_copy(pr, pt))
                self.V(vec, lambda pe=pe, pt=pt, pr=pr: vec.tensor_sub(pe, pt, pr))
            vec.wait_ge(self.sem["tM"], S["tM"] + KC + 1)
            rz = self.rows[:, T:T + TS]
            self.V(vec, lambda rz=rz: vec.reciprocal(rz, self.bank(5)[0:1, :TS]))
            vec.wait_ge(self.sem["tM"], S["tM"] + KC + 3)
            vec.wait_ge(self.sem["aC"], S["aC"] + KC + 1)
            self.wait_stores(vec, S)
            tm = self.tmp32[:, 0:TS]
            lr = self.ltmp[:, 0:TS]
            le = self.ltmp[:, TS:2 * TS]
            self.V(vec, lambda tm=tm: vec.tensor_mul(
                tm, self.bank(7)[:, :TS], self.bcast[:, TS:2 * TS]))
            self.V(vec, lambda lr=lr, tm=tm: vec.tensor_copy(lr, tm))
            self.V(vec, lambda le=le, tm=tm, lr=lr: vec.tensor_sub(le, tm, lr))

        @b.gpsimd
        def _(gp):
            gp.wait_ge(self.sem["vC"], self.ct["vC"])
            lr = self.ltmp[:, 0:TS]
            le = self.ltmp[:, TS:2 * TS]
            self.dma(gp, "stO", self.attnL[0, h, :, qt * TS:(qt + 1) * TS], lr)
            self.dma(gp, "stO", self.attnL[1, h, :, qt * TS:(qt + 1) * TS], le)

    def ph_end(self):
        b = self.block
        S = dict(self.ct)

        @b.gpsimd
        def _(gp):
            for s in ALL_SEMS:
                if S[s] > 0:
                    gp.wait_ge(self.sem[s], S[s])


def make_inputs_per_core(cfg, core, full):
    c = cfg
    seq = core // 2
    half = core % 2
    x = np.asarray(full["L0_attn_hidden_states"])
    xs = np.ascontiguousarray(
        x[seq, half * c.T:(half + 1) * c.T, :].T).reshape(c.DC, 128, c.T)

    def u8(a):
        a = np.asarray(a)
        return a.view(np.uint8) if a.dtype == np.bool_ else a.astype(np.uint8)

    cei = np.ones(c.D, dtype=np.float32)
    cei[c.NE:] = np.float32(c.EI_NEG)
    return {
        "x0": xs.astype(np.float32),
        "qsc_in": np.asarray(full["L0_attn_q_scale"], dtype=np.float32),
        "ksc_in": np.asarray(full["L0_attn_k_scale"], dtype=np.float32),
        "vsc_in": np.asarray(full["L0_attn_v_scale"], dtype=np.float32),
        "osc_in": np.asarray(full["L0_attn_o_scale"], dtype=np.float32),
        "wqT": np.ascontiguousarray(np.asarray(full["L0_attn_q_weight"], dtype=np.float32).T),
        "wkT": np.ascontiguousarray(np.asarray(full["L0_attn_k_weight"], dtype=np.float32).T),
        "wvT": np.ascontiguousarray(np.asarray(full["L0_attn_v_weight"], dtype=np.float32).T),
        "woT": np.ascontiguousarray(np.asarray(full["L0_attn_o_weight"], dtype=np.float32).T),
        "k0u": u8(full["L0_ffn_k0"]),
        "k1u": u8(full["L0_ffn_k1"]),
        "k2u": u8(full["L0_ffn_k2"]),
        "s0": np.asarray(full["L0_ffn_s0"], dtype=np.float32).reshape(1, 1),
        "s1": np.asarray(full["L0_ffn_s1"], dtype=np.float32).reshape(1, 1),
        "s2": np.asarray(full["L0_ffn_s2"], dtype=np.float32).reshape(1, 1),
        "cei": np.ascontiguousarray(cei.reshape(c.DC, 128).T),
        "cei2": np.ascontiguousarray((2.0 * cei).reshape(c.DC, 128).T),
    }


def build_program(cfg, name="mkmac1", phase_limit=None):
    nc = bass.Bass("TRN2", target_bir_lowering=False, name=name)
    Build(nc, cfg, phase_limit=phase_limit).build()
    return nc


def kernel(**inputs):
    cfg = CFG()
    nc = build_program(cfg, name="mkmacaron_v2")
    in_maps = [make_inputs_per_core(cfg, core, inputs) for core in range(8)]
    res = run_bass_kernel_spmd(nc, in_maps, core_ids=list(range(8)))
    out = np.empty((4, 2048, 4096), dtype=np.float32)
    for core in range(8):
        seq, half = core // 2, core % 2
        y = res.results[core]["yout"].reshape(cfg.D, cfg.T)
        out[seq, half * cfg.T:(half + 1) * cfg.T, :] = y.T
    return out
